# version 17
# baseline (speedup 1.0000x reference)
import os
import numpy as np
import ml_dtypes
import concourse.bass as bass
import concourse.mybir as mybir
from concourse.bass_utils import run_bass_kernel_spmd
from contextlib import ExitStack

F32 = mybir.dt.float32
BF16 = mybir.dt.bfloat16
AF = mybir.ActivationFunctionType
ALU = mybir.AluOpType
AX = mybir.AxisListType
BF = ml_dtypes.bfloat16

P = 128
D = 1024
T = 4096
B2 = 2
W = 512
NW = T // W
HEADS = 8
DH = 64
INNER = 512
HID = 2730
DFF = 5460
QK_SCALE = 8.0
EPS = 1e-5

DEBUG_STAGE = os.environ.get("MK_DEBUG_STAGE", "full")


def host_consts():
    c = {}
    c["ident"] = np.eye(128, dtype=np.float32)
    half = np.arange(0, DH, 2, dtype=np.float64)
    inv_freq = 1.0 / 10000.0 ** (half / DH)
    base = (half + 0.4 * DH) / (1.4 * DH)
    sgn = np.concatenate([-np.ones(32), np.ones(32)])[None, :]

    def rot_tables(idx):
        freqs = idx[:, None] * inv_freq[None, :]
        power = idx / float(W // 2)
        scale = base[None, :] ** power[:, None]
        cos2 = np.concatenate([np.cos(freqs)] * 2, axis=1)
        sin2 = np.concatenate([np.sin(freqs)] * 2, axis=1)
        sc2 = np.concatenate([scale] * 2, axis=1)
        return cos2, sin2, sc2

    qi = np.arange(W, dtype=np.float64)
    cq, sq_, scq = rot_tables(qi)
    c["tqc"] = (QK_SCALE * cq * scq).astype(np.float32)
    c["tqs"] = (QK_SCALE * sq_ * scq * sgn).astype(np.float32)
    ki = np.arange(2 * W, dtype=np.float64) - W
    ck, sk, sck = rot_tables(ki)
    c["tkc"] = (ck / sck).astype(np.float32)
    c["tks"] = (sk / sck * sgn).astype(np.float32)
    i = np.arange(128)[:, None]
    j = np.arange(128)[None, :]
    c["mask_diag"] = np.where(j <= i, 0.0, -1e30).astype(np.float32)
    c["mask_trail"] = np.where(j >= i, 0.0, -1e30).astype(np.float32)
    sh = np.zeros((128, 128), dtype=np.float32)
    sh[np.arange(127), np.arange(1, 128)] = 1.0
    c["shiftT"] = sh
    b127 = np.zeros((128, 128), dtype=np.float32)
    b127[127, 0] = 1.0
    c["b127"] = b127
    return c


class Eng:
    def __init__(self, ctx, key):
        self.ctx = ctx
        self.key = key
        self.last = None

    @property
    def active(self):
        return self.ctx.active == self.key

    def __getattr__(self, name):
        def call(*args, **kw):
            if self.active:
                self.last = getattr(self.ctx.handles[self.key], name)(*args, **kw)
                self.last_has_inc = False
                return self.last
            return None
        return call

    def wait(self, sem, val):
        if val is None or val <= 0:
            return
        if self.active:
            self.ctx.handles[self.key].wait_ge(self.ctx.sems[sem], val)

    def waitp(self, pair):
        if pair is None:
            return
        if isinstance(pair, list):
            for p in pair:
                self.waitp(p)
            return
        sem, val = pair
        self.wait(sem, val)

    def inc(self, sem, n=1):
        self.ctx.counts[sem] += n
        if self.active:
            assert self.last is not None, f"inc({sem}) with no prior op on {self.key}"
            if getattr(self, "last_has_inc", False):
                self.last = self.ctx.handles[self.key].nop()
            self.last.then_inc(self.ctx.sems[sem], n)
            self.last_has_inc = True

    def ms(self, sem):
        self.inc(sem, 1)
        if self.active:
            h = self.ctx.handles[self.key]
            h.nop().then_inc(self.ctx.sems[sem], 1)
        self.ctx.counts[sem] += 1
        return (sem, self.ctx.counts[sem])

    def msd(self, sem):
        self.inc(sem, 1)
        if self.active:
            h = self.ctx.handles[self.key]
            h.drain()
            h.nop().then_inc(self.ctx.sems[sem], 1)
        self.ctx.counts[sem] += 1
        return (sem, self.ctx.counts[sem])

    def selfsync(self, sem):
        self.inc(sem, 1)
        if self.active:
            self.ctx.handles[self.key].wait_ge(self.ctx.sems[sem], self.ctx.counts[sem])

    def wait_now(self, sem):
        self.wait(sem, self.ctx.counts[sem])

    def dma(self, out, in_, sem, n=16, **kw):
        self.ctx.counts[sem] += n
        if self.active:
            self.last = self.ctx.handles[self.key].dma_start(out=out, in_=in_, **kw)
            self.last.then_inc(self.ctx.sems[sem], n)
            self.last_has_inc = True
        return (sem, self.ctx.counts[sem])


class Ctx:
    def __init__(self, sem_names):
        self.sem_names = sem_names
        self.sems = None
        self.counts = {k: 0 for k in sem_names}
        self.active = None
        self.handles = {}

    def val(self, sem):
        return (sem, self.counts[sem])


class Rot:

    def __init__(self, n, seed=None):
        self.n = n
        self.hist = [seed] * n if seed is not None else []
        self.i = n if seed is not None else 0
        self.presends = self.i

    def acquire(self):
        idx = self.i % self.n
        if self.i >= self.n:
            assert len(self.hist) > self.i - self.n, (
                "resource acquired before its prior user's release was recorded")
            waitp = self.hist[self.i - self.n]
        else:
            waitp = None
        self.i += 1
        return idx, waitp

    def release(self, pair):
        self.hist.append(pair)
        assert len(self.hist) <= self.i, "release() without matching acquire()"


class Arena:

    def __init__(self, handle, total_f32cols, reserved_f32cols):
        self.h = handle
        self.total = total_f32cols
        self.base = reserved_f32cols
        self.cur = reserved_f32cols

    def reset(self):
        self.cur = self.base

    def f32(self, cols):
        o = self.cur
        self.cur += cols
        assert self.cur <= self.total, f"arena overflow: {self.cur} > {self.total}"
        return self.h[:, o:o + cols]

    def bf16(self, cols):
        f32c = (cols + 1) // 2
        o = self.cur
        self.cur += f32c
        assert self.cur <= self.total, f"arena overflow: {self.cur} > {self.total}"
        return self.h[:, o:o + f32c].bitcast(BF16)[:, 0:cols]

    def reserved_f32(self, off, cols):
        assert off + cols <= self.base
        return self.h[:, off:off + cols]

    def reserved_bf16(self, off, cols):
        assert off + (cols + 1) // 2 <= self.base
        return self.h[:, off:off + (cols + 1) // 2].bitcast(BF16)[:, 0:cols]


def build_program():
    nc = bass.Bass("TRN2", target_bir_lowering=False)

    ns = {}
    ns["nc"] = nc

    def din(name, shape):
        ns[name] = nc.dram_tensor(name, shape, F32, kind="ExternalInput").ap()

    din("x0", [B2, T, D])
    din("log_lambda", [D]); din("b_w", [D, D]); din("b_b", [D]); din("c_w", [D, D]); din("c_b", [D])
    din("wqkv", [2, D, 3 * INNER]); din("q_scale", [2, DH]); din("k_scale", [2, DH])
    din("wo", [2, INNER, D]); din("ln_g", [2, D]); din("ln_b", [2, D])
    din("w1", [2, D, DFF]); din("b1", [2, DFF]); din("w2", [2, HID, D]); din("b2", [2, D])
    din("ident", [128, 128])
    din("tqc", [W, DH]); din("tqs", [W, DH]); din("tkc", [2 * W, DH]); din("tks", [2 * W, DH])
    din("mask_diag", [128, 128]); din("mask_trail", [128, 128])
    din("shiftT", [128, 128]); din("b127", [128, 128])

    ns["out"] = nc.dram_tensor("out", [B2, T, D], F32, kind="ExternalOutput").ap()
    if DEBUG_STAGE != "full":
        ns["dbg_tap"] = nc.dram_tensor("dbg_tap", [B2, T, D], F32, kind="ExternalOutput").ap()
    ns["s_x1"] = nc.dram_tensor("s_x1", [B2, T, D], F32, kind="Internal").ap()
    ns["s_ya"] = nc.dram_tensor("s_ya", [B2, T, D], F32, kind="Internal").ap()
    ns["s_yb"] = nc.dram_tensor("s_yb", [B2, T, D], F32, kind="Internal").ap()

    sem_names = ["lw", "lwg", "li", "lib", "lwc", "lbw", "lqs", "ly0", "ly1", "ly2", "lb0", "lb1", "lb2",
                 "lx0", "lx1", "le0", "le1", "so0", "so1", "pb",
                 "pt", "pm", "pq", "vt", "vm", "vx", "sq", "sg", "gp"]
    ctx = Ctx(sem_names)
    TT, V, S, G, Q = (Eng(ctx, k) for k in ["t", "v", "s", "g", "q"])

    with ExitStack() as big:
        sems = {n: big.enter_context(nc.semaphore(n)) for n in sem_names}
        ctx.sems = sems

        sb = lambda nm, shp, dt: big.enter_context(nc.sbuf_tensor(nm, shp, dt))
        ns["identf"] = sb("identf", [P, 128], F32)
        ns["identb"] = sb("identb", [P, 128], BF16)
        ns["maskstack"] = sb("maskstack", [P, 256], F32)
        ns["shmatb"] = sb("shmatb", [P, 256], BF16)
        ns["lamf"] = sb("lamf", [P, 8], F32)
        ns["omlf"] = sb("omlf", [P, 8], F32)
        ns["bbf"] = sb("bbf", [P, 8], F32)
        ns["cbf"] = sb("cbf", [P, 8], F32)
        ns["onesf"] = sb("onesf", [P, 512], F32)
        ns["stats"] = sb("stats", [P, 128], F32)
        ns["epsc"] = sb("epsc", [P, 1], F32)
        ns["bscr"] = sb("bscr", [P, 8], F32)
        ns["scrap1k"] = sb("scrap1k", [P, 1024], BF16)
        ns["onesb"] = sb("onesb", [P, 512], BF16)
        ns["rowbuf"] = sb("rowbuf", [P, 2048], F32)
        ns["cbrow"] = sb("cbrow", [P, 1024], BF16)
        ns["bbrow"] = sb("bbrow", [P, 1024], BF16)
        ARENA_COLS = 48000
        RESV = 8192
        arena_h = sb("arena", [P, ARENA_COLS], F32)
        ns["arena"] = Arena(arena_h, ARENA_COLS, RESV)
        ns["wqkvb"] = ns["arena"].reserved_bf16(0, 8 * 1536)
        ns["wob"] = ns["arena"].reserved_bf16(6144, 4 * D)
        ns["psA"] = big.enter_context(nc.psum_tensor("psA", [P, 4096], F32))
        ns["big"] = big

        def emit(active, handle):
            ctx.active = active
            ctx.handles = {active: handle}
            for k in sem_names:
                ctx.counts[k] = 0
            program(ctx, TT, V, S, G, Q, ns)

        with nc.Block() as block:
            @block.tensor
            def _(tensor):
                emit("t", tensor)

            @block.vector
            def _(vector):
                emit("v", vector)

            @block.scalar
            def _(scalar):
                emit("s", scalar)

            @block.gpsimd
            def _(gpsimd):
                emit("g", gpsimd)

            @block.sync
            def _(sync):
                emit("q", sync)

    return nc


def phase_barrier(ctx, TT, V, S, G, Q, ns):
    bscr = ns["bscr"]
    Q.wait_now("so0")
    Q.wait_now("so1")
    V.tensor_copy(bscr[:, 0:1], bscr[:, 1:2])
    V.inc("pb", 1)
    S.copy(bscr[:, 2:3], bscr[:, 3:4])
    S.inc("pb", 1)
    G.tensor_copy(bscr[:, 4:5], bscr[:, 5:6])
    G.inc("pb", 1)
    Q.nop()
    Q.inc("pb", 1)
    TT.wait("pb", ctx.counts["pb"])
    TT.transpose(ns["psA"].ap()[:, 0:128], ns["identf"][:, :], ns["identf"][:, :])
    TT.inc("pb", 1)
    target = ctx.counts["pb"]
    for E in (V, S, G, Q):
        E.wait("pb", target)


def program(ctx, TT, V, S, G, Q, ns):
    phase_setup(ctx, TT, V, S, G, Q, ns)
    phase_A(ctx, TT, V, S, G, Q, ns)
    if DEBUG_STAGE != "A":
        phase_barrier(ctx, TT, V, S, G, Q, ns)
        phase_B(ctx, TT, V, S, G, Q, ns, 0, ns["s_x1"], ns["s_ya"])
    if DEBUG_STAGE not in ("A", "B0"):
        phase_barrier(ctx, TT, V, S, G, Q, ns)
        phase_C(ctx, TT, V, S, G, Q, ns, 0, ns["s_ya"], ns["s_yb"],
                final_ln=False)
    if DEBUG_STAGE == "full" or DEBUG_STAGE in ("B1", "C1"):
        phase_barrier(ctx, TT, V, S, G, Q, ns)
        phase_B(ctx, TT, V, S, G, Q, ns, 1, ns["s_yb"], ns["s_ya"])
        phase_barrier(ctx, TT, V, S, G, Q, ns)
        phase_C(ctx, TT, V, S, G, Q, ns, 1, ns["s_ya"], None,
                final_ln=True)
    fin = {}
    for sem, val in ns.get("final_waits", []):
        fin[sem] = max(fin.get(sem, 0), val)
    for sem, val in sorted(fin.items()):
        G.wait(sem, val)


def phase_setup(ctx, TT, V, S, G, Q, ns):
    ns["final_waits"] = []
    Q.dma(ns["identf"][:, :], ns["ident"][:, :], "lw")
    Q.dma(ns["maskstack"][:, 0:128], ns["mask_trail"][:, :], "lw")
    Q.dma(ns["maskstack"][:, 128:256], ns["mask_diag"][:, :], "lw")
    Q.dma(ns["lamf"][:, :], ns["log_lambda"].rearrange("(c p) -> p c", p=P), "lw", allow_slow_non_contiguous=True)
    Q.dma(ns["bbf"][:, :], ns["b_b"].rearrange("(c p) -> p c", p=P), "lw", allow_slow_non_contiguous=True)
    Q.dma(ns["cbf"][:, :], ns["c_b"].rearrange("(c p) -> p c", p=P), "lw", allow_slow_non_contiguous=True)
    G.dma(ns["identb"][:, :], ns["ident"][:, :], "li")
    G.dma(ns["shmatb"][:, 0:128], ns["shiftT"][:, :], "li")
    G.dma(ns["shmatb"][:, 128:256], ns["b127"][:, :], "li")
    G.dma(ns["cbrow"][0:1, 0:1024], ns["c_b"].unsqueeze(0), "li")
    G.dma(ns["rowbuf"][0:1, 0:1024], ns["log_lambda"].unsqueeze(0), "li")
    lw0 = [ctx.val("lw"), ctx.val("li")]
    ns["setup_lw"] = lw0

    stats = ns["stats"]
    S.wait_now("lw"); S.wait_now("li")
    S.activation(stats[:, 0:8], ns["lamf"][:, :], AF.Exp)
    S.selfsync("sq")
    S.activation(stats[:, 8:16], stats[:, 0:8], AF.Exp, scale=-1.0)
    sq0 = S.ms("sq")
    V.waitp(sq0)
    V.wait_now("lw"); V.wait_now("li")
    V.tensor_scalar(stats[:, 16:24], stats[:, 8:16], 1.0, None, ALU.add)
    V.selfsync("vx")
    V.reciprocal(ns["lamf"][:, :], stats[:, 16:24])
    V.selfsync("vx")
    V.tensor_scalar(ns["omlf"][:, :], ns["lamf"][:, :], -1.0, 1.0, ALU.mult, ALU.add)
    V.memset(ns["onesf"][:, :], 1.0)
    V.memset(ns["epsc"][:, :], EPS)
    V.memset(ns["bscr"][:, :], 0.0)
    V.memset(ns["onesb"][:, :], 1.0)
    ns["setup_vx"] = V.ms("vx")
    rb = ns["rowbuf"]
    S.activation(rb[0:1, 1024:2048], rb[0:1, 0:1024], AF.Exp)
    S.selfsync("sq")
    S.activation(rb[0:1, 0:1024], rb[0:1, 1024:2048], AF.Exp, scale=-1.0)
    srow = S.ms("sq")
    V.waitp(srow)
    V.tensor_scalar(rb[0:1, 1024:2048], rb[0:1, 0:1024], 1.0, None, ALU.add)
    V.selfsync("vx")
    V.reciprocal(rb[0:1, 0:1024], rb[0:1, 1024:2048])
    V.selfsync("vx")
    V.tensor_scalar(rb[0:1, 1024:2048], rb[0:1, 0:1024], -1.0, 1.0, ALU.mult, ALU.add)
    vrow = V.ms("vx")
    G.waitp(vrow)
    G.dma(ns["bbrow"][0:1, 0:1024], ns["b_b"].unsqueeze(0), "lib")
    V.wait_now("lib")
    V.tensor_tensor(ns["bbrow"][0:1, 0:1024], ns["bbrow"][0:1, 0:1024],
                    rb[0:1, 1024:2048], ALU.mult)
    ns["rows_ready"] = V.ms("vx")


def phase_A(ctx, TT, V, S, G, Q, ns):
    nc = ns["nc"]
    if True:
        ar = ns["arena"]
        ar.reset()
        bwt = ar.bf16(64 * 128)
        cwt = ar.bf16(64 * 128)
        w_tm = ar.bf16(1024)
        lam_bc = ar.f32(8 * 512)
        x0tm = [ar.f32(4 * 1024) for j in range(3)]
        x0fm = [ar.bf16(8 * 512) for j in range(2)]
        hfm = ar.f32(8 * 512)
        carry = ar.f32(8)
        hbf = [ar.bf16(8 * 512) for j in range(2)]
        scrapb = ar.bf16(1024)

        psA = ns["psA"]
        psAb = psA.ap().bitcast(BF16)
        pstb = [psAb[:, 0:1024], psAb[:, 1024:2048]]
        pstf = [psA.ap()[:, 1024:1536], psA.ap()[:, 1536:2048]]
        psxb = [psA.ap()[:, 2048:2560], psA.ap()[:, 2560:3072]]
        psat = [psA.ap()[:, 3072:3584], psA.ap()[:, 3584:4096]]

        x0_d, s_x1 = ns["x0"], ns["s_x1"]
        identf, identb = ns["identf"], ns["identb"]
        lamf, omlf = ns["lamf"], ns["omlf"]
        stats = ns["stats"]

        rows_ready = ns["rows_ready"]
        onesb, cbrow, bbrow = ns["onesb"], ns["cbrow"], ns["bbrow"]

        V.waitp(ns["setup_vx"])
        for ee in range(8):
            V.tensor_scalar_mul(lam_bc[:, ee * 512:(ee + 1) * 512],
                                ns["onesf"][:, :], lamf[:, ee:ee + 1])
        V.selfsync("vx")
        lam_ready = V.ms("vx")

        pool_wT = Rot(2)
        pool_wtm = Rot(1)
        wt_done = {}
        for wi, (w_d, dstt) in enumerate([(ns["b_w"], bwt), (ns["c_w"], cwt)]):
            for ee in range(8):
                _, wstg = pool_wtm.acquire()
                G.waitp(wstg)
                lwp = G.dma(w_tm[:, :], w_d[ee * 128:(ee + 1) * 128, :], "lwc")
                if wi == 0:
                    V.waitp(lwp)
                    V.waitp(ns["setup_vx"])
                    V.tensor_scalar_mul(w_tm[:, :], w_tm[:, :], omlf[:, ee:ee + 1])
                    wrd = V.ms("vt")
                    TT.waitp(wrd)
                else:
                    TT.waitp(lwp)
                TT.wait_now("li")
                for g in range(2):
                    j, wp = pool_wT.acquire()
                    TT.waitp(wp)
                    for u in range(4):
                        kk = g * 4 + u
                        TT.transpose(pstb[j][:, u * 256 if False else u * 128 + 0: u * 128 + 128]
                                     if False else pstb[j][:, u * 128: u * 128 + 128],
                                     w_tm[:, kk * 128: kk * 128 + 128],
                                     identb[:, :])
                    TT.inc("pt", 1)
                    V.waitp(ctx.val("pt"))
                    V.tensor_copy(dstt[:, :].rearrange("p (k e c) -> p k e c", e=8, c=128)
                                  [:, g * 4:(g + 1) * 4, ee, :],
                                  pstb[j][:, 0:512].rearrange("p (k c) -> p k c", c=128))
                    pool_wT.release(V.msd("vt"))
                pool_wtm.release(ctx.val("pt"))
            wt_done[wi] = ctx.val("vt")

        NCH = B2 * NW
        pool_tf = Rot(2)
        pool_xb = Rot(2)
        pool_at = Rot(2)
        pool_x0tm = Rot(3)
        pool_x0fm = Rot(2)
        pool_hbf = Rot(2)
        lx_sem = ["lx0", "lx1", "le0"]

        def load_x0(i):
            sl, wp = pool_x0tm.acquire()
            Q.waitp(wp)
            b, c = divmod(i, NW)
            lp = Q.dma(x0tm[sl][:, :].rearrange("p (tt d) -> p tt d", d=D),
                       x0_d[b, c * 512:(c + 1) * 512, :].rearrange("(tt p) d -> p tt d", p=P),
                       lx_sem[sl])
            return sl, lp

        def tgroups(i, x0sl, lxp):
            fmsl, fmw = pool_x0fm.acquire()
            TT.waitp(lxp)
            TT.waitp(fmw)
            for g in range(8):
                j, wp = pool_tf.acquire()
                TT.waitp(wp)
                for tt in range(4):
                    TT.transpose(pstf[j][:, tt * 128:(tt + 1) * 128],
                                 x0tm[x0sl][:, tt * 1024 + g * 128: tt * 1024 + g * 128 + 128],
                                 identf[:, :])
                TT.inc("pt", 1)
                V.waitp(ctx.val("pt"))
                V.tensor_copy(x0fm[fmsl][:, g * 512:(g + 1) * 512], pstf[j][:, :])
                pool_tf.release(V.msd("vt"))
            return fmsl, ctx.val("vt")

        def xb_chunk(i, fmsl, tfm_done, prev_cast):
            b, c = divmod(i, NW)
            hsl, hw = pool_hbf.acquire()
            TT.waitp(hw)
            TT.waitp(tfm_done)
            if i == 0:
                TT.waitp(wt_done[0])
                TT.waitp(rows_ready)
            for ee in range(8):
                j, wp = pool_xb.acquire()
                TT.waitp(wp)
                for kk in range(8):
                    TT.matmul(psxb[j][:, :],
                              bwt[:, (kk * 8 + ee) * 128:(kk * 8 + ee + 1) * 128],
                              x0fm[fmsl][:, kk * 512:(kk + 1) * 512],
                              start=(kk == 0), stop=False)
                TT.matmul(psxb[j][:, :], bbrow[0:1, ee * 128:(ee + 1) * 128],
                          onesb[0:1, 0:512], start=False, stop=True)
                TT.inc("pm", 1)
                if ee == 7:
                    pool_x0fm.release(ctx.val("pm"))
                V.waitp(ctx.val("pm"))
                if prev_cast is not None:
                    V.waitp(prev_cast)
                    prev_cast = None
                init = 0.0 if c == 0 else carry[:, ee:ee + 1]
                V.tensor_tensor_scan(hfm[:, ee * 512:(ee + 1) * 512],
                                     lam_bc[:, ee * 512:(ee + 1) * 512],
                                     psxb[j][:, :],
                                     init, ALU.mult, ALU.add)
                V.selfsync("vx")
                V.tensor_copy(carry[:, ee:ee + 1], hfm[:, ee * 512 + 511: ee * 512 + 512])
                V.selfsync("vx")
                pool_xb.release(V.msd("vm"))
                scp = ctx.val("vm")
                S.waitp(scp)
                S.copy(hbf[hsl][:, ee * 512:(ee + 1) * 512], hfm[:, ee * 512:(ee + 1) * 512])
                S.ms("sg")
            return hsl, ctx.val("sg")

        def cw_chunk(i, x0sl, hsl, hready):
            b, c = divmod(i, NW)
            TT.waitp(hready)
            if i == 0:
                TT.waitp(wt_done[1])
                TT.wait_now("li")
                TT.waitp(ones_vx)
            sbase = 32 + (i % 2) * 48
            for tt in range(4):
                for nt in range(2):
                    j, wp = pool_at.acquire()
                    TT.waitp(wp)
                    for ee in range(8):
                        TT.matmul(psat[j][:, :],
                                  hbf[hsl][:, ee * 512 + tt * 128: ee * 512 + tt * 128 + 128],
                                  cwt[:, (ee * 8 + nt * 4) * 128:(ee * 8 + nt * 4 + 4) * 128],
                                  start=(ee == 0), stop=False)
                    TT.matmul(psat[j][:, :], onesb[0:1, 0:128],
                              cbrow[0:1, nt * 512:(nt + 1) * 512],
                              start=False, stop=True)
                    TT.inc("pm", 1)
                    if tt == 3 and nt == 1:
                        pool_hbf.release(ctx.val("pm"))
                    V.waitp(ctx.val("pm"))
                    V.scalar_tensor_tensor(
                        x0tm[x0sl][:, tt * 1024 + nt * 512: tt * 1024 + nt * 512 + 512],
                        psat[j][:, :], 1.0,
                        x0tm[x0sl][:, tt * 1024 + nt * 512: tt * 1024 + nt * 512 + 512],
                        ALU.mult, ALU.add,
                        accum_out=stats[:, sbase + tt * 2 + nt: sbase + tt * 2 + nt + 1])
                    pool_at.release(V.msd("vm"))
                svp = ctx.val("vm")
                S.waitp(svp)
                S.activation(scrapb[:, :], x0tm[x0sl][:, tt * 1024:(tt + 1) * 1024], AF.Square,
                             accum_out=stats[:, sbase + 8 + tt: sbase + 8 + tt + 1])
                S.ms("sq")
            sqv = ctx.val("sq")
            V.waitp(sqv)
            V.tensor_tensor(stats[:, sbase + 12: sbase + 16],
                            stats[:, sbase: sbase + 8].rearrange("p (a b) -> p a b", b=2)[:, :, 0],
                            stats[:, sbase: sbase + 8].rearrange("p (a b) -> p a b", b=2)[:, :, 1],
                            ALU.add)
            V.selfsync("vx")
            V.tensor_scalar(stats[:, sbase + 16: sbase + 20], stats[:, sbase + 12: sbase + 16],
                            1.0 / D, None, ALU.mult)
            V.selfsync("vx")
            V.tensor_tensor(stats[:, sbase + 20: sbase + 24],
                            stats[:, sbase + 16: sbase + 20], stats[:, sbase + 16: sbase + 20],
                            ALU.mult)
            V.tensor_scalar(stats[:, sbase + 24: sbase + 28], stats[:, sbase + 8: sbase + 12],
                            1.0 / D, None, ALU.mult)
            V.selfsync("vx")
            V.tensor_tensor(stats[:, sbase + 28: sbase + 32], stats[:, sbase + 24: sbase + 28],
                            stats[:, sbase + 20: sbase + 24], ALU.subtract)
            vstat = V.ms("vx")
            S.waitp(vstat)
            S.activation(stats[:, sbase + 32: sbase + 36], stats[:, sbase + 28: sbase + 32],
                         AF.Ln, bias=ns["epsc"][:, 0:1])
            S.selfsync("sq")
            S.activation(stats[:, sbase + 36: sbase + 40], stats[:, sbase + 32: sbase + 36],
                         AF.Exp, scale=-0.5)
            srstd = S.ms("sq")
            V.waitp(srstd)
            for tt in range(4):
                V.tensor_scalar(x0tm[x0sl][:, tt * 1024:(tt + 1) * 1024],
                                x0tm[x0sl][:, tt * 1024:(tt + 1) * 1024],
                                stats[:, sbase + 16 + tt: sbase + 17 + tt],
                                stats[:, sbase + 36 + tt: sbase + 37 + tt],
                                ALU.subtract, ALU.mult)
            apl = V.ms("vm")
            Q.waitp(apl)
            so = "so0" if i % 2 == 0 else "so1"
            sdp = Q.dma(s_x1[b, c * 512:(c + 1) * 512, :].rearrange("(tt p) d -> p tt d", p=P),
                        x0tm[x0sl][:, :].rearrange("p (tt d) -> p tt d", d=D), so)
            if DEBUG_STAGE == "A":
                sdp = Q.dma(ns["dbg_tap"][b, c * 512:(c + 1) * 512, :]
                            .rearrange("(tt p) d -> p tt d", p=P),
                            x0tm[x0sl][:, :].rearrange("p (tt d) -> p tt d", d=D), so)
            pool_x0tm.release(sdp)
            ns["final_waits"].append(sdp)

        ones_vx = rows_ready

        loads = {0: load_x0(0), 1: load_x0(1)}
        fms = {}
        hs = {}
        casts = {-1: None}
        fms[0] = tgroups(0, loads[0][0], loads[0][1])
        for i in range(NCH):
            if i + 1 < NCH:
                fms[i + 1] = tgroups(i + 1, loads[i + 1][0], loads[i + 1][1])
            hs[i] = xb_chunk(i, *fms.pop(i), casts[i - 1])
            casts[i] = hs[i][1]
            if i >= 1:
                cw_chunk(i - 1, loads[i - 1][0], *hs.pop(i - 1))
                del loads[i - 1]
            if i + 2 < NCH:
                loads[i + 2] = load_x0(i + 2)
        cw_chunk(NCH - 1, loads[NCH - 1][0], *hs.pop(NCH - 1))


def phase_B(ctx, TT, V, S, G, Q, ns, layer, s_in, s_out):
    nc = ns["nc"]
    ar = ns["arena"]
    ar.reset()
    ytm = [ar.f32(4 * 1024) for _ in range(2)]
    ybf = [ar.bf16(4 * 1024) for _ in range(2)]
    yfm = [ar.bf16(8 * 513) for _ in range(2)]
    tqc = ar.bf16(4 * 64); tqs = ar.bf16(4 * 64)
    tkc = ar.bf16(8 * 64); tks = ar.bf16(8 * 64)
    qsb = ar.f32(64); qss = ar.f32(64); ksb = ar.f32(64); kss = ar.f32(64)
    qe = ar.f32(512); ke = ar.f32(512)
    t1 = [ar.f32(512) for _ in range(2)]
    t2 = [ar.f32(512) for _ in range(2)]
    sqt = ar.f32(1024)
    qp_tm = [ar.bf16(512) for _ in range(3)]
    qfm = [ar.bf16(4 * 512) for _ in range(2)]
    kfmc = [ar.bf16(4 * 512) for _ in range(2)]
    kfmp = [ar.bf16(4 * 512) for _ in range(3)]
    vp = [ar.bf16(4 * 512) for _ in range(3)]
    p_sb = [ar.bf16(640) for _ in range(3)]
    pTs = [ar.bf16(640) for _ in range(3)]
    o_bf = [ar.bf16(512) for _ in range(2)]
    ofm = [ar.bf16(4 * 512) for _ in range(2)]
    denom = [ar.f32(32) for _ in range(2)]
    negm = [ar.f32(32) for _ in range(2)]
    rsqq = [ar.f32(8) for _ in range(2)]
    rsqk = [ar.f32(8) for _ in range(2)]
    rdsc = ar.f32(8)

    psA = ns["psA"]
    psAb = psA.ap().bitcast(BF16)

    def bank(j):
        return psA.ap()[:, j * 512:(j + 1) * 512]

    def bankb(j):
        return psAb[:, j * 1024:(j + 1) * 1024]

    def pairf(j):
        b = 2 + 2 * j
        return psA.ap()[:, b * 512: b * 512 + 640]

    def pairtail_b(j):
        b = 2 + 2 * j
        return psAb[:, b * 1024 + 1280: b * 1024 + 1280 + 640]

    identb = ns["identb"]
    maskstack = ns["maskstack"]
    wqkvb, wob = ns["wqkvb"], ns["wob"]
    onesb = ns["onesb"]
    rowb = ns["rowbuf"]

    NWIN = B2 * NW

    for kk in range(8):
        G.dma(wqkvb[:, kk * 1536:(kk + 1) * 1536],
              ns["wqkv"][layer, kk * 128:(kk + 1) * 128, :], "lbw")
    for kk in range(4):
        G.dma(wob[:, kk * 1024:(kk + 1) * 1024],
              ns["wo"][layer, kk * 128:(kk + 1) * 128, :], "lbw")
    G.dma(tqc[:, :].rearrange("p (tt d) -> p tt d", d=64),
          ns["tqc"].rearrange("(tt p) d -> p tt d", p=P), "lbw")
    G.dma(tqs[:, :].rearrange("p (tt d) -> p tt d", d=64),
          ns["tqs"].rearrange("(tt p) d -> p tt d", p=P), "lbw")
    G.dma(tkc[:, :].rearrange("p (tt d) -> p tt d", d=64),
          ns["tkc"].rearrange("(tt p) d -> p tt d", p=P), "lbw")
    G.dma(tks[:, :].rearrange("p (tt d) -> p tt d", d=64),
          ns["tks"].rearrange("(tt p) d -> p tt d", p=P), "lbw")
    Q.dma(rowb[0:1, 0:64], ns["q_scale"][layer].unsqueeze(0), "lqs")
    Q.dma(rowb[0:1, 64:128], ns["k_scale"][layer].unsqueeze(0), "lqs")

    TT.wait_now("lqs")
    TT.wait_now("lbw")
    TT.matmul(bank(0)[:, 0:64], ns["onesf"][0:1, 0:128], rowb[0:1, 0:64], start=True, stop=True)
    TT.matmul(bank(0)[:, 64:128], ns["onesf"][0:1, 0:128], rowb[0:1, 64:128], start=True, stop=True)
    TT.inc("pq", 1)
    V.waitp(ctx.val("pq"))
    V.tensor_copy(qsb[:, :], bank(0)[:, 0:64])
    V.tensor_copy(ksb[:, :], bank(0)[:, 64:128])
    V.selfsync("vx")
    V.tensor_copy(qss[:, 0:32], qsb[:, 32:64])
    V.tensor_copy(qss[:, 32:64], qsb[:, 0:32])
    V.tensor_copy(kss[:, 0:32], ksb[:, 32:64])
    V.tensor_copy(kss[:, 32:64], ksb[:, 0:32])
    V.selfsync("vx")
    V.tensor_tensor(tqc.rearrange("p (a d) -> p a d", d=64),
                    tqc.rearrange("p (a d) -> p a d", d=64),
                    qsb.unsqueeze(1).broadcast_to([P, 4, 64]), ALU.mult)
    V.tensor_tensor(tqs.rearrange("p (a d) -> p a d", d=64),
                    tqs.rearrange("p (a d) -> p a d", d=64),
                    qss.unsqueeze(1).broadcast_to([P, 4, 64]), ALU.mult)
    V.tensor_tensor(tkc.rearrange("p (a d) -> p a d", d=64),
                    tkc.rearrange("p (a d) -> p a d", d=64),
                    ksb.unsqueeze(1).broadcast_to([P, 8, 64]), ALU.mult)
    V.tensor_tensor(tks.rearrange("p (a d) -> p a d", d=64),
                    tks.rearrange("p (a d) -> p a d", d=64),
                    kss.unsqueeze(1).broadcast_to([P, 8, 64]), ALU.mult)
    for b_ in kfmp + vp:
        V.memset(b_[:, :], 0.0)
    tables_ready = V.ms("vx")

    pool_ytm = Rot(2)
    pool_ybf = Rot(2)
    pool_yfm = Rot(2)
    B01 = [0, 1]
    pool01 = Rot(2, seed=tables_ready)
    pool_sim = Rot(3, seed=tables_ready)
    pool_qfm = Rot(2)
    pool_kfmc = Rot(2)
    pool_kfmp = Rot(3)
    pool_vp = Rot(3)
    pool_p = Rot(3)
    pool_pTs = Rot(3)
    pool_ofm = Rot(2)
    pool_dn = Rot(2)
    pool_rq = Rot(2)
    pool_rk = Rot(2)
    lysem = ["ly0", "ly1"]
    lbsem = ["lb0", "lb1"]

    def load_ytm(i):
        sl, wp = pool_ytm.acquire()
        Q.waitp(wp)
        b, w = divmod(i, NW)
        lp = Q.dma(ytm[sl][:, :].rearrange("p (tt d) -> p tt d", d=D),
                   s_in[b, w * 512:(w + 1) * 512, :].rearrange("(tt p) d -> p tt d", p=P),
                   lysem[sl])
        return (sl, lp)

    def load_ybf(i):
        sl2, wp2 = pool_ybf.acquire()
        G.waitp(wp2)
        b, w = divmod(i, NW)
        lp2 = G.dma(ybf[sl2][:, :].rearrange("p (tt d) -> p tt d", d=D),
                    s_in[b, w * 512:(w + 1) * 512, :].rearrange("(tt p) d -> p tt d", p=P),
                    lbsem[sl2])
        return (sl2, lp2)

    def yfm3(fsl):
        return yfm[fsl].rearrange("p (kk c) -> p kk c", c=513)

    def ty_shift(i, ld, prev_ready):
        sl2, lp2 = ld
        b, w = divmod(i, NW)
        fsl, fw = pool_yfm.acquire()
        V.waitp(fw)
        if w == 0:
            V.memset(yfm3(fsl)[:, :, 0:1], 0.0)
        else:
            V.waitp(prev_ready)
            V.tensor_copy(yfm3(fsl)[:, :, 0:1], yfm3(1 - fsl)[:, :, 512:513])
        carry_done = V.ms("vt")
        TT.waitp(lp2)
        for g in range(4):
            j, wp = pool01.acquire()
            TT.waitp(wp)
            for u in range(8):
                kk = 2 * g + u // 4
                tt = u % 4
                TT.transpose(bankb(B01[j])[:, u * 128:(u + 1) * 128],
                             ybf[sl2][:, tt * 1024 + kk * 128: tt * 1024 + kk * 128 + 128],
                             identb[:, :])
            TT.inc("pt", 1)
            S.waitp(ctx.val("pt"))
            if g == 0:
                S.waitp(carry_done)
            S.copy(yfm3(fsl)[:, 2 * g:2 * g + 2, 1:513],
                   bankb(B01[j]).rearrange("p (a c) -> p a c", c=512))
            pool01.release(S.msd("sg"))
        pool_ybf.release(ctx.val("pt"))
        return fsl, ctx.val("sg"), ctx.val("pt")

    rp_state = {"n": 0, "add_ms": {}, "pt_ms": {}}

    def rot_compute(src_sb, src_ready, rsq_ap, tc, ts, tcol, rsq_ms=None):
        n = rp_state["n"]
        rp_state["n"] += 1
        sl = n % 3
        tsl = n % 2
        G.waitp(src_ready)
        G.waitp(rp_state["add_ms"].pop(n - 2, None))
        G.tensor_tensor(t1[tsl][:, :].rearrange("p (h d) -> p h d", d=64),
                        src_sb.rearrange("p (h d) -> p h d", d=64),
                        tc.rearrange("p (a d) -> p a d", d=64)
                        [:, tcol:tcol + 1, :].broadcast_to([P, 8, 64]), ALU.mult)
        rot_src = src_sb.rearrange("p (h u v) -> p h u v", u=2, v=32)[:, :, ::-1, :]
        G.tensor_tensor(t2[tsl][:, :].rearrange("p (h u v) -> p h u v", u=2, v=32),
                        rot_src,
                        ts.rearrange("p (a d) -> p a d", d=64)
                        [:, tcol:tcol + 1, :].rearrange("p a (u v) -> p a u v", v=32)
                        .broadcast_to([P, 8, 2, 32]), ALU.mult)
        gdone = G.ms("gp")
        V.waitp(gdone)
        V.tensor_tensor(t1[tsl][:, :], t1[tsl][:, :], t2[tsl][:, :], ALU.add)
        V.selfsync("vx")
        V.waitp(rsq_ms)
        V.waitp(rp_state["pt_ms"].pop(n - 3, None))
        V.tensor_tensor(qp_tm[sl][:, :].rearrange("p (h d) -> p h d", d=64),
                        t1[tsl][:, :].rearrange("p (h d) -> p h d", d=64),
                        rsq_ap.unsqueeze(2).broadcast_to([P, 8, 64]), ALU.mult)
        tm_done = V.ms("vt")
        rp_state["add_ms"][n] = tm_done
        return n, sl, tm_done

    def rot_transpose(n, sl, tm_done, dst_fm, dslot, dst_col):
        jb, wb = pool01.acquire()
        TT.waitp(tm_done)
        TT.waitp(wb)
        for hp in range(4):
            TT.transpose(bankb(B01[jb])[:, hp * 128:(hp + 1) * 128],
                         qp_tm[sl][:, hp * 128:(hp + 1) * 128], identb[:, :])
        TT.inc("pt", 1)
        rp_state["pt_ms"][n] = ctx.val("pt")
        V.waitp(ctx.val("pt"))
        V.tensor_copy(dst_fm[dslot].rearrange("p (hp c) -> p hp c", c=512)
                      [:, :, dst_col * 128:(dst_col + 1) * 128],
                      bankb(B01[jb])[:, 0:512].rearrange("p (hp c) -> p hp c", c=128))
        pool01.release(V.msd("vt"))

    qkst = {"prev_gp": None}

    def qkv_win(i, fsl, yfm_ready):
        b, w = divmod(i, NW)
        qsl, qw = pool_qfm.acquire()
        kcsl, kcw = pool_kfmc.acquire()
        kpsl, kpw = pool_kfmp.acquire()
        vsl, vw = pool_vp.acquire()
        rqsl, rqw = pool_rq.acquire()
        rksl, rkw = pool_rk.acquire()
        TT.waitp(yfm_ready)
        if i == 0:
            TT.waitp(tables_ready)
        mmst = {}

        def lhs(kk, tt):
            sh = -1 if kk >= 4 else 0
            return yfm[fsl][:, kk * 513 + 1 + tt * 128 + sh:
                            kk * 513 + 1 + tt * 128 + sh + 128]

        def mm_tt(tt):
            pj, pw = pool_sim.acquire()
            TT.waitp(pw)
            vb, vbw = pool01.acquire()
            TT.waitp(vbw)
            base = 2 + 2 * pj
            for col, bk in ((0, base), (512, base + 1), (1024, B01[vb])):
                for kk in range(8):
                    TT.matmul(bank(bk)[:, :], lhs(kk, tt),
                              wqkvb[:, kk * 1536 + col: kk * 1536 + col + 512],
                              start=(kk == 0), stop=(kk == 7))
                TT.inc("pm", 1)
            qkv_pm = ctx.val("pm")
            if tt == 3:
                pool_yfm.release(qkv_pm)
            S.waitp(qkv_pm)
            if tt == 0:
                S.waitp(vw)
            S.copy(vp[vsl][:, tt * 512:(tt + 1) * 512], bank(B01[vb])[:, :])
            vp_evac = S.ms("sg")
            pool01.release(vp_evac)
            mmst[tt] = (pj, qkv_pm)

        def post_tt(tt):
            pj, qkv_pm = mmst.pop(tt)
            base = 2 + 2 * pj
            S.waitp(qkv_pm)
            if tt == 0:
                S.waitp(qw); S.waitp(kcw); S.waitp(kpw)
                S.waitp(rqw); S.waitp(rkw)
            S.waitp(qkst["prev_gp"])
            S.copy(qe[:, :], bank(base)[:, :])
            S.copy(ke[:, :], bank(base + 1)[:, :])
            qk_evac = S.ms("sg")
            pool_sim.release(qk_evac)
            S.activation(sqt[:, 0:512], qe[:, :], AF.Square)
            S.activation(sqt[:, 512:1024], ke[:, :], AF.Square)
            sq_done = S.ms("sq")
            V.waitp(sq_done)
            V.tensor_reduce(rsqq[rqsl][:, 0:8], sqt[:, 0:512].rearrange("p (h d) -> p h d", d=64),
                            AX.X, ALU.add)
            V.tensor_reduce(rsqk[rksl][:, 0:8], sqt[:, 512:1024].rearrange("p (h d) -> p h d", d=64),
                            AX.X, ALU.add)
            V.selfsync("vx")
            V.tensor_scalar_max(rsqq[rqsl][:, 0:8], rsqq[rqsl][:, 0:8], 1e-24)
            V.tensor_scalar_max(rsqk[rksl][:, 0:8], rsqk[rksl][:, 0:8], 1e-24)
            vr = V.ms("vx")
            S.waitp(vr)
            S.activation(rsqq[rqsl][:, 0:8], rsqq[rqsl][:, 0:8], AF.Ln)
            S.activation(rsqk[rksl][:, 0:8], rsqk[rksl][:, 0:8], AF.Ln)
            S.selfsync("sq")
            S.activation(rsqq[rqsl][:, 0:8], rsqq[rqsl][:, 0:8], AF.Exp, scale=-0.5)
            S.activation(rsqk[rksl][:, 0:8], rsqk[rksl][:, 0:8], AF.Exp, scale=-0.5)
            rq_ms = S.ms("sq")
            ca = rot_compute(qe[:, :], qk_evac, rsqq[rqsl][:, 0:8], tqc, tqs, tt, rq_ms)
            cb = rot_compute(ke[:, :], qk_evac, rsqk[rksl][:, 0:8], tkc, tks, 4 + tt, rq_ms)
            cc = rot_compute(ke[:, :], qk_evac, rsqk[rksl][:, 0:8], tkc, tks, tt, rq_ms)
            return ca, cb, cc

        def post_transpose(tt, ca, cb, cc):
            rot_transpose(*ca, qfm, qsl, tt)
            rot_transpose(*cb, kfmc, kcsl, tt)
            rot_transpose(*cc, kfmp, kpsl, tt)
            qkst["prev_gp"] = ctx.val("gp")

        mm_tt(0)
        mm_tt(1)
        for tt in range(4):
            cs = post_tt(tt)
            if tt + 2 < 4:
                mm_tt(tt + 2)
            post_transpose(tt, *cs)
        pool_rq.release(ctx.val("vt"))
        pool_rk.release(ctx.val("vt"))
        return dict(qsl=qsl, kcsl=kcsl, kpsl=kpsl, vsl=vsl, rqsl=rqsl, rksl=rksl,
                    ready=ctx.val("vt"))

    def attention_win(i, info, prev_info, ytm_sl, ytm_lp):
        b, w = divmod(i, NW)
        dsl, dw = pool_dn.acquire()
        ofsl, ofw = pool_ofm.acquire()
        TT.waitp(info["ready"])
        state = {}
        state2 = {}
        pending_o = []
        obT_pt = {}
        first_v = True
        obank = {}
        LAG = 2

        def sim_unit(u):
            tt, h = divmod(u, 8)
            nonlocal first_v
            pj, pw = pool_sim.acquire()
            TT.waitp(pw)
            hp, ho = divmod(h, 2)
            lhsT = qfm[info["qsl"]][ho * 64: ho * 64 + 64,
                                    hp * 512 + tt * 128: hp * 512 + tt * 128 + 128]
            segs = []
            if prev_info is not None:
                kp = kfmp[prev_info["kpsl"]]
                segs.append((0, 512 - 128 * tt,
                             kp[ho * 64: ho * 64 + 64,
                                hp * 512 + tt * 128: hp * 512 + 512]))
            kc = kfmc[info["kcsl"]]
            segs.append((512 - 128 * tt, 128 * tt + 128,
                         kc[ho * 64: ho * 64 + 64,
                            hp * 512: hp * 512 + 128 * tt + 128]))
            for off, ln, rhs in segs:
                pos = 0
                while pos < ln:
                    oc = off + pos
                    lim = 512 - oc if oc < 512 else 640 - oc
                    take = min(ln - pos, lim)
                    TT.matmul(pairf(pj)[:, oc: oc + take],
                              lhsT, rhs[:, pos: pos + take],
                              start=True, stop=True)
                    pos += take
            TT.inc("pm", 1)
            sim_pm = ctx.val("pm")
            V.waitp(sim_pm)
            if first_v:
                V.waitp(dw)
                first_v = False
            if w == 0:
                V.memset(pairf(pj)[:, 0:512 - 128 * tt], -1e30)
                V.tensor_tensor(pairf(pj)[:, 512:640],
                                pairf(pj)[:, 512:640],
                                maskstack[:, 128:256], ALU.add)
            else:
                edges = pairf(pj)[:, 0:640].rearrange("p (a c) -> p a c", c=128)[:, 0::4, :]
                V.tensor_tensor(edges, edges,
                                maskstack.rearrange("p (a c) -> p a c", c=128), ALU.add)
            V.selfsync("vx")
            V.reduce_max(negm[dsl][:, u:u + 1], pairf(pj)[:, :], AX.X, negate=True)
            mask_ms = V.ms("vm")
            psl, ppw = pool_p.acquire()
            S.waitp(mask_ms)
            S.waitp(ppw)
            S.activation(p_sb[psl][:, :], pairf(pj)[:, :], AF.Exp,
                         bias=negm[dsl][:, u:u + 1], scale=1.0,
                         accum_out=denom[dsl][:, u:u + 1])
            exp_ms = S.ms("sg")
            state[u] = dict(pj=pj, psl=psl, exp_ms=exp_ms)

        def finA_unit(u):
            if pending_o and u >= pending_o[0][0] * 8 + 9:
                fin_ttB(*pending_o.pop(0))
            tt, h = divmod(u, 8)
            st = state.pop(u)
            pj, psl = st["pj"], st["psl"]
            tsl, tw = pool_pTs.acquire()
            TT.waitp(st["exp_ms"])
            for c in range(5):
                TT.transpose(pairtail_b(pj)[:, c * 128:(c + 1) * 128],
                             p_sb[psl][:, c * 128:(c + 1) * 128], identb[:, :])
            TT.inc("pt", 1)
            pool_p.release(ctx.val("pt"))
            V.waitp(ctx.val("pt"))
            V.waitp(tw)
            V.tensor_copy(pTs[tsl][:, :], pairtail_b(pj)[:, :])
            pT_evac = V.ms("vt")
            pool_sim.release(pT_evac)
            state2[u] = (tsl, pT_evac)

        def finB_unit(u):
            tt, h = divmod(u, 8)
            tsl, pT_evac = state2.pop(u)
            TT.waitp(pT_evac)
            if h == 0:
                jb, jw = pool01.acquire()
                TT.waitp(jw)
                obank[tt] = jb
            ob = obank[tt]
            for c in range(5):
                kb = tt + c
                slot = (prev_info["vsl"] if prev_info is not None else info["vsl"]) \
                    if kb < 4 else info["vsl"]
                vtt = kb % 4
                TT.matmul(bank(B01[ob])[:, h * 64:(h + 1) * 64],
                          pTs[tsl][:, c * 128:(c + 1) * 128],
                          vp[slot][:, vtt * 512 + h * 64: vtt * 512 + h * 64 + 64],
                          start=(c == 0), stop=(c == 4))
            TT.inc("pm", 1)
            pool_pTs.release(ctx.val("pm"))
            if h == 7:
                fin_ttA(tt)

        def fin_ttA(tt):
            ob = obank.pop(tt)
            o_pm = ctx.val("pm")
            V.waitp(o_pm)
            V.reciprocal(rdsc[:, :], denom[dsl][:, tt * 8:(tt + 1) * 8])
            V.selfsync("vx")
            V.waitp(obT_pt.pop(tt - 2, None))
            V.tensor_tensor(o_bf[tt % 2][:, :].rearrange("p (h d) -> p h d", d=64),
                            bank(B01[ob]).rearrange("p (h d) -> p h d", d=64),
                            rdsc.unsqueeze(2).broadcast_to([P, 8, 64]), ALU.mult)
            o_done = V.msd("vm")
            pending_o.append((tt, o_done, ob))

        def fin_ttB(tt, o_done, ob):
            TT.waitp(o_done)
            for ic in range(4):
                TT.transpose(bankb(B01[ob])[:, ic * 128:(ic + 1) * 128],
                             o_bf[tt % 2][:, ic * 128:(ic + 1) * 128], identb[:, :])
            TT.inc("pt", 1)
            obT_pt[tt] = ctx.val("pt")
            V.waitp(ctx.val("pt"))
            if tt == 0:
                V.waitp(ofw)
            V.tensor_copy(ofm[ofsl][:, tt * 512:(tt + 1) * 512],
                          bankb(B01[ob])[:, 0:512])
            pool01.release(V.msd("vt"))

        units = list(range(32))
        for u in units:
            if u >= LAG:
                finA_unit(u - LAG)
            sim_unit(u)
            if u >= LAG:
                finB_unit(u - LAG)
        for u in units[-LAG:]:
            finA_unit(u)
            finB_unit(u)
        while pending_o:
            fin_ttB(*pending_o.pop(0))
        pool_dn.release(ctx.val("vm"))
        pool_qfm.release(ctx.val("pm"))
        pool_kfmc.release(ctx.val("pm"))
        ofm_ready = ctx.val("vt")

        for tt in range(4):
            pj, pw = pool_sim.acquire()
            TT.waitp(pw)
            TT.waitp(ofm_ready)
            base = 2 + 2 * pj
            for half in range(2):
                for ic in range(4):
                    TT.matmul(bank(base + half)[:, :],
                              ofm[ofsl][:, tt * 512 + ic * 128: tt * 512 + ic * 128 + 128],
                              wob[:, ic * 1024 + half * 512: ic * 1024 + half * 512 + 512],
                              start=(ic == 0), stop=(ic == 3))
            TT.inc("pm", 1)
            if tt == 3:
                pool_ofm.release(ctx.val("pm"))
            wo_pm = ctx.val("pm")
            V.waitp(wo_pm)
            if tt == 0:
                V.waitp(ytm_lp)
            for half in range(2):
                V.scalar_tensor_tensor(
                    ytm[ytm_sl][:, tt * 1024 + half * 512: tt * 1024 + half * 512 + 512],
                    bank(base + half)[:, :], 1.0,
                    ytm[ytm_sl][:, tt * 1024 + half * 512: tt * 1024 + half * 512 + 512],
                    ALU.mult, ALU.add)
            pool_sim.release(V.ms("vm"))
        store_ready = ctx.val("vm")
        Q.waitp(store_ready)
        so = "so0" if i % 2 == 0 else "so1"
        sdp = Q.dma(s_out[b, w * 512:(w + 1) * 512, :].rearrange("(tt p) d -> p tt d", p=P),
                    ytm[ytm_sl][:, :].rearrange("p (tt d) -> p tt d", d=D), so)
        if DEBUG_STAGE == f"B{layer}":
            sdp = Q.dma(ns["dbg_tap"][b, w * 512:(w + 1) * 512, :]
                        .rearrange("(tt p) d -> p tt d", p=P),
                        ytm[ytm_sl][:, :].rearrange("p (tt d) -> p tt d", d=D), so)
        pool_ytm.release(sdp)
        ns["final_waits"].append(sdp)

    ybf_loads = {0: load_ybf(0), 1: load_ybf(1)}
    ytm_loads = {0: load_ytm(0)}
    fms = {}
    infos = {}
    fms[0] = ty_shift(0, ybf_loads[0], None)
    for i in range(NWIN):
        if i + 1 < NWIN:
            fms[i + 1] = ty_shift(i + 1, ybf_loads[i + 1], fms[i][1])
        infos[i] = qkv_win(i, fms[i][0], fms[i][1])
        fms.pop(i)
        if i >= 1:
            w_prev = (i - 1) % NW
            attention_win(i - 1, infos[i - 1],
                          infos.get(i - 2) if w_prev > 0 else None,
                          ytm_loads[i - 1][0], ytm_loads[i - 1][1])
            if i - 2 >= 0:
                pool_kfmp.release(ctx.val("pm"))
                pool_vp.release(ctx.val("pm"))
            if i - 2 in infos:
                infos.pop(i - 2)
            ytm_loads.pop(i - 1)
        if i + 2 < NWIN:
            ybf_loads[i + 2] = load_ybf(i + 2)
        if i + 1 < NWIN:
            ytm_loads[i + 1] = load_ytm(i + 1)
    w_last = (NWIN - 1) % NW
    attention_win(NWIN - 1, infos[NWIN - 1],
                  infos.get(NWIN - 2) if w_last > 0 else None,
                  ytm_loads[NWIN - 1][0], ytm_loads[NWIN - 1][1])


def phase_C(ctx, TT, V, S, G, Q, ns, layer, s_in, s_out, final_ln):
    nc = ns["nc"]
    ar = ns["arena"]
    ar.reset()
    ar.cur = 0
    w1p = ar.bf16(8 * DFF)
    w2p = ar.bf16(22 * 1024)
    rowsb = ar.bf16(HID)
    gbc = ar.bf16(1024)
    bbc = ar.bf16(1024)
    ytm = [ar.f32(1024) for _ in range(2)]
    ybf = [ar.bf16(1024) for _ in range(3)]
    x1tm = [ar.f32(1024) for _ in range(2)] if final_ln else None
    h2n = [ar.bf16(1024) for _ in range(2)]
    h2nfm = [ar.bf16(1024) for _ in range(2)]
    h4tm = [ar.bf16(2816) for _ in range(2)]
    h4fm = ar.bf16(2816)
    gl = [ar.bf16(512) for _ in range(2)]

    psA = ns["psA"]
    psAb = psA.ap().bitcast(BF16)

    def bank(j):
        return psA.ap()[:, j * 512:(j + 1) * 512]

    def bankb(j):
        return psAb[:, j * 1024:(j + 1) * 1024]

    identb = ns["identb"]
    shmatb = ns["shmatb"]
    onesb = ns["onesb"]
    stats = ns["stats"]
    scb = ns["bscr"]

    NT = B2 * T // P

    for kk in range(8):
        G.dma(w1p[:, kk * DFF:(kk + 1) * DFF], ns["w1"][layer, kk * 128:(kk + 1) * 128, :], "lbw")
    for kk in range(21):
        G.dma(w2p[:, kk * 1024:(kk + 1) * 1024], ns["w2"][layer, kk * 128:(kk + 1) * 128, :], "lbw")
    G.dma(w2p[:, 21 * 1024:22 * 1024][0:42, :], ns["w2"][layer, 21 * 128:2730, :], "lbw")
    G.dma(rowsb[0:1, 0:HID], ns["b1"][layer, 0:HID].unsqueeze(0), "lbw")
    G.dma(rowsb[32:33, 0:HID], ns["b1"][layer, HID:DFF].unsqueeze(0), "lbw")
    G.dma(rowsb[64:65, 0:1024], ns["b2"][layer].unsqueeze(0), "lbw")
    G.dma(ns["scrap1k"][0:1, 0:1024], ns["ln_g"][layer].unsqueeze(0), "lbw")
    G.dma(ns["scrap1k"][32:33, 0:1024], ns["ln_b"][layer].unsqueeze(0), "lbw")
    lbw_now = ctx.val("lbw")
    TT.wait_now("lbw")
    TT.matmul(bank(0)[:, :], onesb[0:1, 0:128], ns["scrap1k"][0:1, 0:512], start=True, stop=True)
    TT.matmul(bank(1)[:, :], onesb[0:1, 0:128], ns["scrap1k"][0:1, 512:1024], start=True, stop=True)
    TT.matmul(bank(2)[:, :], onesb[32:33, 0:128], ns["scrap1k"][32:33, 0:512], start=True, stop=True)
    TT.matmul(bank(3)[:, :], onesb[32:33, 0:128], ns["scrap1k"][32:33, 512:1024], start=True, stop=True)
    TT.inc("pq", 1)
    V.waitp(ctx.val("pq"))
    V.tensor_copy(gbc[:, 0:512], bank(0)[:, :])
    V.tensor_copy(gbc[:, 512:1024], bank(1)[:, :])
    V.tensor_copy(bbc[:, 0:512], bank(2)[:, :])
    V.tensor_copy(bbc[:, 512:1024], bank(3)[:, :])
    V.memset(h4tm[0][:, 2730:2816], 0.0)
    V.memset(h4tm[1][:, 2730:2816], 0.0)
    cw_ready = V.ms("vx")

    pool_ytm = Rot(2)
    pool_ybf = Rot(3)
    pool_x1 = Rot(2)
    pool_h2n = Rot(2)
    pool_h2nfm = Rot(2)
    pool_h4 = Rot(2)
    pool_gl = Rot(2)
    pool01 = Rot(2, seed=cw_ready)
    B01 = [0, 1]
    pool_pair = Rot(2, seed=cw_ready)
    pool_st = Rot(2)
    lysem = ["ly0", "ly1", "ly2"]
    lbsem = ["lb0", "lb1", "lb2"]
    lxsem = ["lx0", "lx1"]

    def load_tile(i):
        sl, wp = pool_ytm.acquire()
        Q.waitp(wp)
        b, t0 = divmod(i * P, T)
        lp = Q.dma(ytm[sl][:, :], s_in[b, t0:t0 + P, :], lysem[sl])
        sl2, wp2 = pool_ybf.acquire()
        G.waitp(wp2)
        lp2 = G.dma(ybf[sl2][:, :], s_in[b, t0:t0 + P, :], lbsem[sl2])
        ret = [sl, lp, sl2, lp2, None, None]
        if final_ln:
            sl3, wp3 = pool_x1.acquire()
            Q.waitp(wp3)
            lp3 = Q.dma(x1tm[sl3][:, :], ns["s_x1"][b, t0:t0 + P, :], lxsem[sl3])
            ret[4], ret[5] = sl3, lp3
        return ret

    def pre_tile(i, ld):
        b, t0 = divmod(i * P, T)
        sl, lp, sl2, lp2, sl3, lp3 = ld
        sbase = 32 + (i % 2) * 16
        stw = pool_st.acquire()[1]
        sbk = bank(6 + i % 2)
        TT.waitp(lp2)
        TT.waitp(gst["shift_free"].pop(i - 2, None))
        if i == 0:
            TT.waitp(cw_ready)
        first = (t0 == 0)
        TT.matmul(sbk[:, :], shmatb[:, 0:128], ybf[sl2][:, 512:1024],
                  start=True, stop=first)
        if not first:
            TT.waitp(gst["prev_ybf"][0])
            TT.matmul(sbk[:, :], shmatb[:, 128:256],
                      ybf[gst["prev_ybf"][1]][:, 512:1024], start=False, stop=True)
        TT.inc("pm", 1)
        sh_pm = ctx.val("pm")
        if i >= 1:
            pool_ybf.release(sh_pm)
        gst["prev_ybf"] = (lp2, sl2, sh_pm)
        S.waitp(sh_pm)
        S.waitp(stw)
        if i == 0:
            S.waitp(cw_ready)
        S.activation(ns["scrap1k"][:, 0:512], ybf[sl2][:, 0:512], AF.Square,
                     accum_out=stats[:, sbase + 2: sbase + 3])
        S.activation(ns["scrap1k"][:, 512:1024], sbk[:, :], AF.Square,
                     accum_out=stats[:, sbase + 3: sbase + 4])
        sq_ms = S.ms("sq")
        V.waitp(sq_ms)
        V.reduce_sum(stats[:, sbase + 0: sbase + 1], ybf[sl2][:, 0:512], AX.X)
        V.reduce_sum(stats[:, sbase + 1: sbase + 2], sbk[:, :], AX.X)
        V.selfsync("vx")
        V.tensor_tensor(stats[:, sbase + 4: sbase + 5], stats[:, sbase + 0: sbase + 1],
                        stats[:, sbase + 1: sbase + 2], ALU.add)
        V.tensor_tensor(stats[:, sbase + 5: sbase + 6], stats[:, sbase + 2: sbase + 3],
                        stats[:, sbase + 3: sbase + 4], ALU.add)
        V.selfsync("vx")
        V.tensor_scalar(stats[:, sbase + 6: sbase + 7], stats[:, sbase + 4: sbase + 5],
                        1.0 / D, None, ALU.mult)
        V.tensor_scalar(stats[:, sbase + 7: sbase + 8], stats[:, sbase + 5: sbase + 6],
                        1.0 / D, None, ALU.mult)
        V.selfsync("vx")
        V.tensor_tensor(stats[:, sbase + 8: sbase + 9], stats[:, sbase + 6: sbase + 7],
                        stats[:, sbase + 6: sbase + 7], ALU.mult)
        V.selfsync("vx")
        V.tensor_tensor(stats[:, sbase + 9: sbase + 10], stats[:, sbase + 7: sbase + 8],
                        stats[:, sbase + 8: sbase + 9], ALU.subtract)
        vstat = V.ms("vx")
        S.waitp(vstat)
        S.activation(stats[:, sbase + 10: sbase + 11], stats[:, sbase + 9: sbase + 10],
                     AF.Ln, bias=ns["epsc"][:, 0:1])
        S.selfsync("sq")
        S.activation(stats[:, sbase + 11: sbase + 12], stats[:, sbase + 10: sbase + 11],
                     AF.Exp, scale=-0.5)
        rstd_ms = S.ms("sq")
        h2sl, h2w = pool_h2n.acquire()
        V.waitp(rstd_ms)
        V.waitp(h2w)
        V.tensor_scalar(h2n[h2sl][:, 0:512], ybf[sl2][:, 0:512],
                        stats[:, sbase + 6: sbase + 7], stats[:, sbase + 11: sbase + 12],
                        ALU.subtract, ALU.mult)
        V.tensor_scalar(h2n[h2sl][:, 512:1024], sbk[:, :],
                        stats[:, sbase + 6: sbase + 7], stats[:, sbase + 11: sbase + 12],
                        ALU.subtract, ALU.mult)
        apl_ms = V.msd("vm")
        gst["shift_free"][i] = apl_ms
        G.waitp(apl_ms)
        G.tensor_tensor(h2n[h2sl][:, :], h2n[h2sl][:, :], gbc[:, :], ALU.mult)
        G.selfsync("gp")
        G.tensor_tensor(h2n[h2sl][:, :], h2n[h2sl][:, :], bbc[:, :], ALU.add)
        gdone = G.ms("gp")
        gst["pre"][i] = (sl, lp, sl3, lp3, sbase, h2sl, gdone)

    def preT_tile(i):
        _, _, _, _, _, h2sl, gdone = gst["pre"][i]
        fmsl, fmw = pool_h2nfm.acquire()
        jt, wt = pool01.acquire()
        TT.waitp(gdone)
        TT.waitp(wt)
        TT.waitp(fmw)
        for kk in range(8):
            TT.transpose(bankb(B01[jt])[:, kk * 128:(kk + 1) * 128],
                         h2n[h2sl][:, kk * 128:(kk + 1) * 128], identb[:, :])
        TT.inc("pt", 1)
        pool_h2n.release(ctx.val("pt"))
        V.waitp(ctx.val("pt"))
        V.tensor_copy(h2nfm[fmsl][:, :], bankb(B01[jt])[:, :])
        h2nfm_ms = V.msd("vt")
        pool01.release(h2nfm_ms)
        gst["fm"][i] = (fmsl, h2nfm_ms)

    def gemm_tile(i):
        b, t0 = divmod(i * P, T)
        sl, lp, sl3, lp3, sbase, h2sl, gdone = gst["pre"].pop(i)
        fmsl, h2nfm_ms = gst["fm"].pop(i)
        h4sl, h4w = pool_h4.acquire()
        TT.waitp(h2nfm_ms)
        segs = [(512 * jj, min(512, HID - 512 * jj)) for jj in range(6)]
        for si, (c0, ln) in enumerate(segs):
            pj, pw = pool_pair.acquire()
            TT.waitp(pw)
            base = 2 + 2 * pj
            for half, cc in ((0, c0), (1, HID + c0)):
                for kk in range(8):
                    TT.matmul(bank(base + half)[:, 0:ln],
                              h2nfm[fmsl][:, kk * 128:(kk + 1) * 128],
                              w1p[:, kk * DFF + cc: kk * DFF + cc + ln],
                              start=(kk == 0), stop=False)
                hp = 0 if half == 0 else 32
                TT.matmul(bank(base + half)[:, 0:ln], onesb[hp:hp + 1, 0:128],
                          rowsb[hp:hp + 1, c0: c0 + ln], start=False, stop=True)
            TT.inc("pm", 1)
            w1_pm = ctx.val("pm")
            glsl, glw = pool_gl.acquire()
            S.waitp(w1_pm)
            S.waitp(glw)
            S.activation(gl[glsl][:, 0:ln], bank(base + 1)[:, 0:ln], AF.Gelu)
            gelu_ms = S.ms("sg")
            V.waitp(gelu_ms)
            if si == 0:
                V.waitp(h4w)
            V.tensor_tensor(h4tm[h4sl][:, c0:c0 + ln], bank(base)[:, 0:ln],
                            gl[glsl][:, 0:ln], ALU.mult)
            vg = V.msd("vm")
            pool_gl.release(vg)
            pool_pair.release(vg)
        pool_h2nfm.release(ctx.val("pm"))
        h4_ms = ctx.val("vm")
        pj, pw = pool_pair.acquire()
        base = 2 + 2 * pj
        TT.waitp(pw)
        h4fm_parts = []
        for grp in range(3):
            jt2, wt2 = pool01.acquire()
            TT.waitp(wt2)
            if grp == 0:
                TT.waitp(h4_ms)
            n_in = min(8, 22 - grp * 8)
            for u in range(n_in):
                TT.transpose(bankb(B01[jt2])[:, u * 128:(u + 1) * 128],
                             h4tm[h4sl][:, (grp * 8 + u) * 128:(grp * 8 + u + 1) * 128],
                             identb[:, :])
            TT.inc("pt", 1)
            V.waitp(ctx.val("pt"))
            if grp == 0:
                V.waitp(gst["prev_w2"])
            V.tensor_copy(h4fm[:, grp * 1024: grp * 1024 + n_in * 128],
                          bankb(B01[jt2])[:, 0:n_in * 128])
            h4fm_parts.append(V.msd("vt"))
            pool01.release(h4fm_parts[-1])
        pool_h4.release(ctx.val("pt"))
        TT.waitp(h4fm_parts[-1])
        for half in range(2):
            for kk in range(22):
                kcount = 128 if kk < 21 else 42
                TT.matmul(bank(base + half)[:, :],
                          h4fm[0:kcount, kk * 128:(kk + 1) * 128],
                          w2p[0:kcount, kk * 1024 + half * 512: kk * 1024 + half * 512 + 512],
                          start=(kk == 0), stop=False)
            TT.matmul(bank(base + half)[:, :], onesb[64:65, 0:128],
                      rowsb[64:65, half * 512:(half + 1) * 512], start=False, stop=True)
        TT.inc("pm", 1)
        gst["prev_w2"] = ctx.val("pm")
        w2_pm = ctx.val("pm")
        V.waitp(w2_pm)
        V.waitp(lp)
        for half in range(2):
            V.scalar_tensor_tensor(ytm[sl][:, half * 512:(half + 1) * 512],
                                   bank(base + half)[:, :], 1.0,
                                   ytm[sl][:, half * 512:(half + 1) * 512],
                                   ALU.mult, ALU.add)
        pool_pair.release(V.msd("vm"))
        gst["fin"][i] = (sl, sl3, lp3, sbase, b, t0)

    def fin_store(i):
        sl, sl3, lp3, sbase, b, t0 = gst["fin"].pop(i)
        if final_ln:
            V.waitp(lp3)
            V.selfsync("vm")
            V.scalar_tensor_tensor(ytm[sl][:, :], x1tm[sl3][:, :], 1.0, ytm[sl][:, :],
                                   ALU.mult, ALU.add,
                                   accum_out=stats[:, sbase + 0: sbase + 1])
            fv_ms = V.ms("vm")
            pool_x1.release(fv_ms)
            S.waitp(fv_ms)
            S.activation(ns["scrap1k"][:, :], ytm[sl][:, :], AF.Square,
                         accum_out=stats[:, sbase + 1: sbase + 2])
            fsq = S.ms("sq")
            V.waitp(fsq)
            V.tensor_scalar(stats[:, sbase + 6: sbase + 7], stats[:, sbase + 0: sbase + 1],
                            1.0 / D, None, ALU.mult)
            V.tensor_scalar(stats[:, sbase + 7: sbase + 8], stats[:, sbase + 1: sbase + 2],
                            1.0 / D, None, ALU.mult)
            V.selfsync("vx")
            V.tensor_tensor(stats[:, sbase + 8: sbase + 9], stats[:, sbase + 6: sbase + 7],
                            stats[:, sbase + 6: sbase + 7], ALU.mult)
            V.selfsync("vx")
            V.tensor_tensor(stats[:, sbase + 9: sbase + 10], stats[:, sbase + 7: sbase + 8],
                            stats[:, sbase + 8: sbase + 9], ALU.subtract)
            fstat = V.ms("vx")
            S.waitp(fstat)
            S.activation(stats[:, sbase + 10: sbase + 11], stats[:, sbase + 9: sbase + 10],
                         AF.Ln, bias=ns["epsc"][:, 0:1])
            S.selfsync("sq")
            S.activation(stats[:, sbase + 11: sbase + 12], stats[:, sbase + 10: sbase + 11],
                         AF.Exp, scale=-0.5)
            frstd = S.ms("sq")
            V.waitp(frstd)
            V.tensor_scalar(ytm[sl][:, :], ytm[sl][:, :],
                            stats[:, sbase + 6: sbase + 7], stats[:, sbase + 11: sbase + 12],
                            ALU.subtract, ALU.mult)
        stile_ms = V.ms("vm")
        pool_st.release(stile_ms)
        Q.waitp(stile_ms)
        so = "so0" if i % 2 == 0 else "so1"
        dst = ns["out"] if final_ln else s_out
        sdp = Q.dma(dst[b, t0:t0 + P, :], ytm[sl][:, :], so)
        if DEBUG_STAGE == f"C{layer}" and not final_ln:
            sdp = Q.dma(ns["dbg_tap"][b, t0:t0 + P, :], ytm[sl][:, :], so)
        pool_ytm.release(sdp)
        ns["final_waits"].append(sdp)

    gst = {"prev_w2": None, "prev_ybf": (None, None, None),
           "shift_free": {}, "pre": {}, "fm": {}, "fin": {}}
    pre_tile(0, load_tile(0))
    pre_tile(1, load_tile(1))
    preT_tile(0)
    for i in range(NT):
        gemm_tile(i)
        if i + 1 < NT:
            preT_tile(i + 1)
        fin_store(i)
        if i + 2 < NT:
            pre_tile(i + 2, load_tile(i + 2))


def make_inputs_maps(inputs):
    hc = host_consts()
    maps = []
    x = np.ascontiguousarray(inputs["L0_attn_x"])
    full = dict(
        log_lambda=inputs["L0_attn_log_lambda"], b_w=inputs["L0_attn_b_w"],
        b_b=inputs["L0_attn_b_b"], c_w=inputs["L0_attn_c_w"], c_b=inputs["L0_attn_c_b"],
        wqkv=inputs["L0_ffn_wqkv"], q_scale=inputs["L0_ffn_q_scale"],
        k_scale=inputs["L0_ffn_k_scale"], wo=inputs["L0_ffn_wo"],
        ln_g=inputs["L0_ffn_ln_g"], ln_b=inputs["L0_ffn_ln_b"],
        w1=inputs["L0_ffn_w1"], b1=inputs["L0_ffn_b1"],
        w2=inputs["L0_ffn_w2"], b2=inputs["L0_ffn_b2"],
        **hc,
    )
    full = {k: np.ascontiguousarray(np.asarray(v, dtype=np.float32)) for k, v in full.items()}
    for c in range(8):
        m = dict(full)
        m["x0"] = x[2 * c:2 * c + 2]
        maps.append(m)
    return maps


def kernel(**inputs):
    nc = build_program()
    maps = make_inputs_maps(inputs)
    res = run_bass_kernel_spmd(nc, maps, core_ids=list(range(8)))
    return np.concatenate([r["out"] for r in res.results], axis=0)


if __name__ == "__main__":
    print("module ok")


# version 18 (final)
# speedup vs baseline: 1.0015x; 1.0015x over previous
import os
import numpy as np
import ml_dtypes
import concourse.bass as bass
import concourse.mybir as mybir
from concourse.bass_utils import run_bass_kernel_spmd
from contextlib import ExitStack

F32 = mybir.dt.float32
BF16 = mybir.dt.bfloat16
AF = mybir.ActivationFunctionType
ALU = mybir.AluOpType
AX = mybir.AxisListType
BF = ml_dtypes.bfloat16

P = 128
D = 1024
T = 4096
B2 = 2
W = 512
NW = T // W
HEADS = 8
DH = 64
INNER = 512
HID = 2730
DFF = 5460
QK_SCALE = 8.0
EPS = 1e-5

DEBUG_STAGE = os.environ.get("MK_DEBUG_STAGE", "full")


def host_consts():
    c = {}
    c["ident"] = np.eye(128, dtype=np.float32)
    half = np.arange(0, DH, 2, dtype=np.float64)
    inv_freq = 1.0 / 10000.0 ** (half / DH)
    base = (half + 0.4 * DH) / (1.4 * DH)
    sgn = np.concatenate([-np.ones(32), np.ones(32)])[None, :]

    def rot_tables(idx):
        freqs = idx[:, None] * inv_freq[None, :]
        power = idx / float(W // 2)
        scale = base[None, :] ** power[:, None]
        cos2 = np.concatenate([np.cos(freqs)] * 2, axis=1)
        sin2 = np.concatenate([np.sin(freqs)] * 2, axis=1)
        sc2 = np.concatenate([scale] * 2, axis=1)
        return cos2, sin2, sc2

    qi = np.arange(W, dtype=np.float64)
    cq, sq_, scq = rot_tables(qi)
    c["tqc"] = (QK_SCALE * cq * scq).astype(np.float32)
    c["tqs"] = (QK_SCALE * sq_ * scq * sgn).astype(np.float32)
    ki = np.arange(2 * W, dtype=np.float64) - W
    ck, sk, sck = rot_tables(ki)
    c["tkc"] = (ck / sck).astype(np.float32)
    c["tks"] = (sk / sck * sgn).astype(np.float32)
    i = np.arange(128)[:, None]
    j = np.arange(128)[None, :]
    c["mask_diag"] = np.where(j <= i, 0.0, -1e30).astype(np.float32)
    c["mask_trail"] = np.where(j >= i, 0.0, -1e30).astype(np.float32)
    sh = np.zeros((128, 128), dtype=np.float32)
    sh[np.arange(127), np.arange(1, 128)] = 1.0
    c["shiftT"] = sh
    b127 = np.zeros((128, 128), dtype=np.float32)
    b127[127, 0] = 1.0
    c["b127"] = b127
    return c


class Eng:
    def __init__(self, ctx, key):
        self.ctx = ctx
        self.key = key
        self.last = None

    @property
    def active(self):
        return self.ctx.active == self.key

    def __getattr__(self, name):
        def call(*args, **kw):
            if self.active:
                self.last = getattr(self.ctx.handles[self.key], name)(*args, **kw)
                self.last_has_inc = False
                return self.last
            return None
        return call

    def wait(self, sem, val):
        if val is None or val <= 0:
            return
        if self.active:
            self.ctx.handles[self.key].wait_ge(self.ctx.sems[sem], val)

    def waitp(self, pair):
        if pair is None:
            return
        if isinstance(pair, list):
            for p in pair:
                self.waitp(p)
            return
        sem, val = pair
        self.wait(sem, val)

    def inc(self, sem, n=1):
        self.ctx.counts[sem] += n
        if self.active:
            assert self.last is not None, f"inc({sem}) with no prior op on {self.key}"
            if getattr(self, "last_has_inc", False):
                self.last = self.ctx.handles[self.key].nop()
            self.last.then_inc(self.ctx.sems[sem], n)
            self.last_has_inc = True

    def ms(self, sem):
        self.inc(sem, 1)
        if self.active:
            h = self.ctx.handles[self.key]
            h.nop().then_inc(self.ctx.sems[sem], 1)
        self.ctx.counts[sem] += 1
        return (sem, self.ctx.counts[sem])

    def msd(self, sem):
        self.inc(sem, 1)
        if self.active:
            h = self.ctx.handles[self.key]
            h.drain()
            h.nop().then_inc(self.ctx.sems[sem], 1)
        self.ctx.counts[sem] += 1
        return (sem, self.ctx.counts[sem])

    def selfsync(self, sem):
        self.inc(sem, 1)
        if self.active:
            self.ctx.handles[self.key].wait_ge(self.ctx.sems[sem], self.ctx.counts[sem])

    def wait_now(self, sem):
        self.wait(sem, self.ctx.counts[sem])

    def dma(self, out, in_, sem, n=16, **kw):
        self.ctx.counts[sem] += n
        if self.active:
            self.last = self.ctx.handles[self.key].dma_start(out=out, in_=in_, **kw)
            self.last.then_inc(self.ctx.sems[sem], n)
            self.last_has_inc = True
        return (sem, self.ctx.counts[sem])


class Ctx:
    def __init__(self, sem_names):
        self.sem_names = sem_names
        self.sems = None
        self.counts = {k: 0 for k in sem_names}
        self.active = None
        self.handles = {}

    def val(self, sem):
        return (sem, self.counts[sem])


class Rot:

    def __init__(self, n, seed=None):
        self.n = n
        self.hist = [seed] * n if seed is not None else []
        self.i = n if seed is not None else 0
        self.presends = self.i

    def acquire(self):
        idx = self.i % self.n
        if self.i >= self.n:
            assert len(self.hist) > self.i - self.n, (
                "resource acquired before its prior user's release was recorded")
            waitp = self.hist[self.i - self.n]
        else:
            waitp = None
        self.i += 1
        return idx, waitp

    def release(self, pair):
        self.hist.append(pair)
        assert len(self.hist) <= self.i, "release() without matching acquire()"


class Arena:

    def __init__(self, handle, total_f32cols, reserved_f32cols):
        self.h = handle
        self.total = total_f32cols
        self.base = reserved_f32cols
        self.cur = reserved_f32cols

    def reset(self):
        self.cur = self.base

    def f32(self, cols):
        o = self.cur
        self.cur += cols
        assert self.cur <= self.total, f"arena overflow: {self.cur} > {self.total}"
        return self.h[:, o:o + cols]

    def bf16(self, cols):
        f32c = (cols + 1) // 2
        o = self.cur
        self.cur += f32c
        assert self.cur <= self.total, f"arena overflow: {self.cur} > {self.total}"
        return self.h[:, o:o + f32c].bitcast(BF16)[:, 0:cols]

    def reserved_f32(self, off, cols):
        assert off + cols <= self.base
        return self.h[:, off:off + cols]

    def reserved_bf16(self, off, cols):
        assert off + (cols + 1) // 2 <= self.base
        return self.h[:, off:off + (cols + 1) // 2].bitcast(BF16)[:, 0:cols]


def build_program():
    nc = bass.Bass("TRN2", target_bir_lowering=False)

    ns = {}
    ns["nc"] = nc

    def din(name, shape):
        ns[name] = nc.dram_tensor(name, shape, F32, kind="ExternalInput").ap()

    din("x0", [B2, T, D])
    din("log_lambda", [D]); din("b_w", [D, D]); din("b_b", [D]); din("c_w", [D, D]); din("c_b", [D])
    din("wqkv", [2, D, 3 * INNER]); din("q_scale", [2, DH]); din("k_scale", [2, DH])
    din("wo", [2, INNER, D]); din("ln_g", [2, D]); din("ln_b", [2, D])
    din("w1", [2, D, DFF]); din("b1", [2, DFF]); din("w2", [2, HID, D]); din("b2", [2, D])
    din("ident", [128, 128])
    din("tqc", [W, DH]); din("tqs", [W, DH]); din("tkc", [2 * W, DH]); din("tks", [2 * W, DH])
    din("mask_diag", [128, 128]); din("mask_trail", [128, 128])
    din("shiftT", [128, 128]); din("b127", [128, 128])

    ns["out"] = nc.dram_tensor("out", [B2, T, D], F32, kind="ExternalOutput").ap()
    if DEBUG_STAGE != "full":
        ns["dbg_tap"] = nc.dram_tensor("dbg_tap", [B2, T, D], F32, kind="ExternalOutput").ap()
    ns["s_x1"] = nc.dram_tensor("s_x1", [B2, T, D], F32, kind="Internal").ap()
    ns["s_ya"] = nc.dram_tensor("s_ya", [B2, T, D], F32, kind="Internal").ap()
    ns["s_yb"] = nc.dram_tensor("s_yb", [B2, T, D], F32, kind="Internal").ap()

    sem_names = ["lw", "lwg", "li", "lib", "lwc", "lbw", "lqs", "ly0", "ly1", "ly2", "lb0", "lb1", "lb2",
                 "lx0", "lx1", "le0", "le1", "so0", "so1", "pb",
                 "pt", "pm", "pq", "vt", "vm", "vx", "sq", "sg", "gp"]
    ctx = Ctx(sem_names)
    TT, V, S, G, Q = (Eng(ctx, k) for k in ["t", "v", "s", "g", "q"])

    with ExitStack() as big:
        sems = {n: big.enter_context(nc.semaphore(n)) for n in sem_names}
        ctx.sems = sems

        sb = lambda nm, shp, dt: big.enter_context(nc.sbuf_tensor(nm, shp, dt))
        ns["identf"] = sb("identf", [P, 128], F32)
        ns["identb"] = sb("identb", [P, 128], BF16)
        ns["maskstack"] = sb("maskstack", [P, 256], F32)
        ns["shmatb"] = sb("shmatb", [P, 256], BF16)
        ns["lamf"] = sb("lamf", [P, 8], F32)
        ns["omlf"] = sb("omlf", [P, 8], F32)
        ns["bbf"] = sb("bbf", [P, 8], F32)
        ns["cbf"] = sb("cbf", [P, 8], F32)
        ns["onesf"] = sb("onesf", [P, 512], F32)
        ns["stats"] = sb("stats", [P, 128], F32)
        ns["epsc"] = sb("epsc", [P, 1], F32)
        ns["bscr"] = sb("bscr", [P, 8], F32)
        ns["scrap1k"] = sb("scrap1k", [P, 1024], BF16)
        ns["onesb"] = sb("onesb", [P, 512], BF16)
        ns["rowbuf"] = sb("rowbuf", [P, 2048], F32)
        ns["cbrow"] = sb("cbrow", [P, 1024], BF16)
        ns["bbrow"] = sb("bbrow", [P, 1024], BF16)
        ARENA_COLS = 48000
        RESV = 8192
        arena_h = sb("arena", [P, ARENA_COLS], F32)
        ns["arena"] = Arena(arena_h, ARENA_COLS, RESV)
        ns["wqkvb"] = ns["arena"].reserved_bf16(0, 8 * 1536)
        ns["wob"] = ns["arena"].reserved_bf16(6144, 4 * D)
        ns["psA"] = big.enter_context(nc.psum_tensor("psA", [P, 4096], F32))
        ns["big"] = big

        def emit(active, handle):
            ctx.active = active
            ctx.handles = {active: handle}
            for k in sem_names:
                ctx.counts[k] = 0
            program(ctx, TT, V, S, G, Q, ns)

        with nc.Block() as block:
            @block.tensor
            def _(tensor):
                emit("t", tensor)

            @block.vector
            def _(vector):
                emit("v", vector)

            @block.scalar
            def _(scalar):
                emit("s", scalar)

            @block.gpsimd
            def _(gpsimd):
                emit("g", gpsimd)

            @block.sync
            def _(sync):
                emit("q", sync)

    return nc


def phase_barrier(ctx, TT, V, S, G, Q, ns):
    bscr = ns["bscr"]
    Q.wait_now("so0")
    Q.wait_now("so1")
    V.tensor_copy(bscr[:, 0:1], bscr[:, 1:2])
    V.inc("pb", 1)
    S.copy(bscr[:, 2:3], bscr[:, 3:4])
    S.inc("pb", 1)
    G.tensor_copy(bscr[:, 4:5], bscr[:, 5:6])
    G.inc("pb", 1)
    Q.nop()
    Q.inc("pb", 1)
    TT.wait("pb", ctx.counts["pb"])
    TT.transpose(ns["psA"].ap()[:, 0:128], ns["identf"][:, :], ns["identf"][:, :])
    TT.inc("pb", 1)
    target = ctx.counts["pb"]
    for E in (V, S, G, Q):
        E.wait("pb", target)


def program(ctx, TT, V, S, G, Q, ns):
    phase_setup(ctx, TT, V, S, G, Q, ns)
    phase_A(ctx, TT, V, S, G, Q, ns)
    if DEBUG_STAGE != "A":
        phase_barrier(ctx, TT, V, S, G, Q, ns)
        phase_B(ctx, TT, V, S, G, Q, ns, 0, ns["s_x1"], ns["s_ya"])
    if DEBUG_STAGE not in ("A", "B0"):
        phase_barrier(ctx, TT, V, S, G, Q, ns)
        phase_C(ctx, TT, V, S, G, Q, ns, 0, ns["s_ya"], ns["s_yb"],
                final_ln=False)
    if DEBUG_STAGE == "full" or DEBUG_STAGE in ("B1", "C1"):
        phase_barrier(ctx, TT, V, S, G, Q, ns)
        phase_B(ctx, TT, V, S, G, Q, ns, 1, ns["s_yb"], ns["s_ya"])
        phase_barrier(ctx, TT, V, S, G, Q, ns)
        phase_C(ctx, TT, V, S, G, Q, ns, 1, ns["s_ya"], None,
                final_ln=True)
    fin = {}
    for sem, val in ns.get("final_waits", []):
        fin[sem] = max(fin.get(sem, 0), val)
    for sem, val in sorted(fin.items()):
        G.wait(sem, val)


def phase_setup(ctx, TT, V, S, G, Q, ns):
    ns["final_waits"] = []
    Q.dma(ns["identf"][:, :], ns["ident"][:, :], "lw")
    Q.dma(ns["maskstack"][:, 0:128], ns["mask_trail"][:, :], "lw")
    Q.dma(ns["maskstack"][:, 128:256], ns["mask_diag"][:, :], "lw")
    Q.dma(ns["lamf"][:, :], ns["log_lambda"].rearrange("(c p) -> p c", p=P), "lw", allow_slow_non_contiguous=True)
    Q.dma(ns["bbf"][:, :], ns["b_b"].rearrange("(c p) -> p c", p=P), "lw", allow_slow_non_contiguous=True)
    Q.dma(ns["cbf"][:, :], ns["c_b"].rearrange("(c p) -> p c", p=P), "lw", allow_slow_non_contiguous=True)
    G.dma(ns["identb"][:, :], ns["ident"][:, :], "li")
    G.dma(ns["shmatb"][:, 0:128], ns["shiftT"][:, :], "li")
    G.dma(ns["shmatb"][:, 128:256], ns["b127"][:, :], "li")
    G.dma(ns["cbrow"][0:1, 0:1024], ns["c_b"].unsqueeze(0), "li")
    G.dma(ns["rowbuf"][0:1, 0:1024], ns["log_lambda"].unsqueeze(0), "li")
    lw0 = [ctx.val("lw"), ctx.val("li")]
    ns["setup_lw"] = lw0

    stats = ns["stats"]
    S.wait_now("lw"); S.wait_now("li")
    S.activation(stats[:, 0:8], ns["lamf"][:, :], AF.Exp)
    S.selfsync("sq")
    S.activation(stats[:, 8:16], stats[:, 0:8], AF.Exp, scale=-1.0)
    sq0 = S.ms("sq")
    V.waitp(sq0)
    V.wait_now("lw"); V.wait_now("li")
    V.tensor_scalar(stats[:, 16:24], stats[:, 8:16], 1.0, None, ALU.add)
    V.selfsync("vx")
    V.reciprocal(ns["lamf"][:, :], stats[:, 16:24])
    V.selfsync("vx")
    V.tensor_scalar(ns["omlf"][:, :], ns["lamf"][:, :], -1.0, 1.0, ALU.mult, ALU.add)
    V.memset(ns["onesf"][:, :], 1.0)
    V.memset(ns["epsc"][:, :], EPS)
    V.memset(ns["bscr"][:, :], 0.0)
    V.memset(ns["onesb"][:, :], 1.0)
    ns["setup_vx"] = V.ms("vx")
    rb = ns["rowbuf"]
    S.activation(rb[0:1, 1024:2048], rb[0:1, 0:1024], AF.Exp)
    S.selfsync("sq")
    S.activation(rb[0:1, 0:1024], rb[0:1, 1024:2048], AF.Exp, scale=-1.0)
    srow = S.ms("sq")
    V.waitp(srow)
    V.tensor_scalar(rb[0:1, 1024:2048], rb[0:1, 0:1024], 1.0, None, ALU.add)
    V.selfsync("vx")
    V.reciprocal(rb[0:1, 0:1024], rb[0:1, 1024:2048])
    V.selfsync("vx")
    V.tensor_scalar(rb[0:1, 1024:2048], rb[0:1, 0:1024], -1.0, 1.0, ALU.mult, ALU.add)
    vrow = V.ms("vx")
    G.waitp(vrow)
    G.dma(ns["bbrow"][0:1, 0:1024], ns["b_b"].unsqueeze(0), "lib")
    V.wait_now("lib")
    V.tensor_tensor(ns["bbrow"][0:1, 0:1024], ns["bbrow"][0:1, 0:1024],
                    rb[0:1, 1024:2048], ALU.mult)
    ns["rows_ready"] = V.ms("vx")


def phase_A(ctx, TT, V, S, G, Q, ns):
    nc = ns["nc"]
    if True:
        ar = ns["arena"]
        ar.reset()
        bwt = ar.bf16(64 * 128)
        cwt = ar.bf16(64 * 128)
        w_tm = ar.bf16(1024)
        lam_bc = ar.f32(8 * 512)
        x0tm = [ar.f32(4 * 1024) for j in range(3)]
        x0fm = [ar.bf16(8 * 512) for j in range(2)]
        hfm = ar.f32(8 * 512)
        carry = ar.f32(8)
        hbf = [ar.bf16(8 * 512) for j in range(2)]
        scrapb = ar.bf16(1024)

        psA = ns["psA"]
        psAb = psA.ap().bitcast(BF16)
        pstb = [psAb[:, 0:1024], psAb[:, 1024:2048]]
        pstf = [psA.ap()[:, 1024:1536], psA.ap()[:, 1536:2048]]
        psxb = [psA.ap()[:, 2048:2560], psA.ap()[:, 2560:3072]]
        psat = [psA.ap()[:, 3072:3584], psA.ap()[:, 3584:4096]]

        x0_d, s_x1 = ns["x0"], ns["s_x1"]
        identf, identb = ns["identf"], ns["identb"]
        lamf, omlf = ns["lamf"], ns["omlf"]
        stats = ns["stats"]

        rows_ready = ns["rows_ready"]
        onesb, cbrow, bbrow = ns["onesb"], ns["cbrow"], ns["bbrow"]

        V.waitp(ns["setup_vx"])
        for ee in range(8):
            V.tensor_scalar_mul(lam_bc[:, ee * 512:(ee + 1) * 512],
                                ns["onesf"][:, :], lamf[:, ee:ee + 1])
        V.selfsync("vx")
        lam_ready = V.ms("vx")

        pool_wT = Rot(2)
        pool_wtm = Rot(1)
        wt_done = {}
        for wi, (w_d, dstt) in enumerate([(ns["b_w"], bwt), (ns["c_w"], cwt)]):
            for ee in range(8):
                _, wstg = pool_wtm.acquire()
                G.waitp(wstg)
                lwp = G.dma(w_tm[:, :], w_d[ee * 128:(ee + 1) * 128, :], "lwc")
                if wi == 0:
                    V.waitp(lwp)
                    V.waitp(ns["setup_vx"])
                    V.tensor_scalar_mul(w_tm[:, :], w_tm[:, :], omlf[:, ee:ee + 1])
                    wrd = V.ms("vt")
                    TT.waitp(wrd)
                else:
                    TT.waitp(lwp)
                TT.wait_now("li")
                for g in range(2):
                    j, wp = pool_wT.acquire()
                    TT.waitp(wp)
                    for u in range(4):
                        kk = g * 4 + u
                        TT.transpose(pstb[j][:, u * 256 if False else u * 128 + 0: u * 128 + 128]
                                     if False else pstb[j][:, u * 128: u * 128 + 128],
                                     w_tm[:, kk * 128: kk * 128 + 128],
                                     identb[:, :])
                    TT.inc("pt", 1)
                    V.waitp(ctx.val("pt"))
                    V.tensor_copy(dstt[:, :].rearrange("p (k e c) -> p k e c", e=8, c=128)
                                  [:, g * 4:(g + 1) * 4, ee, :],
                                  pstb[j][:, 0:512].rearrange("p (k c) -> p k c", c=128))
                    pool_wT.release(V.msd("vt"))
                pool_wtm.release(ctx.val("pt"))
            wt_done[wi] = ctx.val("vt")

        NCH = B2 * NW
        pool_tf = Rot(2)
        pool_xb = Rot(2)
        pool_at = Rot(2)
        pool_x0tm = Rot(3)
        pool_x0fm = Rot(2)
        pool_hbf = Rot(2)
        lx_sem = ["lx0", "lx1", "le0"]

        def load_x0(i):
            sl, wp = pool_x0tm.acquire()
            Q.waitp(wp)
            b, c = divmod(i, NW)
            lp = Q.dma(x0tm[sl][:, :].rearrange("p (tt d) -> p tt d", d=D),
                       x0_d[b, c * 512:(c + 1) * 512, :].rearrange("(tt p) d -> p tt d", p=P),
                       lx_sem[sl])
            return sl, lp

        def tgroups(i, x0sl, lxp):
            fmsl, fmw = pool_x0fm.acquire()
            TT.waitp(lxp)
            TT.waitp(fmw)
            for g in range(8):
                j, wp = pool_tf.acquire()
                TT.waitp(wp)
                for tt in range(4):
                    TT.transpose(pstf[j][:, tt * 128:(tt + 1) * 128],
                                 x0tm[x0sl][:, tt * 1024 + g * 128: tt * 1024 + g * 128 + 128],
                                 identf[:, :])
                TT.inc("pt", 1)
                V.waitp(ctx.val("pt"))
                V.tensor_copy(x0fm[fmsl][:, g * 512:(g + 1) * 512], pstf[j][:, :])
                pool_tf.release(V.msd("vt"))
            return fmsl, ctx.val("vt")

        def xb_chunk(i, fmsl, tfm_done, prev_cast):
            b, c = divmod(i, NW)
            hsl, hw = pool_hbf.acquire()
            TT.waitp(hw)
            TT.waitp(tfm_done)
            if i == 0:
                TT.waitp(wt_done[0])
                TT.waitp(rows_ready)
            for ee in range(8):
                j, wp = pool_xb.acquire()
                TT.waitp(wp)
                for kk in range(8):
                    TT.matmul(psxb[j][:, :],
                              bwt[:, (kk * 8 + ee) * 128:(kk * 8 + ee + 1) * 128],
                              x0fm[fmsl][:, kk * 512:(kk + 1) * 512],
                              start=(kk == 0), stop=False)
                TT.matmul(psxb[j][:, :], bbrow[0:1, ee * 128:(ee + 1) * 128],
                          onesb[0:1, 0:512], start=False, stop=True)
                TT.inc("pm", 1)
                if ee == 7:
                    pool_x0fm.release(ctx.val("pm"))
                V.waitp(ctx.val("pm"))
                if prev_cast is not None:
                    V.waitp(prev_cast)
                    prev_cast = None
                init = 0.0 if c == 0 else carry[:, ee:ee + 1]
                V.tensor_tensor_scan(hfm[:, ee * 512:(ee + 1) * 512],
                                     lam_bc[:, ee * 512:(ee + 1) * 512],
                                     psxb[j][:, :],
                                     init, ALU.mult, ALU.add)
                V.selfsync("vx")
                V.tensor_copy(carry[:, ee:ee + 1], hfm[:, ee * 512 + 511: ee * 512 + 512])
                V.selfsync("vx")
                pool_xb.release(V.msd("vm"))
                scp = ctx.val("vm")
                S.waitp(scp)
                S.copy(hbf[hsl][:, ee * 512:(ee + 1) * 512], hfm[:, ee * 512:(ee + 1) * 512])
                S.ms("sg")
            return hsl, ctx.val("sg")

        def cw_chunk(i, x0sl, hsl, hready):
            b, c = divmod(i, NW)
            TT.waitp(hready)
            if i == 0:
                TT.waitp(wt_done[1])
                TT.wait_now("li")
                TT.waitp(ones_vx)
            sbase = 32 + (i % 2) * 48
            for tt in range(4):
                for nt in range(2):
                    j, wp = pool_at.acquire()
                    TT.waitp(wp)
                    for ee in range(8):
                        TT.matmul(psat[j][:, :],
                                  hbf[hsl][:, ee * 512 + tt * 128: ee * 512 + tt * 128 + 128],
                                  cwt[:, (ee * 8 + nt * 4) * 128:(ee * 8 + nt * 4 + 4) * 128],
                                  start=(ee == 0), stop=False)
                    TT.matmul(psat[j][:, :], onesb[0:1, 0:128],
                              cbrow[0:1, nt * 512:(nt + 1) * 512],
                              start=False, stop=True)
                    TT.inc("pm", 1)
                    if tt == 3 and nt == 1:
                        pool_hbf.release(ctx.val("pm"))
                    V.waitp(ctx.val("pm"))
                    V.scalar_tensor_tensor(
                        x0tm[x0sl][:, tt * 1024 + nt * 512: tt * 1024 + nt * 512 + 512],
                        psat[j][:, :], 1.0,
                        x0tm[x0sl][:, tt * 1024 + nt * 512: tt * 1024 + nt * 512 + 512],
                        ALU.mult, ALU.add,
                        accum_out=stats[:, sbase + tt * 2 + nt: sbase + tt * 2 + nt + 1])
                    pool_at.release(V.msd("vm"))
                svp = ctx.val("vm")
                S.waitp(svp)
                S.activation(scrapb[:, :], x0tm[x0sl][:, tt * 1024:(tt + 1) * 1024], AF.Square,
                             accum_out=stats[:, sbase + 8 + tt: sbase + 8 + tt + 1])
                S.ms("sq")
            sqv = ctx.val("sq")
            V.waitp(sqv)
            V.tensor_tensor(stats[:, sbase + 12: sbase + 16],
                            stats[:, sbase: sbase + 8].rearrange("p (a b) -> p a b", b=2)[:, :, 0],
                            stats[:, sbase: sbase + 8].rearrange("p (a b) -> p a b", b=2)[:, :, 1],
                            ALU.add)
            V.selfsync("vx")
            V.tensor_scalar(stats[:, sbase + 16: sbase + 20], stats[:, sbase + 12: sbase + 16],
                            1.0 / D, None, ALU.mult)
            V.selfsync("vx")
            V.tensor_tensor(stats[:, sbase + 20: sbase + 24],
                            stats[:, sbase + 16: sbase + 20], stats[:, sbase + 16: sbase + 20],
                            ALU.mult)
            V.tensor_scalar(stats[:, sbase + 24: sbase + 28], stats[:, sbase + 8: sbase + 12],
                            1.0 / D, None, ALU.mult)
            V.selfsync("vx")
            V.tensor_tensor(stats[:, sbase + 28: sbase + 32], stats[:, sbase + 24: sbase + 28],
                            stats[:, sbase + 20: sbase + 24], ALU.subtract)
            vstat = V.ms("vx")
            S.waitp(vstat)
            S.activation(stats[:, sbase + 32: sbase + 36], stats[:, sbase + 28: sbase + 32],
                         AF.Ln, bias=ns["epsc"][:, 0:1])
            S.selfsync("sq")
            S.activation(stats[:, sbase + 36: sbase + 40], stats[:, sbase + 32: sbase + 36],
                         AF.Exp, scale=-0.5)
            srstd = S.ms("sq")
            V.waitp(srstd)
            for tt in range(4):
                V.tensor_scalar(x0tm[x0sl][:, tt * 1024:(tt + 1) * 1024],
                                x0tm[x0sl][:, tt * 1024:(tt + 1) * 1024],
                                stats[:, sbase + 16 + tt: sbase + 17 + tt],
                                stats[:, sbase + 36 + tt: sbase + 37 + tt],
                                ALU.subtract, ALU.mult)
            apl = V.ms("vm")
            Q.waitp(apl)
            so = "so0" if i % 2 == 0 else "so1"
            sdp = Q.dma(s_x1[b, c * 512:(c + 1) * 512, :].rearrange("(tt p) d -> p tt d", p=P),
                        x0tm[x0sl][:, :].rearrange("p (tt d) -> p tt d", d=D), so)
            if DEBUG_STAGE == "A":
                sdp = Q.dma(ns["dbg_tap"][b, c * 512:(c + 1) * 512, :]
                            .rearrange("(tt p) d -> p tt d", p=P),
                            x0tm[x0sl][:, :].rearrange("p (tt d) -> p tt d", d=D), so)
            pool_x0tm.release(sdp)
            ns["final_waits"].append(sdp)

        ones_vx = rows_ready

        loads = {0: load_x0(0), 1: load_x0(1)}
        fms = {}
        hs = {}
        casts = {-1: None}
        fms[0] = tgroups(0, loads[0][0], loads[0][1])
        for i in range(NCH):
            if i + 1 < NCH:
                fms[i + 1] = tgroups(i + 1, loads[i + 1][0], loads[i + 1][1])
            hs[i] = xb_chunk(i, *fms.pop(i), casts[i - 1])
            casts[i] = hs[i][1]
            if i >= 1:
                cw_chunk(i - 1, loads[i - 1][0], *hs.pop(i - 1))
                del loads[i - 1]
            if i + 2 < NCH:
                loads[i + 2] = load_x0(i + 2)
        cw_chunk(NCH - 1, loads[NCH - 1][0], *hs.pop(NCH - 1))


def phase_B(ctx, TT, V, S, G, Q, ns, layer, s_in, s_out):
    nc = ns["nc"]
    ar = ns["arena"]
    ar.reset()
    ytm = [ar.f32(4 * 1024) for _ in range(2)]
    ybf = [ar.bf16(4 * 1024) for _ in range(2)]
    yfm = [ar.bf16(8 * 513) for _ in range(2)]
    tqc = ar.bf16(4 * 64); tqs = ar.bf16(4 * 64)
    tkc = ar.bf16(8 * 64); tks = ar.bf16(8 * 64)
    qsb = ar.f32(64); qss = ar.f32(64); ksb = ar.f32(64); kss = ar.f32(64)
    qe = ar.f32(512); ke = ar.f32(512)
    t1 = [ar.f32(512) for _ in range(2)]
    t2 = [ar.f32(512) for _ in range(2)]
    sqt = ar.f32(1024)
    qp_tm = [ar.bf16(512) for _ in range(3)]
    qfm = [ar.bf16(4 * 512) for _ in range(2)]
    kfmc = [ar.bf16(4 * 512) for _ in range(2)]
    kfmp = [ar.bf16(4 * 512) for _ in range(3)]
    vp = [ar.bf16(4 * 512) for _ in range(3)]
    p_sb = [ar.bf16(640) for _ in range(3)]
    pTs = [ar.bf16(640) for _ in range(3)]
    o_bf = [ar.bf16(512) for _ in range(2)]
    ofm = [ar.bf16(4 * 512) for _ in range(2)]
    denom = [ar.f32(32) for _ in range(2)]
    negm = [ar.f32(32) for _ in range(2)]
    rsqq = [ar.f32(8) for _ in range(2)]
    rsqk = [ar.f32(8) for _ in range(2)]
    rdsc = ar.f32(8)

    psA = ns["psA"]
    psAb = psA.ap().bitcast(BF16)

    def bank(j):
        return psA.ap()[:, j * 512:(j + 1) * 512]

    def bankb(j):
        return psAb[:, j * 1024:(j + 1) * 1024]

    def pairf(j):
        b = 2 + 2 * j
        return psA.ap()[:, b * 512: b * 512 + 640]

    def pairtail_b(j):
        b = 2 + 2 * j
        return psAb[:, b * 1024 + 1280: b * 1024 + 1280 + 640]

    identb = ns["identb"]
    maskstack = ns["maskstack"]
    wqkvb, wob = ns["wqkvb"], ns["wob"]
    onesb = ns["onesb"]
    rowb = ns["rowbuf"]

    NWIN = B2 * NW

    for kk in range(8):
        G.dma(wqkvb[:, kk * 1536:(kk + 1) * 1536],
              ns["wqkv"][layer, kk * 128:(kk + 1) * 128, :], "lbw")
    for kk in range(4):
        G.dma(wob[:, kk * 1024:(kk + 1) * 1024],
              ns["wo"][layer, kk * 128:(kk + 1) * 128, :], "lbw")
    G.dma(tqc[:, :].rearrange("p (tt d) -> p tt d", d=64),
          ns["tqc"].rearrange("(tt p) d -> p tt d", p=P), "lbw")
    G.dma(tqs[:, :].rearrange("p (tt d) -> p tt d", d=64),
          ns["tqs"].rearrange("(tt p) d -> p tt d", p=P), "lbw")
    G.dma(tkc[:, :].rearrange("p (tt d) -> p tt d", d=64),
          ns["tkc"].rearrange("(tt p) d -> p tt d", p=P), "lbw")
    G.dma(tks[:, :].rearrange("p (tt d) -> p tt d", d=64),
          ns["tks"].rearrange("(tt p) d -> p tt d", p=P), "lbw")
    Q.dma(rowb[0:1, 0:64], ns["q_scale"][layer].unsqueeze(0), "lqs")
    Q.dma(rowb[0:1, 64:128], ns["k_scale"][layer].unsqueeze(0), "lqs")

    TT.wait_now("lqs")
    TT.wait_now("lbw")
    TT.matmul(bank(0)[:, 0:64], ns["onesf"][0:1, 0:128], rowb[0:1, 0:64], start=True, stop=True)
    TT.matmul(bank(0)[:, 64:128], ns["onesf"][0:1, 0:128], rowb[0:1, 64:128], start=True, stop=True)
    TT.inc("pq", 1)
    V.waitp(ctx.val("pq"))
    V.tensor_copy(qsb[:, :], bank(0)[:, 0:64])
    V.tensor_copy(ksb[:, :], bank(0)[:, 64:128])
    V.selfsync("vx")
    V.tensor_copy(qss[:, 0:32], qsb[:, 32:64])
    V.tensor_copy(qss[:, 32:64], qsb[:, 0:32])
    V.tensor_copy(kss[:, 0:32], ksb[:, 32:64])
    V.tensor_copy(kss[:, 32:64], ksb[:, 0:32])
    V.selfsync("vx")
    V.tensor_tensor(tqc.rearrange("p (a d) -> p a d", d=64),
                    tqc.rearrange("p (a d) -> p a d", d=64),
                    qsb.unsqueeze(1).broadcast_to([P, 4, 64]), ALU.mult)
    V.tensor_tensor(tqs.rearrange("p (a d) -> p a d", d=64),
                    tqs.rearrange("p (a d) -> p a d", d=64),
                    qss.unsqueeze(1).broadcast_to([P, 4, 64]), ALU.mult)
    V.tensor_tensor(tkc.rearrange("p (a d) -> p a d", d=64),
                    tkc.rearrange("p (a d) -> p a d", d=64),
                    ksb.unsqueeze(1).broadcast_to([P, 8, 64]), ALU.mult)
    V.tensor_tensor(tks.rearrange("p (a d) -> p a d", d=64),
                    tks.rearrange("p (a d) -> p a d", d=64),
                    kss.unsqueeze(1).broadcast_to([P, 8, 64]), ALU.mult)
    for b_ in kfmp + vp:
        V.memset(b_[:, :], 0.0)
    tables_ready = V.ms("vx")

    pool_ytm = Rot(2)
    pool_ybf = Rot(2)
    pool_yfm = Rot(2)
    B01 = [0, 1]
    pool01 = Rot(2, seed=tables_ready)
    pool_sim = Rot(3, seed=tables_ready)
    pool_qfm = Rot(2)
    pool_kfmc = Rot(2)
    pool_kfmp = Rot(3)
    pool_vp = Rot(3)
    pool_p = Rot(3)
    pool_pTs = Rot(3)
    pool_ofm = Rot(2)
    pool_dn = Rot(2)
    pool_rq = Rot(2)
    pool_rk = Rot(2)
    lysem = ["ly0", "ly1"]
    lbsem = ["lb0", "lb1"]

    def load_ytm(i):
        sl, wp = pool_ytm.acquire()
        Q.waitp(wp)
        b, w = divmod(i, NW)
        lp = Q.dma(ytm[sl][:, :].rearrange("p (tt d) -> p tt d", d=D),
                   s_in[b, w * 512:(w + 1) * 512, :].rearrange("(tt p) d -> p tt d", p=P),
                   lysem[sl])
        return (sl, lp)

    def load_ybf(i):
        sl2, wp2 = pool_ybf.acquire()
        G.waitp(wp2)
        b, w = divmod(i, NW)
        lp2 = G.dma(ybf[sl2][:, :].rearrange("p (tt d) -> p tt d", d=D),
                    s_in[b, w * 512:(w + 1) * 512, :].rearrange("(tt p) d -> p tt d", p=P),
                    lbsem[sl2])
        return (sl2, lp2)

    def yfm3(fsl):
        return yfm[fsl].rearrange("p (kk c) -> p kk c", c=513)

    def ty_shift(i, ld, prev_ready):
        sl2, lp2 = ld
        b, w = divmod(i, NW)
        fsl, fw = pool_yfm.acquire()
        V.waitp(fw)
        if w == 0:
            V.memset(yfm3(fsl)[:, :, 0:1], 0.0)
        else:
            V.waitp(prev_ready)
            V.tensor_copy(yfm3(fsl)[:, :, 0:1], yfm3(1 - fsl)[:, :, 512:513])
        carry_done = V.ms("vt")
        TT.waitp(lp2)
        for g in range(4):
            j, wp = pool01.acquire()
            TT.waitp(wp)
            for u in range(8):
                kk = 2 * g + u // 4
                tt = u % 4
                TT.transpose(bankb(B01[j])[:, u * 128:(u + 1) * 128],
                             ybf[sl2][:, tt * 1024 + kk * 128: tt * 1024 + kk * 128 + 128],
                             identb[:, :])
            TT.inc("pt", 1)
            S.waitp(ctx.val("pt"))
            if g == 0:
                S.waitp(carry_done)
            S.copy(yfm3(fsl)[:, 2 * g:2 * g + 2, 1:513],
                   bankb(B01[j]).rearrange("p (a c) -> p a c", c=512))
            pool01.release(S.msd("sg"))
        pool_ybf.release(ctx.val("pt"))
        return fsl, ctx.val("sg"), ctx.val("pt")

    rp_state = {"n": 0, "add_ms": {}, "pt_ms": {}}

    def rot_compute(src_sb, src_ready, rsq_ap, tc, ts, tcol, rsq_ms=None):
        n = rp_state["n"]
        rp_state["n"] += 1
        sl = n % 3
        tsl = n % 2
        G.waitp(src_ready)
        G.waitp(rp_state["add_ms"].pop(n - 2, None))
        G.tensor_tensor(t1[tsl][:, :].rearrange("p (h d) -> p h d", d=64),
                        src_sb.rearrange("p (h d) -> p h d", d=64),
                        tc.rearrange("p (a d) -> p a d", d=64)
                        [:, tcol:tcol + 1, :].broadcast_to([P, 8, 64]), ALU.mult)
        rot_src = src_sb.rearrange("p (h u v) -> p h u v", u=2, v=32)[:, :, ::-1, :]
        G.tensor_tensor(t2[tsl][:, :].rearrange("p (h u v) -> p h u v", u=2, v=32),
                        rot_src,
                        ts.rearrange("p (a d) -> p a d", d=64)
                        [:, tcol:tcol + 1, :].rearrange("p a (u v) -> p a u v", v=32)
                        .broadcast_to([P, 8, 2, 32]), ALU.mult)
        gdone = G.ms("gp")
        V.waitp(gdone)
        V.tensor_tensor(t1[tsl][:, :], t1[tsl][:, :], t2[tsl][:, :], ALU.add)
        V.selfsync("vx")
        V.waitp(rsq_ms)
        V.waitp(rp_state["pt_ms"].pop(n - 3, None))
        V.tensor_tensor(qp_tm[sl][:, :].rearrange("p (h d) -> p h d", d=64),
                        t1[tsl][:, :].rearrange("p (h d) -> p h d", d=64),
                        rsq_ap.unsqueeze(2).broadcast_to([P, 8, 64]), ALU.mult)
        tm_done = V.ms("vt")
        rp_state["add_ms"][n] = tm_done
        return n, sl, tm_done

    def rot_transpose(n, sl, tm_done, dst_fm, dslot, dst_col):
        jb, wb = pool01.acquire()
        TT.waitp(tm_done)
        TT.waitp(wb)
        for hp in range(4):
            TT.transpose(bankb(B01[jb])[:, hp * 128:(hp + 1) * 128],
                         qp_tm[sl][:, hp * 128:(hp + 1) * 128], identb[:, :])
        TT.inc("pt", 1)
        rp_state["pt_ms"][n] = ctx.val("pt")
        V.waitp(ctx.val("pt"))
        V.tensor_copy(dst_fm[dslot].rearrange("p (hp c) -> p hp c", c=512)
                      [:, :, dst_col * 128:(dst_col + 1) * 128],
                      bankb(B01[jb])[:, 0:512].rearrange("p (hp c) -> p hp c", c=128))
        pool01.release(V.msd("vt"))

    qkst = {"prev_gp": None}

    def qkv_win(i, fsl, yfm_ready):
        b, w = divmod(i, NW)
        qsl, qw = pool_qfm.acquire()
        kcsl, kcw = pool_kfmc.acquire()
        kpsl, kpw = pool_kfmp.acquire()
        vsl, vw = pool_vp.acquire()
        rqsl, rqw = pool_rq.acquire()
        rksl, rkw = pool_rk.acquire()
        TT.waitp(yfm_ready)
        if i == 0:
            TT.waitp(tables_ready)
        mmst = {}

        def lhs(kk, tt):
            sh = -1 if kk >= 4 else 0
            return yfm[fsl][:, kk * 513 + 1 + tt * 128 + sh:
                            kk * 513 + 1 + tt * 128 + sh + 128]

        def mm_tt(tt):
            pj, pw = pool_sim.acquire()
            TT.waitp(pw)
            vb, vbw = pool01.acquire()
            TT.waitp(vbw)
            base = 2 + 2 * pj
            for col, bk in ((0, base), (512, base + 1), (1024, B01[vb])):
                for kk in range(8):
                    TT.matmul(bank(bk)[:, :], lhs(kk, tt),
                              wqkvb[:, kk * 1536 + col: kk * 1536 + col + 512],
                              start=(kk == 0), stop=(kk == 7))
                TT.inc("pm", 1)
            qkv_pm = ctx.val("pm")
            if tt == 3:
                pool_yfm.release(qkv_pm)
            S.waitp(qkv_pm)
            if tt == 0:
                S.waitp(vw)
            S.copy(vp[vsl][:, tt * 512:(tt + 1) * 512], bank(B01[vb])[:, :])
            vp_evac = S.ms("sg")
            pool01.release(vp_evac)
            mmst[tt] = (pj, qkv_pm)

        def post_tt(tt):
            pj, qkv_pm = mmst.pop(tt)
            base = 2 + 2 * pj
            S.waitp(qkv_pm)
            if tt == 0:
                S.waitp(qw); S.waitp(kcw); S.waitp(kpw)
                S.waitp(rqw); S.waitp(rkw)
            S.waitp(qkst["prev_gp"])
            S.copy(qe[:, :], bank(base)[:, :])
            S.copy(ke[:, :], bank(base + 1)[:, :])
            qk_evac = S.ms("sg")
            pool_sim.release(qk_evac)
            S.activation(sqt[:, 0:512], qe[:, :], AF.Square)
            S.activation(sqt[:, 512:1024], ke[:, :], AF.Square)
            sq_done = S.ms("sq")
            V.waitp(sq_done)
            V.tensor_reduce(rsqq[rqsl][:, 0:8], sqt[:, 0:512].rearrange("p (h d) -> p h d", d=64),
                            AX.X, ALU.add)
            V.tensor_reduce(rsqk[rksl][:, 0:8], sqt[:, 512:1024].rearrange("p (h d) -> p h d", d=64),
                            AX.X, ALU.add)
            V.selfsync("vx")
            V.tensor_scalar_max(rsqq[rqsl][:, 0:8], rsqq[rqsl][:, 0:8], 1e-24)
            V.tensor_scalar_max(rsqk[rksl][:, 0:8], rsqk[rksl][:, 0:8], 1e-24)
            vr = V.ms("vx")
            S.waitp(vr)
            S.activation(rsqq[rqsl][:, 0:8], rsqq[rqsl][:, 0:8], AF.Ln)
            S.activation(rsqk[rksl][:, 0:8], rsqk[rksl][:, 0:8], AF.Ln)
            S.selfsync("sq")
            S.activation(rsqq[rqsl][:, 0:8], rsqq[rqsl][:, 0:8], AF.Exp, scale=-0.5)
            S.activation(rsqk[rksl][:, 0:8], rsqk[rksl][:, 0:8], AF.Exp, scale=-0.5)
            rq_ms = S.ms("sq")
            ca = rot_compute(qe[:, :], qk_evac, rsqq[rqsl][:, 0:8], tqc, tqs, tt, rq_ms)
            cb = rot_compute(ke[:, :], qk_evac, rsqk[rksl][:, 0:8], tkc, tks, 4 + tt, rq_ms)
            cc = rot_compute(ke[:, :], qk_evac, rsqk[rksl][:, 0:8], tkc, tks, tt, rq_ms)
            return ca, cb, cc

        def post_transpose(tt, ca, cb, cc):
            rot_transpose(*ca, qfm, qsl, tt)
            rot_transpose(*cb, kfmc, kcsl, tt)
            rot_transpose(*cc, kfmp, kpsl, tt)
            qkst["prev_gp"] = ctx.val("gp")

        mm_tt(0)
        mm_tt(1)
        for tt in range(4):
            cs = post_tt(tt)
            if tt + 2 < 4:
                mm_tt(tt + 2)
            post_transpose(tt, *cs)
        pool_rq.release(ctx.val("vt"))
        pool_rk.release(ctx.val("vt"))
        return dict(qsl=qsl, kcsl=kcsl, kpsl=kpsl, vsl=vsl, rqsl=rqsl, rksl=rksl,
                    ready=ctx.val("vt"))

    def attention_win(i, info, prev_info, ytm_sl, ytm_lp):
        b, w = divmod(i, NW)
        dsl, dw = pool_dn.acquire()
        ofsl, ofw = pool_ofm.acquire()
        TT.waitp(info["ready"])
        state = {}
        state2 = {}
        pending_o = []
        obT_pt = {}
        first_v = True
        obank = {}
        LAG = 2

        def sim_unit(u):
            tt, h = divmod(u, 8)
            nonlocal first_v
            pj, pw = pool_sim.acquire()
            TT.waitp(pw)
            hp, ho = divmod(h, 2)
            lhsT = qfm[info["qsl"]][ho * 64: ho * 64 + 64,
                                    hp * 512 + tt * 128: hp * 512 + tt * 128 + 128]
            segs = []
            if prev_info is not None:
                kp = kfmp[prev_info["kpsl"]]
                segs.append((0, 512 - 128 * tt,
                             kp[ho * 64: ho * 64 + 64,
                                hp * 512 + tt * 128: hp * 512 + 512]))
            kc = kfmc[info["kcsl"]]
            segs.append((512 - 128 * tt, 128 * tt + 128,
                         kc[ho * 64: ho * 64 + 64,
                            hp * 512: hp * 512 + 128 * tt + 128]))
            for off, ln, rhs in segs:
                pos = 0
                while pos < ln:
                    oc = off + pos
                    lim = 512 - oc if oc < 512 else 640 - oc
                    take = min(ln - pos, lim)
                    TT.matmul(pairf(pj)[:, oc: oc + take],
                              lhsT, rhs[:, pos: pos + take],
                              start=True, stop=True)
                    pos += take
            TT.inc("pm", 1)
            sim_pm = ctx.val("pm")
            V.waitp(sim_pm)
            if first_v:
                V.waitp(dw)
                first_v = False
            if w == 0:
                V.memset(pairf(pj)[:, 0:512 - 128 * tt], -1e30)
                V.tensor_tensor(pairf(pj)[:, 512:640],
                                pairf(pj)[:, 512:640],
                                maskstack[:, 128:256], ALU.add)
            else:
                edges = pairf(pj)[:, 0:640].rearrange("p (a c) -> p a c", c=128)[:, 0::4, :]
                V.tensor_tensor(edges, edges,
                                maskstack.rearrange("p (a c) -> p a c", c=128), ALU.add)
            V.selfsync("vx")
            V.reduce_max(negm[dsl][:, u:u + 1], pairf(pj)[:, :], AX.X, negate=True)
            mask_ms = V.ms("vm")
            psl, ppw = pool_p.acquire()
            S.waitp(mask_ms)
            S.waitp(ppw)
            S.activation(p_sb[psl][:, :], pairf(pj)[:, :], AF.Exp,
                         bias=negm[dsl][:, u:u + 1], scale=1.0,
                         accum_out=denom[dsl][:, u:u + 1])
            exp_ms = S.ms("sg")
            state[u] = dict(pj=pj, psl=psl, exp_ms=exp_ms)

        def finA_unit(u):
            if pending_o and u >= pending_o[0][0] * 8 + 10:
                fin_ttB(*pending_o.pop(0))
            tt, h = divmod(u, 8)
            st = state.pop(u)
            pj, psl = st["pj"], st["psl"]
            tsl, tw = pool_pTs.acquire()
            TT.waitp(st["exp_ms"])
            for c in range(5):
                TT.transpose(pairtail_b(pj)[:, c * 128:(c + 1) * 128],
                             p_sb[psl][:, c * 128:(c + 1) * 128], identb[:, :])
            TT.inc("pt", 1)
            pool_p.release(ctx.val("pt"))
            V.waitp(ctx.val("pt"))
            V.waitp(tw)
            V.tensor_copy(pTs[tsl][:, :], pairtail_b(pj)[:, :])
            pT_evac = V.ms("vt")
            pool_sim.release(pT_evac)
            state2[u] = (tsl, pT_evac)

        def finB_unit(u):
            tt, h = divmod(u, 8)
            tsl, pT_evac = state2.pop(u)
            TT.waitp(pT_evac)
            if h == 0:
                jb, jw = pool01.acquire()
                TT.waitp(jw)
                obank[tt] = jb
            ob = obank[tt]
            for c in range(5):
                kb = tt + c
                slot = (prev_info["vsl"] if prev_info is not None else info["vsl"]) \
                    if kb < 4 else info["vsl"]
                vtt = kb % 4
                TT.matmul(bank(B01[ob])[:, h * 64:(h + 1) * 64],
                          pTs[tsl][:, c * 128:(c + 1) * 128],
                          vp[slot][:, vtt * 512 + h * 64: vtt * 512 + h * 64 + 64],
                          start=(c == 0), stop=(c == 4))
            TT.inc("pm", 1)
            pool_pTs.release(ctx.val("pm"))
            if h == 7:
                fin_ttA(tt)

        def fin_ttA(tt):
            ob = obank.pop(tt)
            o_pm = ctx.val("pm")
            V.waitp(o_pm)
            V.reciprocal(rdsc[:, :], denom[dsl][:, tt * 8:(tt + 1) * 8])
            V.selfsync("vx")
            V.waitp(obT_pt.pop(tt - 2, None))
            V.tensor_tensor(o_bf[tt % 2][:, :].rearrange("p (h d) -> p h d", d=64),
                            bank(B01[ob]).rearrange("p (h d) -> p h d", d=64),
                            rdsc.unsqueeze(2).broadcast_to([P, 8, 64]), ALU.mult)
            o_done = V.msd("vm")
            pending_o.append((tt, o_done, ob))

        def fin_ttB(tt, o_done, ob):
            TT.waitp(o_done)
            for ic in range(4):
                TT.transpose(bankb(B01[ob])[:, ic * 128:(ic + 1) * 128],
                             o_bf[tt % 2][:, ic * 128:(ic + 1) * 128], identb[:, :])
            TT.inc("pt", 1)
            obT_pt[tt] = ctx.val("pt")
            V.waitp(ctx.val("pt"))
            if tt == 0:
                V.waitp(ofw)
            V.tensor_copy(ofm[ofsl][:, tt * 512:(tt + 1) * 512],
                          bankb(B01[ob])[:, 0:512])
            pool01.release(V.msd("vt"))

        units = list(range(32))
        for u in units:
            if u >= LAG:
                finA_unit(u - LAG)
            sim_unit(u)
            if u >= LAG:
                finB_unit(u - LAG)
        for u in units[-LAG:]:
            finA_unit(u)
            finB_unit(u)
        while pending_o:
            fin_ttB(*pending_o.pop(0))
        pool_dn.release(ctx.val("vm"))
        pool_qfm.release(ctx.val("pm"))
        pool_kfmc.release(ctx.val("pm"))
        ofm_ready = ctx.val("vt")

        for tt in range(4):
            pj, pw = pool_sim.acquire()
            TT.waitp(pw)
            TT.waitp(ofm_ready)
            base = 2 + 2 * pj
            for half in range(2):
                for ic in range(4):
                    TT.matmul(bank(base + half)[:, :],
                              ofm[ofsl][:, tt * 512 + ic * 128: tt * 512 + ic * 128 + 128],
                              wob[:, ic * 1024 + half * 512: ic * 1024 + half * 512 + 512],
                              start=(ic == 0), stop=(ic == 3))
            TT.inc("pm", 1)
            if tt == 3:
                pool_ofm.release(ctx.val("pm"))
            wo_pm = ctx.val("pm")
            V.waitp(wo_pm)
            if tt == 0:
                V.waitp(ytm_lp)
            for half in range(2):
                V.scalar_tensor_tensor(
                    ytm[ytm_sl][:, tt * 1024 + half * 512: tt * 1024 + half * 512 + 512],
                    bank(base + half)[:, :], 1.0,
                    ytm[ytm_sl][:, tt * 1024 + half * 512: tt * 1024 + half * 512 + 512],
                    ALU.mult, ALU.add)
            pool_sim.release(V.ms("vm"))
        store_ready = ctx.val("vm")
        Q.waitp(store_ready)
        so = "so0" if i % 2 == 0 else "so1"
        sdp = Q.dma(s_out[b, w * 512:(w + 1) * 512, :].rearrange("(tt p) d -> p tt d", p=P),
                    ytm[ytm_sl][:, :].rearrange("p (tt d) -> p tt d", d=D), so)
        if DEBUG_STAGE == f"B{layer}":
            sdp = Q.dma(ns["dbg_tap"][b, w * 512:(w + 1) * 512, :]
                        .rearrange("(tt p) d -> p tt d", p=P),
                        ytm[ytm_sl][:, :].rearrange("p (tt d) -> p tt d", d=D), so)
        pool_ytm.release(sdp)
        ns["final_waits"].append(sdp)

    ybf_loads = {0: load_ybf(0), 1: load_ybf(1)}
    ytm_loads = {0: load_ytm(0)}
    fms = {}
    infos = {}
    fms[0] = ty_shift(0, ybf_loads[0], None)
    for i in range(NWIN):
        if i + 1 < NWIN:
            fms[i + 1] = ty_shift(i + 1, ybf_loads[i + 1], fms[i][1])
        infos[i] = qkv_win(i, fms[i][0], fms[i][1])
        fms.pop(i)
        if i >= 1:
            w_prev = (i - 1) % NW
            attention_win(i - 1, infos[i - 1],
                          infos.get(i - 2) if w_prev > 0 else None,
                          ytm_loads[i - 1][0], ytm_loads[i - 1][1])
            if i - 2 >= 0:
                pool_kfmp.release(ctx.val("pm"))
                pool_vp.release(ctx.val("pm"))
            if i - 2 in infos:
                infos.pop(i - 2)
            ytm_loads.pop(i - 1)
        if i + 2 < NWIN:
            ybf_loads[i + 2] = load_ybf(i + 2)
        if i + 1 < NWIN:
            ytm_loads[i + 1] = load_ytm(i + 1)
    w_last = (NWIN - 1) % NW
    attention_win(NWIN - 1, infos[NWIN - 1],
                  infos.get(NWIN - 2) if w_last > 0 else None,
                  ytm_loads[NWIN - 1][0], ytm_loads[NWIN - 1][1])


def phase_C(ctx, TT, V, S, G, Q, ns, layer, s_in, s_out, final_ln):
    nc = ns["nc"]
    ar = ns["arena"]
    ar.reset()
    ar.cur = 0
    w1p = ar.bf16(8 * DFF)
    w2p = ar.bf16(22 * 1024)
    rowsb = ar.bf16(HID)
    gbc = ar.bf16(1024)
    bbc = ar.bf16(1024)
    ytm = [ar.f32(1024) for _ in range(2)]
    ybf = [ar.bf16(1024) for _ in range(3)]
    x1tm = [ar.f32(1024) for _ in range(2)] if final_ln else None
    h2n = [ar.bf16(1024) for _ in range(2)]
    h2nfm = [ar.bf16(1024) for _ in range(2)]
    h4tm = [ar.bf16(2816) for _ in range(2)]
    h4fm = ar.bf16(2816)
    gl = [ar.bf16(512) for _ in range(2)]

    psA = ns["psA"]
    psAb = psA.ap().bitcast(BF16)

    def bank(j):
        return psA.ap()[:, j * 512:(j + 1) * 512]

    def bankb(j):
        return psAb[:, j * 1024:(j + 1) * 1024]

    identb = ns["identb"]
    shmatb = ns["shmatb"]
    onesb = ns["onesb"]
    stats = ns["stats"]
    scb = ns["bscr"]

    NT = B2 * T // P

    for kk in range(8):
        G.dma(w1p[:, kk * DFF:(kk + 1) * DFF], ns["w1"][layer, kk * 128:(kk + 1) * 128, :], "lbw")
    for kk in range(21):
        G.dma(w2p[:, kk * 1024:(kk + 1) * 1024], ns["w2"][layer, kk * 128:(kk + 1) * 128, :], "lbw")
    G.dma(w2p[:, 21 * 1024:22 * 1024][0:42, :], ns["w2"][layer, 21 * 128:2730, :], "lbw")
    G.dma(rowsb[0:1, 0:HID], ns["b1"][layer, 0:HID].unsqueeze(0), "lbw")
    G.dma(rowsb[32:33, 0:HID], ns["b1"][layer, HID:DFF].unsqueeze(0), "lbw")
    G.dma(rowsb[64:65, 0:1024], ns["b2"][layer].unsqueeze(0), "lbw")
    G.dma(ns["scrap1k"][0:1, 0:1024], ns["ln_g"][layer].unsqueeze(0), "lbw")
    G.dma(ns["scrap1k"][32:33, 0:1024], ns["ln_b"][layer].unsqueeze(0), "lbw")
    lbw_now = ctx.val("lbw")
    TT.wait_now("lbw")
    TT.matmul(bank(0)[:, :], onesb[0:1, 0:128], ns["scrap1k"][0:1, 0:512], start=True, stop=True)
    TT.matmul(bank(1)[:, :], onesb[0:1, 0:128], ns["scrap1k"][0:1, 512:1024], start=True, stop=True)
    TT.matmul(bank(2)[:, :], onesb[32:33, 0:128], ns["scrap1k"][32:33, 0:512], start=True, stop=True)
    TT.matmul(bank(3)[:, :], onesb[32:33, 0:128], ns["scrap1k"][32:33, 512:1024], start=True, stop=True)
    TT.inc("pq", 1)
    V.waitp(ctx.val("pq"))
    V.tensor_copy(gbc[:, 0:512], bank(0)[:, :])
    V.tensor_copy(gbc[:, 512:1024], bank(1)[:, :])
    V.tensor_copy(bbc[:, 0:512], bank(2)[:, :])
    V.tensor_copy(bbc[:, 512:1024], bank(3)[:, :])
    V.memset(h4tm[0][:, 2730:2816], 0.0)
    V.memset(h4tm[1][:, 2730:2816], 0.0)
    cw_ready = V.ms("vx")

    pool_ytm = Rot(2)
    pool_ybf = Rot(3)
    pool_x1 = Rot(2)
    pool_h2n = Rot(2)
    pool_h2nfm = Rot(2)
    pool_h4 = Rot(2)
    pool_gl = Rot(2)
    pool01 = Rot(2, seed=cw_ready)
    B01 = [0, 1]
    pool_pair = Rot(2, seed=cw_ready)
    pool_st = Rot(2)
    lysem = ["ly0", "ly1", "ly2"]
    lbsem = ["lb0", "lb1", "lb2"]
    lxsem = ["lx0", "lx1"]

    def load_tile(i):
        sl, wp = pool_ytm.acquire()
        Q.waitp(wp)
        b, t0 = divmod(i * P, T)
        lp = Q.dma(ytm[sl][:, :], s_in[b, t0:t0 + P, :], lysem[sl])
        sl2, wp2 = pool_ybf.acquire()
        G.waitp(wp2)
        lp2 = G.dma(ybf[sl2][:, :], s_in[b, t0:t0 + P, :], lbsem[sl2])
        ret = [sl, lp, sl2, lp2, None, None]
        if final_ln:
            sl3, wp3 = pool_x1.acquire()
            Q.waitp(wp3)
            lp3 = Q.dma(x1tm[sl3][:, :], ns["s_x1"][b, t0:t0 + P, :], lxsem[sl3])
            ret[4], ret[5] = sl3, lp3
        return ret

    def pre_tile(i, ld):
        b, t0 = divmod(i * P, T)
        sl, lp, sl2, lp2, sl3, lp3 = ld
        sbase = 32 + (i % 2) * 16
        stw = pool_st.acquire()[1]
        sbk = bank(6 + i % 2)
        TT.waitp(lp2)
        TT.waitp(gst["shift_free"].pop(i - 2, None))
        if i == 0:
            TT.waitp(cw_ready)
        first = (t0 == 0)
        TT.matmul(sbk[:, :], shmatb[:, 0:128], ybf[sl2][:, 512:1024],
                  start=True, stop=first)
        if not first:
            TT.waitp(gst["prev_ybf"][0])
            TT.matmul(sbk[:, :], shmatb[:, 128:256],
                      ybf[gst["prev_ybf"][1]][:, 512:1024], start=False, stop=True)
        TT.inc("pm", 1)
        sh_pm = ctx.val("pm")
        if i >= 1:
            pool_ybf.release(sh_pm)
        gst["prev_ybf"] = (lp2, sl2, sh_pm)
        S.waitp(sh_pm)
        S.waitp(stw)
        if i == 0:
            S.waitp(cw_ready)
        S.activation(ns["scrap1k"][:, 0:512], ybf[sl2][:, 0:512], AF.Square,
                     accum_out=stats[:, sbase + 2: sbase + 3])
        S.activation(ns["scrap1k"][:, 512:1024], sbk[:, :], AF.Square,
                     accum_out=stats[:, sbase + 3: sbase + 4])
        sq_ms = S.ms("sq")
        V.waitp(sq_ms)
        V.reduce_sum(stats[:, sbase + 0: sbase + 1], ybf[sl2][:, 0:512], AX.X)
        V.reduce_sum(stats[:, sbase + 1: sbase + 2], sbk[:, :], AX.X)
        V.selfsync("vx")
        V.tensor_tensor(stats[:, sbase + 4: sbase + 5], stats[:, sbase + 0: sbase + 1],
                        stats[:, sbase + 1: sbase + 2], ALU.add)
        V.tensor_tensor(stats[:, sbase + 5: sbase + 6], stats[:, sbase + 2: sbase + 3],
                        stats[:, sbase + 3: sbase + 4], ALU.add)
        V.selfsync("vx")
        V.tensor_scalar(stats[:, sbase + 6: sbase + 7], stats[:, sbase + 4: sbase + 5],
                        1.0 / D, None, ALU.mult)
        V.tensor_scalar(stats[:, sbase + 7: sbase + 8], stats[:, sbase + 5: sbase + 6],
                        1.0 / D, None, ALU.mult)
        V.selfsync("vx")
        V.tensor_tensor(stats[:, sbase + 8: sbase + 9], stats[:, sbase + 6: sbase + 7],
                        stats[:, sbase + 6: sbase + 7], ALU.mult)
        V.selfsync("vx")
        V.tensor_tensor(stats[:, sbase + 9: sbase + 10], stats[:, sbase + 7: sbase + 8],
                        stats[:, sbase + 8: sbase + 9], ALU.subtract)
        vstat = V.ms("vx")
        S.waitp(vstat)
        S.activation(stats[:, sbase + 10: sbase + 11], stats[:, sbase + 9: sbase + 10],
                     AF.Ln, bias=ns["epsc"][:, 0:1])
        S.selfsync("sq")
        S.activation(stats[:, sbase + 11: sbase + 12], stats[:, sbase + 10: sbase + 11],
                     AF.Exp, scale=-0.5)
        rstd_ms = S.ms("sq")
        h2sl, h2w = pool_h2n.acquire()
        V.waitp(rstd_ms)
        V.waitp(h2w)
        V.tensor_scalar(h2n[h2sl][:, 0:512], ybf[sl2][:, 0:512],
                        stats[:, sbase + 6: sbase + 7], stats[:, sbase + 11: sbase + 12],
                        ALU.subtract, ALU.mult)
        V.tensor_scalar(h2n[h2sl][:, 512:1024], sbk[:, :],
                        stats[:, sbase + 6: sbase + 7], stats[:, sbase + 11: sbase + 12],
                        ALU.subtract, ALU.mult)
        apl_ms = V.msd("vm")
        gst["shift_free"][i] = apl_ms
        G.waitp(apl_ms)
        G.tensor_tensor(h2n[h2sl][:, :], h2n[h2sl][:, :], gbc[:, :], ALU.mult)
        G.selfsync("gp")
        G.tensor_tensor(h2n[h2sl][:, :], h2n[h2sl][:, :], bbc[:, :], ALU.add)
        gdone = G.ms("gp")
        gst["pre"][i] = (sl, lp, sl3, lp3, sbase, h2sl, gdone)

    def preT_tile(i):
        _, _, _, _, _, h2sl, gdone = gst["pre"][i]
        fmsl, fmw = pool_h2nfm.acquire()
        jt, wt = pool01.acquire()
        TT.waitp(gdone)
        TT.waitp(wt)
        TT.waitp(fmw)
        for kk in range(8):
            TT.transpose(bankb(B01[jt])[:, kk * 128:(kk + 1) * 128],
                         h2n[h2sl][:, kk * 128:(kk + 1) * 128], identb[:, :])
        TT.inc("pt", 1)
        pool_h2n.release(ctx.val("pt"))
        V.waitp(ctx.val("pt"))
        V.tensor_copy(h2nfm[fmsl][:, :], bankb(B01[jt])[:, :])
        h2nfm_ms = V.msd("vt")
        pool01.release(h2nfm_ms)
        gst["fm"][i] = (fmsl, h2nfm_ms)

    def gemm_tile(i):
        b, t0 = divmod(i * P, T)
        sl, lp, sl3, lp3, sbase, h2sl, gdone = gst["pre"].pop(i)
        fmsl, h2nfm_ms = gst["fm"].pop(i)
        h4sl, h4w = pool_h4.acquire()
        TT.waitp(h2nfm_ms)
        segs = [(512 * jj, min(512, HID - 512 * jj)) for jj in range(6)]
        for si, (c0, ln) in enumerate(segs):
            pj, pw = pool_pair.acquire()
            TT.waitp(pw)
            base = 2 + 2 * pj
            for half, cc in ((0, c0), (1, HID + c0)):
                for kk in range(8):
                    TT.matmul(bank(base + half)[:, 0:ln],
                              h2nfm[fmsl][:, kk * 128:(kk + 1) * 128],
                              w1p[:, kk * DFF + cc: kk * DFF + cc + ln],
                              start=(kk == 0), stop=False)
                hp = 0 if half == 0 else 32
                TT.matmul(bank(base + half)[:, 0:ln], onesb[hp:hp + 1, 0:128],
                          rowsb[hp:hp + 1, c0: c0 + ln], start=False, stop=True)
            TT.inc("pm", 1)
            w1_pm = ctx.val("pm")
            glsl, glw = pool_gl.acquire()
            S.waitp(w1_pm)
            S.waitp(glw)
            S.activation(gl[glsl][:, 0:ln], bank(base + 1)[:, 0:ln], AF.Gelu)
            gelu_ms = S.ms("sg")
            V.waitp(gelu_ms)
            if si == 0:
                V.waitp(h4w)
            V.tensor_tensor(h4tm[h4sl][:, c0:c0 + ln], bank(base)[:, 0:ln],
                            gl[glsl][:, 0:ln], ALU.mult)
            vg = V.msd("vm")
            pool_gl.release(vg)
            pool_pair.release(vg)
        pool_h2nfm.release(ctx.val("pm"))
        h4_ms = ctx.val("vm")
        pj, pw = pool_pair.acquire()
        base = 2 + 2 * pj
        TT.waitp(pw)
        h4fm_parts = []
        for grp in range(3):
            jt2, wt2 = pool01.acquire()
            TT.waitp(wt2)
            if grp == 0:
                TT.waitp(h4_ms)
            n_in = min(8, 22 - grp * 8)
            for u in range(n_in):
                TT.transpose(bankb(B01[jt2])[:, u * 128:(u + 1) * 128],
                             h4tm[h4sl][:, (grp * 8 + u) * 128:(grp * 8 + u + 1) * 128],
                             identb[:, :])
            TT.inc("pt", 1)
            V.waitp(ctx.val("pt"))
            if grp == 0:
                V.waitp(gst["prev_w2"])
            V.tensor_copy(h4fm[:, grp * 1024: grp * 1024 + n_in * 128],
                          bankb(B01[jt2])[:, 0:n_in * 128])
            h4fm_parts.append(V.msd("vt"))
            pool01.release(h4fm_parts[-1])
        pool_h4.release(ctx.val("pt"))
        TT.waitp(h4fm_parts[-1])
        for half in range(2):
            for kk in range(22):
                kcount = 128 if kk < 21 else 42
                TT.matmul(bank(base + half)[:, :],
                          h4fm[0:kcount, kk * 128:(kk + 1) * 128],
                          w2p[0:kcount, kk * 1024 + half * 512: kk * 1024 + half * 512 + 512],
                          start=(kk == 0), stop=False)
            TT.matmul(bank(base + half)[:, :], onesb[64:65, 0:128],
                      rowsb[64:65, half * 512:(half + 1) * 512], start=False, stop=True)
        TT.inc("pm", 1)
        gst["prev_w2"] = ctx.val("pm")
        w2_pm = ctx.val("pm")
        V.waitp(w2_pm)
        V.waitp(lp)
        for half in range(2):
            V.scalar_tensor_tensor(ytm[sl][:, half * 512:(half + 1) * 512],
                                   bank(base + half)[:, :], 1.0,
                                   ytm[sl][:, half * 512:(half + 1) * 512],
                                   ALU.mult, ALU.add)
        pool_pair.release(V.msd("vm"))
        if final_ln:
            V.waitp(lp3)
            V.selfsync("vm")
            V.scalar_tensor_tensor(ytm[sl][:, :], x1tm[sl3][:, :], 1.0, ytm[sl][:, :],
                                   ALU.mult, ALU.add,
                                   accum_out=stats[:, sbase + 0: sbase + 1])
            fv_ms = V.ms("vm")
            pool_x1.release(fv_ms)
            S.waitp(fv_ms)
            S.activation(ns["scrap1k"][:, :], ytm[sl][:, :], AF.Square,
                         accum_out=stats[:, sbase + 1: sbase + 2])
            fsq = S.ms("sq")
            V.waitp(fsq)
            V.tensor_scalar(stats[:, sbase + 6: sbase + 7], stats[:, sbase + 0: sbase + 1],
                            1.0 / D, None, ALU.mult)
            V.tensor_scalar(stats[:, sbase + 7: sbase + 8], stats[:, sbase + 1: sbase + 2],
                            1.0 / D, None, ALU.mult)
            V.selfsync("vx")
            V.tensor_tensor(stats[:, sbase + 8: sbase + 9], stats[:, sbase + 6: sbase + 7],
                            stats[:, sbase + 6: sbase + 7], ALU.mult)
            V.selfsync("vx")
            V.tensor_tensor(stats[:, sbase + 9: sbase + 10], stats[:, sbase + 7: sbase + 8],
                            stats[:, sbase + 8: sbase + 9], ALU.subtract)
            fstat = V.ms("vx")
            S.waitp(fstat)
            S.activation(stats[:, sbase + 10: sbase + 11], stats[:, sbase + 9: sbase + 10],
                         AF.Ln, bias=ns["epsc"][:, 0:1])
            S.selfsync("sq")
            S.activation(stats[:, sbase + 11: sbase + 12], stats[:, sbase + 10: sbase + 11],
                         AF.Exp, scale=-0.5)
            frstd = S.ms("sq")
            V.waitp(frstd)
            V.tensor_scalar(ytm[sl][:, :], ytm[sl][:, :],
                            stats[:, sbase + 6: sbase + 7], stats[:, sbase + 11: sbase + 12],
                            ALU.subtract, ALU.mult)
        stile_ms = V.ms("vm")
        pool_st.release(stile_ms)
        Q.waitp(stile_ms)
        so = "so0" if i % 2 == 0 else "so1"
        dst = ns["out"] if final_ln else s_out
        sdp = Q.dma(dst[b, t0:t0 + P, :], ytm[sl][:, :], so)
        if DEBUG_STAGE == f"C{layer}" and not final_ln:
            sdp = Q.dma(ns["dbg_tap"][b, t0:t0 + P, :], ytm[sl][:, :], so)
        pool_ytm.release(sdp)
        ns["final_waits"].append(sdp)

    gst = {"prev_w2": None, "prev_ybf": (None, None, None),
           "shift_free": {}, "pre": {}, "fm": {}}
    pre_tile(0, load_tile(0))
    pre_tile(1, load_tile(1))
    preT_tile(0)
    for i in range(NT):
        gemm_tile(i)
        if i + 1 < NT:
            preT_tile(i + 1)
        if i + 2 < NT:
            pre_tile(i + 2, load_tile(i + 2))


def make_inputs_maps(inputs):
    hc = host_consts()
    maps = []
    x = np.ascontiguousarray(inputs["L0_attn_x"])
    full = dict(
        log_lambda=inputs["L0_attn_log_lambda"], b_w=inputs["L0_attn_b_w"],
        b_b=inputs["L0_attn_b_b"], c_w=inputs["L0_attn_c_w"], c_b=inputs["L0_attn_c_b"],
        wqkv=inputs["L0_ffn_wqkv"], q_scale=inputs["L0_ffn_q_scale"],
        k_scale=inputs["L0_ffn_k_scale"], wo=inputs["L0_ffn_wo"],
        ln_g=inputs["L0_ffn_ln_g"], ln_b=inputs["L0_ffn_ln_b"],
        w1=inputs["L0_ffn_w1"], b1=inputs["L0_ffn_b1"],
        w2=inputs["L0_ffn_w2"], b2=inputs["L0_ffn_b2"],
        **hc,
    )
    full = {k: np.ascontiguousarray(np.asarray(v, dtype=np.float32)) for k, v in full.items()}
    for c in range(8):
        m = dict(full)
        m["x0"] = x[2 * c:2 * c + 2]
        maps.append(m)
    return maps


def kernel(**inputs):
    nc = build_program()
    maps = make_inputs_maps(inputs)
    res = run_bass_kernel_spmd(nc, maps, core_ids=list(range(8)))
    return np.concatenate([r["out"] for r in res.results], axis=0)


if __name__ == "__main__":
    print("module ok")


# version 19
# speedup vs baseline: 1.0184x; 1.0170x over previous
import os
import numpy as np
import ml_dtypes
import concourse.bass as bass
import concourse.mybir as mybir
from concourse.bass_utils import run_bass_kernel_spmd
from contextlib import ExitStack

F32 = mybir.dt.float32
BF16 = mybir.dt.bfloat16
AF = mybir.ActivationFunctionType
ALU = mybir.AluOpType
AX = mybir.AxisListType
BF = ml_dtypes.bfloat16

P = 128
D = 1024
T = 4096
B2 = 2
W = 512
NW = T // W
HEADS = 8
DH = 64
INNER = 512
HID = 2730
DFF = 5460
QK_SCALE = 8.0
EPS = 1e-5

DEBUG_STAGE = os.environ.get("MK_DEBUG_STAGE", "full")


def host_consts():
    c = {}
    c["ident"] = np.eye(128, dtype=np.float32)
    half = np.arange(0, DH, 2, dtype=np.float64)
    inv_freq = 1.0 / 10000.0 ** (half / DH)
    base = (half + 0.4 * DH) / (1.4 * DH)
    sgn = np.concatenate([-np.ones(32), np.ones(32)])[None, :]

    def rot_tables(idx):
        freqs = idx[:, None] * inv_freq[None, :]
        power = idx / float(W // 2)
        scale = base[None, :] ** power[:, None]
        cos2 = np.concatenate([np.cos(freqs)] * 2, axis=1)
        sin2 = np.concatenate([np.sin(freqs)] * 2, axis=1)
        sc2 = np.concatenate([scale] * 2, axis=1)
        return cos2, sin2, sc2

    qi = np.arange(W, dtype=np.float64)
    cq, sq_, scq = rot_tables(qi)
    c["tqc"] = (QK_SCALE * cq * scq).astype(np.float32)
    c["tqs"] = (QK_SCALE * sq_ * scq * sgn).astype(np.float32)
    ki = np.arange(2 * W, dtype=np.float64) - W
    ck, sk, sck = rot_tables(ki)
    c["tkc"] = (ck / sck).astype(np.float32)
    c["tks"] = (sk / sck * sgn).astype(np.float32)
    i = np.arange(128)[:, None]
    j = np.arange(128)[None, :]
    c["mask_diag"] = np.where(j <= i, 0.0, -1e30).astype(np.float32)
    c["mask_trail"] = np.where(j >= i, 0.0, -1e30).astype(np.float32)
    sh = np.zeros((128, 128), dtype=np.float32)
    sh[np.arange(127), np.arange(1, 128)] = 1.0
    c["shiftT"] = sh
    b127 = np.zeros((128, 128), dtype=np.float32)
    b127[127, 0] = 1.0
    c["b127"] = b127
    return c


class Eng:
    def __init__(self, ctx, key):
        self.ctx = ctx
        self.key = key
        self.last = None

    @property
    def active(self):
        return self.ctx.active == self.key

    def __getattr__(self, name):
        def call(*args, **kw):
            if self.active:
                self.last = getattr(self.ctx.handles[self.key], name)(*args, **kw)
                self.last_has_inc = False
                return self.last
            return None
        return call

    def wait(self, sem, val):
        if val is None or val <= 0:
            return
        if self.active:
            self.ctx.handles[self.key].wait_ge(self.ctx.sems[sem], val)

    def waitp(self, pair):
        if pair is None:
            return
        if isinstance(pair, list):
            for p in pair:
                self.waitp(p)
            return
        sem, val = pair
        self.wait(sem, val)

    def inc(self, sem, n=1):
        self.ctx.counts[sem] += n
        if self.active:
            assert self.last is not None, f"inc({sem}) with no prior op on {self.key}"
            if getattr(self, "last_has_inc", False):
                self.last = self.ctx.handles[self.key].nop()
            self.last.then_inc(self.ctx.sems[sem], n)
            self.last_has_inc = True

    def ms(self, sem):
        self.inc(sem, 1)
        if self.active:
            h = self.ctx.handles[self.key]
            h.nop().then_inc(self.ctx.sems[sem], 1)
        self.ctx.counts[sem] += 1
        return (sem, self.ctx.counts[sem])

    def msd(self, sem):
        self.inc(sem, 1)
        if self.active:
            h = self.ctx.handles[self.key]
            h.drain()
            h.nop().then_inc(self.ctx.sems[sem], 1)
        self.ctx.counts[sem] += 1
        return (sem, self.ctx.counts[sem])

    def selfsync(self, sem):
        self.inc(sem, 1)
        if self.active:
            self.ctx.handles[self.key].wait_ge(self.ctx.sems[sem], self.ctx.counts[sem])

    def wait_now(self, sem):
        self.wait(sem, self.ctx.counts[sem])

    def dma(self, out, in_, sem, n=16, **kw):
        self.ctx.counts[sem] += n
        if self.active:
            self.last = self.ctx.handles[self.key].dma_start(out=out, in_=in_, **kw)
            self.last.then_inc(self.ctx.sems[sem], n)
            self.last_has_inc = True
        return (sem, self.ctx.counts[sem])


class Ctx:
    def __init__(self, sem_names):
        self.sem_names = sem_names
        self.sems = None
        self.counts = {k: 0 for k in sem_names}
        self.active = None
        self.handles = {}

    def val(self, sem):
        return (sem, self.counts[sem])


class Rot:

    def __init__(self, n, seed=None):
        self.n = n
        self.hist = [seed] * n if seed is not None else []
        self.i = n if seed is not None else 0
        self.presends = self.i

    def acquire(self):
        idx = self.i % self.n
        if self.i >= self.n:
            assert len(self.hist) > self.i - self.n, (
                "resource acquired before its prior user's release was recorded")
            waitp = self.hist[self.i - self.n]
        else:
            waitp = None
        self.i += 1
        return idx, waitp

    def release(self, pair):
        self.hist.append(pair)
        assert len(self.hist) <= self.i, "release() without matching acquire()"


class Arena:

    def __init__(self, handle, total_f32cols, reserved_f32cols):
        self.h = handle
        self.total = total_f32cols
        self.base = reserved_f32cols
        self.cur = reserved_f32cols

    def reset(self):
        self.cur = self.base

    def f32(self, cols):
        o = self.cur
        self.cur += cols
        assert self.cur <= self.total, f"arena overflow: {self.cur} > {self.total}"
        return self.h[:, o:o + cols]

    def bf16(self, cols):
        f32c = (cols + 1) // 2
        o = self.cur
        self.cur += f32c
        assert self.cur <= self.total, f"arena overflow: {self.cur} > {self.total}"
        return self.h[:, o:o + f32c].bitcast(BF16)[:, 0:cols]

    def reserved_f32(self, off, cols):
        assert off + cols <= self.base
        return self.h[:, off:off + cols]

    def reserved_bf16(self, off, cols):
        assert off + (cols + 1) // 2 <= self.base
        return self.h[:, off:off + (cols + 1) // 2].bitcast(BF16)[:, 0:cols]


def build_program():
    nc = bass.Bass("TRN2", target_bir_lowering=False)

    ns = {}
    ns["nc"] = nc

    def din(name, shape):
        ns[name] = nc.dram_tensor(name, shape, F32, kind="ExternalInput").ap()

    din("x0", [B2, T, D])
    din("log_lambda", [D]); din("b_w", [D, D]); din("b_b", [D]); din("c_w", [D, D]); din("c_b", [D])
    din("wqkv", [2, D, 3 * INNER]); din("q_scale", [2, DH]); din("k_scale", [2, DH])
    din("wo", [2, INNER, D]); din("ln_g", [2, D]); din("ln_b", [2, D])
    din("w1", [2, D, DFF]); din("b1", [2, DFF]); din("w2", [2, HID, D]); din("b2", [2, D])
    din("ident", [128, 128])
    din("tqc", [W, DH]); din("tqs", [W, DH]); din("tkc", [2 * W, DH]); din("tks", [2 * W, DH])
    din("mask_diag", [128, 128]); din("mask_trail", [128, 128])
    din("shiftT", [128, 128]); din("b127", [128, 128])

    ns["out"] = nc.dram_tensor("out", [B2, T, D], F32, kind="ExternalOutput").ap()
    if DEBUG_STAGE != "full":
        ns["dbg_tap"] = nc.dram_tensor("dbg_tap", [B2, T, D], F32, kind="ExternalOutput").ap()
    ns["s_x1"] = nc.dram_tensor("s_x1", [B2, T, D], F32, kind="Internal").ap()
    ns["s_ya"] = nc.dram_tensor("s_ya", [B2, T, D], F32, kind="Internal").ap()
    ns["s_yb"] = nc.dram_tensor("s_yb", [B2, T, D], F32, kind="Internal").ap()

    sem_names = ["lw", "lwg", "li", "lib", "lwc", "lbw", "lqs", "ly0", "ly1", "ly2", "lb0", "lb1", "lb2",
                 "lx0", "lx1", "le0", "le1", "so0", "so1", "pb",
                 "pt", "pm", "pq", "vt", "vm", "vx", "sq", "sg", "gp"]
    ctx = Ctx(sem_names)
    TT, V, S, G, Q = (Eng(ctx, k) for k in ["t", "v", "s", "g", "q"])

    with ExitStack() as big:
        sems = {n: big.enter_context(nc.semaphore(n)) for n in sem_names}
        ctx.sems = sems

        sb = lambda nm, shp, dt: big.enter_context(nc.sbuf_tensor(nm, shp, dt))
        ns["identf"] = sb("identf", [P, 128], F32)
        ns["identb"] = sb("identb", [P, 128], BF16)
        ns["maskstack"] = sb("maskstack", [P, 256], F32)
        ns["shmatb"] = sb("shmatb", [P, 256], BF16)
        ns["lamf"] = sb("lamf", [P, 8], F32)
        ns["omlf"] = sb("omlf", [P, 8], F32)
        ns["bbf"] = sb("bbf", [P, 8], F32)
        ns["cbf"] = sb("cbf", [P, 8], F32)
        ns["onesf"] = sb("onesf", [P, 512], F32)
        ns["stats"] = sb("stats", [P, 128], F32)
        ns["epsc"] = sb("epsc", [P, 1], F32)
        ns["bscr"] = sb("bscr", [P, 8], F32)
        ns["scrap1k"] = sb("scrap1k", [P, 1024], BF16)
        ns["onesb"] = sb("onesb", [P, 512], BF16)
        ns["rowbuf"] = sb("rowbuf", [P, 2048], F32)
        ns["cbrow"] = sb("cbrow", [P, 1024], BF16)
        ns["bbrow"] = sb("bbrow", [P, 1024], BF16)
        ARENA_COLS = 48000
        RESV = 8192
        arena_h = sb("arena", [P, ARENA_COLS], F32)
        ns["arena"] = Arena(arena_h, ARENA_COLS, RESV)
        ns["wqkvb"] = ns["arena"].reserved_bf16(0, 8 * 1536)
        ns["wob"] = ns["arena"].reserved_bf16(6144, 4 * D)
        ns["psA"] = big.enter_context(nc.psum_tensor("psA", [P, 4096], F32))
        ns["big"] = big

        def emit(active, handle):
            ctx.active = active
            ctx.handles = {active: handle}
            for k in sem_names:
                ctx.counts[k] = 0
            program(ctx, TT, V, S, G, Q, ns)

        with nc.Block() as block:
            @block.tensor
            def _(tensor):
                emit("t", tensor)

            @block.vector
            def _(vector):
                emit("v", vector)

            @block.scalar
            def _(scalar):
                emit("s", scalar)

            @block.gpsimd
            def _(gpsimd):
                emit("g", gpsimd)

            @block.sync
            def _(sync):
                emit("q", sync)

    return nc


def phase_barrier(ctx, TT, V, S, G, Q, ns):
    bscr = ns["bscr"]
    Q.wait_now("so0")
    Q.wait_now("so1")
    V.tensor_copy(bscr[:, 0:1], bscr[:, 1:2])
    V.inc("pb", 1)
    S.copy(bscr[:, 2:3], bscr[:, 3:4])
    S.inc("pb", 1)
    G.tensor_copy(bscr[:, 4:5], bscr[:, 5:6])
    G.inc("pb", 1)
    Q.nop()
    Q.inc("pb", 1)
    TT.wait("pb", ctx.counts["pb"])
    TT.transpose(ns["psA"].ap()[:, 0:128], ns["identf"][:, :], ns["identf"][:, :])
    TT.inc("pb", 1)
    target = ctx.counts["pb"]
    for E in (V, S, G, Q):
        E.wait("pb", target)


def program(ctx, TT, V, S, G, Q, ns):
    phase_setup(ctx, TT, V, S, G, Q, ns)
    phase_A(ctx, TT, V, S, G, Q, ns)
    if DEBUG_STAGE != "A":
        phase_barrier(ctx, TT, V, S, G, Q, ns)
        phase_B(ctx, TT, V, S, G, Q, ns, 0, ns["s_x1"], ns["s_ya"])
    if DEBUG_STAGE not in ("A", "B0"):
        phase_barrier(ctx, TT, V, S, G, Q, ns)
        phase_C(ctx, TT, V, S, G, Q, ns, 0, ns["s_ya"], ns["s_yb"],
                final_ln=False)
    if DEBUG_STAGE == "full" or DEBUG_STAGE in ("B1", "C1"):
        phase_barrier(ctx, TT, V, S, G, Q, ns)
        phase_B(ctx, TT, V, S, G, Q, ns, 1, ns["s_yb"], ns["s_ya"])
        phase_barrier(ctx, TT, V, S, G, Q, ns)
        phase_C(ctx, TT, V, S, G, Q, ns, 1, ns["s_ya"], None,
                final_ln=True)
    fin = {}
    for sem, val in ns.get("final_waits", []):
        fin[sem] = max(fin.get(sem, 0), val)
    for sem, val in sorted(fin.items()):
        G.wait(sem, val)


def phase_setup(ctx, TT, V, S, G, Q, ns):
    ns["final_waits"] = []
    Q.dma(ns["identf"][:, :], ns["ident"][:, :], "lw")
    Q.dma(ns["maskstack"][:, 0:128], ns["mask_trail"][:, :], "lw")
    Q.dma(ns["maskstack"][:, 128:256], ns["mask_diag"][:, :], "lw")
    Q.dma(ns["lamf"][:, :], ns["log_lambda"].rearrange("(c p) -> p c", p=P), "lw", allow_slow_non_contiguous=True)
    Q.dma(ns["bbf"][:, :], ns["b_b"].rearrange("(c p) -> p c", p=P), "lw", allow_slow_non_contiguous=True)
    Q.dma(ns["cbf"][:, :], ns["c_b"].rearrange("(c p) -> p c", p=P), "lw", allow_slow_non_contiguous=True)
    G.dma(ns["identb"][:, :], ns["ident"][:, :], "li")
    G.dma(ns["shmatb"][:, 0:128], ns["shiftT"][:, :], "li")
    G.dma(ns["shmatb"][:, 128:256], ns["b127"][:, :], "li")
    G.dma(ns["cbrow"][0:1, 0:1024], ns["c_b"].unsqueeze(0), "li")
    G.dma(ns["rowbuf"][0:1, 0:1024], ns["log_lambda"].unsqueeze(0), "li")
    lw0 = [ctx.val("lw"), ctx.val("li")]
    ns["setup_lw"] = lw0

    stats = ns["stats"]
    S.wait_now("lw"); S.wait_now("li")
    S.activation(stats[:, 0:8], ns["lamf"][:, :], AF.Exp)
    S.selfsync("sq")
    S.activation(stats[:, 8:16], stats[:, 0:8], AF.Exp, scale=-1.0)
    sq0 = S.ms("sq")
    V.waitp(sq0)
    V.wait_now("lw"); V.wait_now("li")
    V.tensor_scalar(stats[:, 16:24], stats[:, 8:16], 1.0, None, ALU.add)
    V.selfsync("vx")
    V.reciprocal(ns["lamf"][:, :], stats[:, 16:24])
    V.selfsync("vx")
    V.tensor_scalar(ns["omlf"][:, :], ns["lamf"][:, :], -1.0, 1.0, ALU.mult, ALU.add)
    V.memset(ns["onesf"][:, :], 1.0)
    V.memset(ns["epsc"][:, :], EPS)
    V.memset(ns["bscr"][:, :], 0.0)
    V.memset(ns["onesb"][:, :], 1.0)
    ns["setup_vx"] = V.ms("vx")
    rb = ns["rowbuf"]
    S.activation(rb[0:1, 1024:2048], rb[0:1, 0:1024], AF.Exp)
    S.selfsync("sq")
    S.activation(rb[0:1, 0:1024], rb[0:1, 1024:2048], AF.Exp, scale=-1.0)
    srow = S.ms("sq")
    V.waitp(srow)
    V.tensor_scalar(rb[0:1, 1024:2048], rb[0:1, 0:1024], 1.0, None, ALU.add)
    V.selfsync("vx")
    V.reciprocal(rb[0:1, 0:1024], rb[0:1, 1024:2048])
    V.selfsync("vx")
    V.tensor_scalar(rb[0:1, 1024:2048], rb[0:1, 0:1024], -1.0, 1.0, ALU.mult, ALU.add)
    vrow = V.ms("vx")
    G.waitp(vrow)
    G.dma(ns["bbrow"][0:1, 0:1024], ns["b_b"].unsqueeze(0), "lib")
    V.wait_now("lib")
    V.tensor_tensor(ns["bbrow"][0:1, 0:1024], ns["bbrow"][0:1, 0:1024],
                    rb[0:1, 1024:2048], ALU.mult)
    ns["rows_ready"] = V.ms("vx")


def phase_A(ctx, TT, V, S, G, Q, ns):
    nc = ns["nc"]
    if True:
        ar = ns["arena"]
        ar.reset()
        bwt = ar.bf16(64 * 128)
        cwt = ar.bf16(64 * 128)
        w_tm = ar.bf16(1024)
        lam_bc = ar.f32(8 * 512)
        x0tm = [ar.f32(4 * 1024) for j in range(3)]
        x0fm = [ar.bf16(8 * 512) for j in range(2)]
        hfm = ar.f32(8 * 512)
        carry = ar.f32(8)
        hbf = [ar.bf16(8 * 512) for j in range(2)]
        scrapb = ar.bf16(1024)

        psA = ns["psA"]
        psAb = psA.ap().bitcast(BF16)
        pstb = [psAb[:, 0:1024], psAb[:, 1024:2048]]
        pstf = [psA.ap()[:, 1024:1536], psA.ap()[:, 1536:2048]]
        psxb = [psA.ap()[:, 2048:2560], psA.ap()[:, 2560:3072]]
        psat = [psA.ap()[:, 3072:3584], psA.ap()[:, 3584:4096]]

        x0_d, s_x1 = ns["x0"], ns["s_x1"]
        identf, identb = ns["identf"], ns["identb"]
        lamf, omlf = ns["lamf"], ns["omlf"]
        stats = ns["stats"]

        rows_ready = ns["rows_ready"]
        onesb, cbrow, bbrow = ns["onesb"], ns["cbrow"], ns["bbrow"]

        V.waitp(ns["setup_vx"])
        for ee in range(8):
            V.tensor_scalar_mul(lam_bc[:, ee * 512:(ee + 1) * 512],
                                ns["onesf"][:, :], lamf[:, ee:ee + 1])
        V.selfsync("vx")
        lam_ready = V.ms("vx")

        pool_wT = Rot(2)
        pool_wtm = Rot(1)
        wt_done = {}
        for wi, (w_d, dstt) in enumerate([(ns["b_w"], bwt), (ns["c_w"], cwt)]):
            for ee in range(8):
                _, wstg = pool_wtm.acquire()
                G.waitp(wstg)
                lwp = G.dma(w_tm[:, :], w_d[ee * 128:(ee + 1) * 128, :], "lwc")
                if wi == 0:
                    V.waitp(lwp)
                    V.waitp(ns["setup_vx"])
                    V.tensor_scalar_mul(w_tm[:, :], w_tm[:, :], omlf[:, ee:ee + 1])
                    wrd = V.ms("vt")
                    TT.waitp(wrd)
                else:
                    TT.waitp(lwp)
                TT.wait_now("li")
                for g in range(2):
                    j, wp = pool_wT.acquire()
                    TT.waitp(wp)
                    for u in range(4):
                        kk = g * 4 + u
                        TT.transpose(pstb[j][:, u * 256 if False else u * 128 + 0: u * 128 + 128]
                                     if False else pstb[j][:, u * 128: u * 128 + 128],
                                     w_tm[:, kk * 128: kk * 128 + 128],
                                     identb[:, :])
                    TT.inc("pt", 1)
                    V.waitp(ctx.val("pt"))
                    V.tensor_copy(dstt[:, :].rearrange("p (k e c) -> p k e c", e=8, c=128)
                                  [:, g * 4:(g + 1) * 4, ee, :],
                                  pstb[j][:, 0:512].rearrange("p (k c) -> p k c", c=128))
                    pool_wT.release(V.msd("vt"))
                pool_wtm.release(ctx.val("pt"))
            wt_done[wi] = ctx.val("vt")

        NCH = B2 * NW
        pool_tf = Rot(2)
        pool_xb = Rot(2)
        pool_at = Rot(2)
        pool_x0tm = Rot(3)
        pool_x0fm = Rot(2)
        pool_hbf = Rot(2)
        lx_sem = ["lx0", "lx1", "le0"]

        def load_x0(i):
            sl, wp = pool_x0tm.acquire()
            Q.waitp(wp)
            b, c = divmod(i, NW)
            lp = Q.dma(x0tm[sl][:, :].rearrange("p (tt d) -> p tt d", d=D),
                       x0_d[b, c * 512:(c + 1) * 512, :].rearrange("(tt p) d -> p tt d", p=P),
                       lx_sem[sl])
            return sl, lp

        def tgroups(i, x0sl, lxp):
            fmsl, fmw = pool_x0fm.acquire()
            TT.waitp(lxp)
            TT.waitp(fmw)
            for g in range(8):
                j, wp = pool_tf.acquire()
                TT.waitp(wp)
                for tt in range(4):
                    TT.transpose(pstf[j][:, tt * 128:(tt + 1) * 128],
                                 x0tm[x0sl][:, tt * 1024 + g * 128: tt * 1024 + g * 128 + 128],
                                 identf[:, :])
                TT.inc("pt", 1)
                V.waitp(ctx.val("pt"))
                V.tensor_copy(x0fm[fmsl][:, g * 512:(g + 1) * 512], pstf[j][:, :])
                pool_tf.release(V.msd("vt"))
            return fmsl, ctx.val("vt")

        def xb_chunk(i, fmsl, tfm_done, prev_cast):
            b, c = divmod(i, NW)
            hsl, hw = pool_hbf.acquire()
            TT.waitp(hw)
            TT.waitp(tfm_done)
            if i == 0:
                TT.waitp(wt_done[0])
                TT.waitp(rows_ready)
            for ee in range(8):
                j, wp = pool_xb.acquire()
                TT.waitp(wp)
                for kk in range(8):
                    TT.matmul(psxb[j][:, :],
                              bwt[:, (kk * 8 + ee) * 128:(kk * 8 + ee + 1) * 128],
                              x0fm[fmsl][:, kk * 512:(kk + 1) * 512],
                              start=(kk == 0), stop=False)
                TT.matmul(psxb[j][:, :], bbrow[0:1, ee * 128:(ee + 1) * 128],
                          onesb[0:1, 0:512], start=False, stop=True)
                TT.inc("pm", 1)
                if ee == 7:
                    pool_x0fm.release(ctx.val("pm"))
                V.waitp(ctx.val("pm"))
                if prev_cast is not None:
                    V.waitp(prev_cast)
                    prev_cast = None
                init = 0.0 if c == 0 else carry[:, ee:ee + 1]
                V.tensor_tensor_scan(hfm[:, ee * 512:(ee + 1) * 512],
                                     lam_bc[:, ee * 512:(ee + 1) * 512],
                                     psxb[j][:, :],
                                     init, ALU.mult, ALU.add)
                V.selfsync("vx")
                V.tensor_copy(carry[:, ee:ee + 1], hfm[:, ee * 512 + 511: ee * 512 + 512])
                V.selfsync("vx")
                pool_xb.release(V.msd("vm"))
                scp = ctx.val("vm")
                S.waitp(scp)
                S.copy(hbf[hsl][:, ee * 512:(ee + 1) * 512], hfm[:, ee * 512:(ee + 1) * 512])
                S.ms("sg")
            return hsl, ctx.val("sg")

        def cw_chunk(i, x0sl, hsl, hready):
            b, c = divmod(i, NW)
            TT.waitp(hready)
            if i == 0:
                TT.waitp(wt_done[1])
                TT.wait_now("li")
                TT.waitp(ones_vx)
            sbase = 32 + (i % 2) * 48
            for tt in range(4):
                for nt in range(2):
                    j, wp = pool_at.acquire()
                    TT.waitp(wp)
                    for ee in range(8):
                        TT.matmul(psat[j][:, :],
                                  hbf[hsl][:, ee * 512 + tt * 128: ee * 512 + tt * 128 + 128],
                                  cwt[:, (ee * 8 + nt * 4) * 128:(ee * 8 + nt * 4 + 4) * 128],
                                  start=(ee == 0), stop=False)
                    TT.matmul(psat[j][:, :], onesb[0:1, 0:128],
                              cbrow[0:1, nt * 512:(nt + 1) * 512],
                              start=False, stop=True)
                    TT.inc("pm", 1)
                    if tt == 3 and nt == 1:
                        pool_hbf.release(ctx.val("pm"))
                    V.waitp(ctx.val("pm"))
                    V.scalar_tensor_tensor(
                        x0tm[x0sl][:, tt * 1024 + nt * 512: tt * 1024 + nt * 512 + 512],
                        psat[j][:, :], 1.0,
                        x0tm[x0sl][:, tt * 1024 + nt * 512: tt * 1024 + nt * 512 + 512],
                        ALU.mult, ALU.add,
                        accum_out=stats[:, sbase + tt * 2 + nt: sbase + tt * 2 + nt + 1])
                    pool_at.release(V.msd("vm"))
                svp = ctx.val("vm")
                S.waitp(svp)
                S.activation(scrapb[:, :], x0tm[x0sl][:, tt * 1024:(tt + 1) * 1024], AF.Square,
                             accum_out=stats[:, sbase + 8 + tt: sbase + 8 + tt + 1])
                S.ms("sq")
            sqv = ctx.val("sq")
            V.waitp(sqv)
            V.tensor_tensor(stats[:, sbase + 12: sbase + 16],
                            stats[:, sbase: sbase + 8].rearrange("p (a b) -> p a b", b=2)[:, :, 0],
                            stats[:, sbase: sbase + 8].rearrange("p (a b) -> p a b", b=2)[:, :, 1],
                            ALU.add)
            V.selfsync("vx")
            V.tensor_scalar(stats[:, sbase + 16: sbase + 20], stats[:, sbase + 12: sbase + 16],
                            1.0 / D, None, ALU.mult)
            V.selfsync("vx")
            V.tensor_tensor(stats[:, sbase + 20: sbase + 24],
                            stats[:, sbase + 16: sbase + 20], stats[:, sbase + 16: sbase + 20],
                            ALU.mult)
            V.tensor_scalar(stats[:, sbase + 24: sbase + 28], stats[:, sbase + 8: sbase + 12],
                            1.0 / D, None, ALU.mult)
            V.selfsync("vx")
            V.tensor_tensor(stats[:, sbase + 28: sbase + 32], stats[:, sbase + 24: sbase + 28],
                            stats[:, sbase + 20: sbase + 24], ALU.subtract)
            vstat = V.ms("vx")
            S.waitp(vstat)
            S.activation(stats[:, sbase + 32: sbase + 36], stats[:, sbase + 28: sbase + 32],
                         AF.Ln, bias=ns["epsc"][:, 0:1])
            S.selfsync("sq")
            S.activation(stats[:, sbase + 36: sbase + 40], stats[:, sbase + 32: sbase + 36],
                         AF.Exp, scale=-0.5)
            srstd = S.ms("sq")
            V.waitp(srstd)
            for tt in range(4):
                V.tensor_scalar(x0tm[x0sl][:, tt * 1024:(tt + 1) * 1024],
                                x0tm[x0sl][:, tt * 1024:(tt + 1) * 1024],
                                stats[:, sbase + 16 + tt: sbase + 17 + tt],
                                stats[:, sbase + 36 + tt: sbase + 37 + tt],
                                ALU.subtract, ALU.mult)
            apl = V.ms("vm")
            Q.waitp(apl)
            so = "so0" if i % 2 == 0 else "so1"
            sdp = Q.dma(s_x1[b, c * 512:(c + 1) * 512, :].rearrange("(tt p) d -> p tt d", p=P),
                        x0tm[x0sl][:, :].rearrange("p (tt d) -> p tt d", d=D), so)
            if DEBUG_STAGE == "A":
                sdp = Q.dma(ns["dbg_tap"][b, c * 512:(c + 1) * 512, :]
                            .rearrange("(tt p) d -> p tt d", p=P),
                            x0tm[x0sl][:, :].rearrange("p (tt d) -> p tt d", d=D), so)
            pool_x0tm.release(sdp)
            ns["final_waits"].append(sdp)

        ones_vx = rows_ready

        loads = {0: load_x0(0), 1: load_x0(1)}
        fms = {}
        hs = {}
        casts = {-1: None}
        fms[0] = tgroups(0, loads[0][0], loads[0][1])
        for i in range(NCH):
            if i + 1 < NCH:
                fms[i + 1] = tgroups(i + 1, loads[i + 1][0], loads[i + 1][1])
            hs[i] = xb_chunk(i, *fms.pop(i), casts[i - 1])
            casts[i] = hs[i][1]
            if i >= 1:
                cw_chunk(i - 1, loads[i - 1][0], *hs.pop(i - 1))
                del loads[i - 1]
            if i + 2 < NCH:
                loads[i + 2] = load_x0(i + 2)
        cw_chunk(NCH - 1, loads[NCH - 1][0], *hs.pop(NCH - 1))


def phase_B(ctx, TT, V, S, G, Q, ns, layer, s_in, s_out):
    nc = ns["nc"]
    ar = ns["arena"]
    ar.reset()
    ytm = [ar.f32(4 * 1024) for _ in range(2)]
    ybf = [ar.bf16(4 * 1024) for _ in range(2)]
    yfm = [ar.bf16(8 * 513) for _ in range(2)]
    tqc = ar.bf16(4 * 64); tqs = ar.bf16(4 * 64)
    tkc = ar.bf16(8 * 64); tks = ar.bf16(8 * 64)
    qsb = ar.f32(64); qss = ar.f32(64); ksb = ar.f32(64); kss = ar.f32(64)
    qe = ar.f32(512); ke = ar.f32(512)
    t1 = [ar.f32(512) for _ in range(2)]
    t2 = [ar.f32(512) for _ in range(2)]
    sqt = ar.f32(1024)
    qp_tm = [ar.bf16(512) for _ in range(3)]
    qfm = [ar.bf16(4 * 512) for _ in range(2)]
    kfmc = [ar.bf16(4 * 512) for _ in range(2)]
    kfmp = [ar.bf16(4 * 512) for _ in range(3)]
    vp = [ar.bf16(4 * 512) for _ in range(3)]
    p_sb = [ar.bf16(640) for _ in range(3)]
    pTs = [ar.bf16(640) for _ in range(3)]
    o_bf = [ar.bf16(512) for _ in range(2)]
    ofm = [ar.bf16(4 * 512) for _ in range(2)]
    denom = [ar.f32(32) for _ in range(2)]
    negm = [ar.f32(32) for _ in range(2)]
    rsqq = [ar.f32(8) for _ in range(2)]
    rsqk = [ar.f32(8) for _ in range(2)]
    rdsc = ar.f32(8)

    psA = ns["psA"]
    psAb = psA.ap().bitcast(BF16)

    def bank(j):
        return psA.ap()[:, j * 512:(j + 1) * 512]

    def bankb(j):
        return psAb[:, j * 1024:(j + 1) * 1024]

    def pairf(j):
        b = 2 + 2 * j
        return psA.ap()[:, b * 512: b * 512 + 640]

    def pairtail_b(j):
        b = 2 + 2 * j
        return psAb[:, b * 1024 + 1280: b * 1024 + 1280 + 640]

    identb = ns["identb"]
    maskstack = ns["maskstack"]
    wqkvb, wob = ns["wqkvb"], ns["wob"]
    onesb = ns["onesb"]
    rowb = ns["rowbuf"]

    NWIN = B2 * NW

    for kk in range(8):
        G.dma(wqkvb[:, kk * 1536:(kk + 1) * 1536],
              ns["wqkv"][layer, kk * 128:(kk + 1) * 128, :], "lbw")
    for kk in range(4):
        G.dma(wob[:, kk * 1024:(kk + 1) * 1024],
              ns["wo"][layer, kk * 128:(kk + 1) * 128, :], "lbw")
    G.dma(tqc[:, :].rearrange("p (tt d) -> p tt d", d=64),
          ns["tqc"].rearrange("(tt p) d -> p tt d", p=P), "lbw")
    G.dma(tqs[:, :].rearrange("p (tt d) -> p tt d", d=64),
          ns["tqs"].rearrange("(tt p) d -> p tt d", p=P), "lbw")
    G.dma(tkc[:, :].rearrange("p (tt d) -> p tt d", d=64),
          ns["tkc"].rearrange("(tt p) d -> p tt d", p=P), "lbw")
    G.dma(tks[:, :].rearrange("p (tt d) -> p tt d", d=64),
          ns["tks"].rearrange("(tt p) d -> p tt d", p=P), "lbw")
    Q.dma(rowb[0:1, 0:64], ns["q_scale"][layer].unsqueeze(0), "lqs")
    Q.dma(rowb[0:1, 64:128], ns["k_scale"][layer].unsqueeze(0), "lqs")

    TT.wait_now("lqs")
    TT.wait_now("lbw")
    TT.matmul(bank(0)[:, 0:64], ns["onesf"][0:1, 0:128], rowb[0:1, 0:64], start=True, stop=True)
    TT.matmul(bank(0)[:, 64:128], ns["onesf"][0:1, 0:128], rowb[0:1, 64:128], start=True, stop=True)
    TT.inc("pq", 1)
    V.waitp(ctx.val("pq"))
    V.tensor_copy(qsb[:, :], bank(0)[:, 0:64])
    V.tensor_copy(ksb[:, :], bank(0)[:, 64:128])
    V.selfsync("vx")
    V.tensor_copy(qss[:, 0:32], qsb[:, 32:64])
    V.tensor_copy(qss[:, 32:64], qsb[:, 0:32])
    V.tensor_copy(kss[:, 0:32], ksb[:, 32:64])
    V.tensor_copy(kss[:, 32:64], ksb[:, 0:32])
    V.selfsync("vx")
    V.tensor_tensor(tqc.rearrange("p (a d) -> p a d", d=64),
                    tqc.rearrange("p (a d) -> p a d", d=64),
                    qsb.unsqueeze(1).broadcast_to([P, 4, 64]), ALU.mult)
    V.tensor_tensor(tqs.rearrange("p (a d) -> p a d", d=64),
                    tqs.rearrange("p (a d) -> p a d", d=64),
                    qss.unsqueeze(1).broadcast_to([P, 4, 64]), ALU.mult)
    V.tensor_tensor(tkc.rearrange("p (a d) -> p a d", d=64),
                    tkc.rearrange("p (a d) -> p a d", d=64),
                    ksb.unsqueeze(1).broadcast_to([P, 8, 64]), ALU.mult)
    V.tensor_tensor(tks.rearrange("p (a d) -> p a d", d=64),
                    tks.rearrange("p (a d) -> p a d", d=64),
                    kss.unsqueeze(1).broadcast_to([P, 8, 64]), ALU.mult)
    for b_ in kfmp + vp:
        V.memset(b_[:, :], 0.0)
    tables_ready = V.ms("vx")

    pool_ytm = Rot(2)
    pool_ybf = Rot(2)
    pool_yfm = Rot(2)
    B01 = [0, 1]
    pool01 = Rot(2, seed=tables_ready)
    pool_sim = Rot(3, seed=tables_ready)
    pool_qfm = Rot(2)
    pool_kfmc = Rot(2)
    pool_kfmp = Rot(3)
    pool_vp = Rot(3)
    pool_p = Rot(3)
    pool_pTs = Rot(3)
    pool_ofm = Rot(2)
    pool_dn = Rot(2)
    pool_rq = Rot(2)
    pool_rk = Rot(2)
    lysem = ["ly0", "ly1"]
    lbsem = ["lb0", "lb1"]

    def load_ytm(i):
        sl, wp = pool_ytm.acquire()
        Q.waitp(wp)
        b, w = divmod(i, NW)
        lp = Q.dma(ytm[sl][:, :].rearrange("p (tt d) -> p tt d", d=D),
                   s_in[b, w * 512:(w + 1) * 512, :].rearrange("(tt p) d -> p tt d", p=P),
                   lysem[sl])
        return (sl, lp)

    def load_ybf(i):
        sl2, wp2 = pool_ybf.acquire()
        G.waitp(wp2)
        b, w = divmod(i, NW)
        lp2 = G.dma(ybf[sl2][:, :].rearrange("p (tt d) -> p tt d", d=D),
                    s_in[b, w * 512:(w + 1) * 512, :].rearrange("(tt p) d -> p tt d", p=P),
                    lbsem[sl2])
        return (sl2, lp2)

    def yfm3(fsl):
        return yfm[fsl].rearrange("p (kk c) -> p kk c", c=513)

    def ty_shift(i, ld, prev_ready):
        sl2, lp2 = ld
        b, w = divmod(i, NW)
        fsl, fw = pool_yfm.acquire()
        V.waitp(fw)
        if w == 0:
            V.memset(yfm3(fsl)[:, :, 0:1], 0.0)
        else:
            V.waitp(prev_ready)
            V.tensor_copy(yfm3(fsl)[:, :, 0:1], yfm3(1 - fsl)[:, :, 512:513])
        carry_done = V.ms("vt")
        TT.waitp(lp2)
        for g in range(4):
            j, wp = pool01.acquire()
            TT.waitp(wp)
            for u in range(8):
                kk = 2 * g + u // 4
                tt = u % 4
                TT.transpose(bankb(B01[j])[:, u * 128:(u + 1) * 128],
                             ybf[sl2][:, tt * 1024 + kk * 128: tt * 1024 + kk * 128 + 128],
                             identb[:, :])
            TT.inc("pt", 1)
            S.waitp(ctx.val("pt"))
            if g == 0:
                S.waitp(carry_done)
            S.copy(yfm3(fsl)[:, 2 * g:2 * g + 2, 1:513],
                   bankb(B01[j]).rearrange("p (a c) -> p a c", c=512))
            pool01.release(S.msd("sg"))
        pool_ybf.release(ctx.val("pt"))
        return fsl, ctx.val("sg"), ctx.val("pt")

    rp_state = {"n": 0, "add_ms": {}, "pt_ms": {}}

    def rot_compute(src_sb, src_ready, rsq_ap, tc, ts, tcol, rsq_ms=None):
        n = rp_state["n"]
        rp_state["n"] += 1
        sl = n % 3
        tsl = n % 2
        G.waitp(src_ready)
        G.waitp(rp_state["add_ms"].pop(n - 2, None))
        G.tensor_tensor(t1[tsl][:, :].rearrange("p (h d) -> p h d", d=64),
                        src_sb.rearrange("p (h d) -> p h d", d=64),
                        tc.rearrange("p (a d) -> p a d", d=64)
                        [:, tcol:tcol + 1, :].broadcast_to([P, 8, 64]), ALU.mult)
        rot_src = src_sb.rearrange("p (h u v) -> p h u v", u=2, v=32)[:, :, ::-1, :]
        G.tensor_tensor(t2[tsl][:, :].rearrange("p (h u v) -> p h u v", u=2, v=32),
                        rot_src,
                        ts.rearrange("p (a d) -> p a d", d=64)
                        [:, tcol:tcol + 1, :].rearrange("p a (u v) -> p a u v", v=32)
                        .broadcast_to([P, 8, 2, 32]), ALU.mult)
        gdone = G.ms("gp")
        V.waitp(gdone)
        V.tensor_tensor(t1[tsl][:, :], t1[tsl][:, :], t2[tsl][:, :], ALU.add)
        V.selfsync("vx")
        V.waitp(rsq_ms)
        V.waitp(rp_state["pt_ms"].pop(n - 3, None))
        V.tensor_tensor(qp_tm[sl][:, :].rearrange("p (h d) -> p h d", d=64),
                        t1[tsl][:, :].rearrange("p (h d) -> p h d", d=64),
                        rsq_ap.unsqueeze(2).broadcast_to([P, 8, 64]), ALU.mult)
        tm_done = V.ms("vt")
        rp_state["add_ms"][n] = tm_done
        return n, sl, tm_done

    def rot_transpose(n, sl, tm_done, dst_fm, dslot, dst_col):
        jb, wb = pool01.acquire()
        TT.waitp(tm_done)
        TT.waitp(wb)
        for hp in range(4):
            TT.transpose(bankb(B01[jb])[:, hp * 128:(hp + 1) * 128],
                         qp_tm[sl][:, hp * 128:(hp + 1) * 128], identb[:, :])
        TT.inc("pt", 1)
        rp_state["pt_ms"][n] = ctx.val("pt")
        V.waitp(ctx.val("pt"))
        V.tensor_copy(dst_fm[dslot].rearrange("p (hp c) -> p hp c", c=512)
                      [:, :, dst_col * 128:(dst_col + 1) * 128],
                      bankb(B01[jb])[:, 0:512].rearrange("p (hp c) -> p hp c", c=128))
        pool01.release(V.msd("vt"))

    qkst = {"prev_gp": None}

    def qkv_win(i, fsl, yfm_ready):
        b, w = divmod(i, NW)
        qsl, qw = pool_qfm.acquire()
        kcsl, kcw = pool_kfmc.acquire()
        kpsl, kpw = pool_kfmp.acquire()
        vsl, vw = pool_vp.acquire()
        rqsl, rqw = pool_rq.acquire()
        rksl, rkw = pool_rk.acquire()
        TT.waitp(yfm_ready)
        if i == 0:
            TT.waitp(tables_ready)
        mmst = {}

        def lhs(kk, tt):
            sh = -1 if kk >= 4 else 0
            return yfm[fsl][:, kk * 513 + 1 + tt * 128 + sh:
                            kk * 513 + 1 + tt * 128 + sh + 128]

        def mm_tt(tt):
            pj, pw = pool_sim.acquire()
            TT.waitp(pw)
            vb, vbw = pool01.acquire()
            TT.waitp(vbw)
            base = 2 + 2 * pj
            for col, bk in ((0, base), (512, base + 1), (1024, B01[vb])):
                for kk in range(8):
                    TT.matmul(bank(bk)[:, :], lhs(kk, tt),
                              wqkvb[:, kk * 1536 + col: kk * 1536 + col + 512],
                              start=(kk == 0), stop=(kk == 7))
                TT.inc("pm", 1)
            qkv_pm = ctx.val("pm")
            if tt == 3:
                pool_yfm.release(qkv_pm)
            S.waitp(qkv_pm)
            if tt == 0:
                S.waitp(vw)
            S.copy(vp[vsl][:, tt * 512:(tt + 1) * 512], bank(B01[vb])[:, :])
            vp_evac = S.ms("sg")
            pool01.release(vp_evac)
            mmst[tt] = (pj, qkv_pm)

        def post_tt(tt):
            pj, qkv_pm = mmst.pop(tt)
            base = 2 + 2 * pj
            S.waitp(qkv_pm)
            if tt == 0:
                S.waitp(qw); S.waitp(kcw); S.waitp(kpw)
                S.waitp(rqw); S.waitp(rkw)
            S.waitp(qkst["prev_gp"])
            S.copy(qe[:, :], bank(base)[:, :])
            S.copy(ke[:, :], bank(base + 1)[:, :])
            qk_evac = S.ms("sg")
            pool_sim.release(qk_evac)
            S.activation(sqt[:, 0:512], qe[:, :], AF.Square)
            S.activation(sqt[:, 512:1024], ke[:, :], AF.Square)
            sq_done = S.ms("sq")
            V.waitp(sq_done)
            V.tensor_reduce(rsqq[rqsl][:, 0:8], sqt[:, 0:512].rearrange("p (h d) -> p h d", d=64),
                            AX.X, ALU.add)
            V.tensor_reduce(rsqk[rksl][:, 0:8], sqt[:, 512:1024].rearrange("p (h d) -> p h d", d=64),
                            AX.X, ALU.add)
            V.selfsync("vx")
            V.tensor_scalar_max(rsqq[rqsl][:, 0:8], rsqq[rqsl][:, 0:8], 1e-24)
            V.tensor_scalar_max(rsqk[rksl][:, 0:8], rsqk[rksl][:, 0:8], 1e-24)
            vr = V.ms("vx")
            S.waitp(vr)
            S.activation(rsqq[rqsl][:, 0:8], rsqq[rqsl][:, 0:8], AF.Ln)
            S.activation(rsqk[rksl][:, 0:8], rsqk[rksl][:, 0:8], AF.Ln)
            S.selfsync("sq")
            S.activation(rsqq[rqsl][:, 0:8], rsqq[rqsl][:, 0:8], AF.Exp, scale=-0.5)
            S.activation(rsqk[rksl][:, 0:8], rsqk[rksl][:, 0:8], AF.Exp, scale=-0.5)
            rq_ms = S.ms("sq")
            ca = rot_compute(qe[:, :], qk_evac, rsqq[rqsl][:, 0:8], tqc, tqs, tt, rq_ms)
            cb = rot_compute(ke[:, :], qk_evac, rsqk[rksl][:, 0:8], tkc, tks, 4 + tt, rq_ms)
            cc = rot_compute(ke[:, :], qk_evac, rsqk[rksl][:, 0:8], tkc, tks, tt, rq_ms)
            return ca, cb, cc

        def post_transpose(tt, ca, cb, cc):
            rot_transpose(*ca, qfm, qsl, tt)
            rot_transpose(*cb, kfmc, kcsl, tt)
            rot_transpose(*cc, kfmp, kpsl, tt)
            qkst["prev_gp"] = ctx.val("gp")

        mm_tt(0)
        mm_tt(1)
        for tt in range(4):
            cs = post_tt(tt)
            if tt + 2 < 4:
                mm_tt(tt + 2)
            post_transpose(tt, *cs)
        pool_rq.release(ctx.val("vt"))
        pool_rk.release(ctx.val("vt"))
        return dict(qsl=qsl, kcsl=kcsl, kpsl=kpsl, vsl=vsl, rqsl=rqsl, rksl=rksl,
                    ready=ctx.val("vt"))

    def attention_win(i, info, prev_info, ytm_sl, ytm_lp):
        b, w = divmod(i, NW)
        dsl, dw = pool_dn.acquire()
        ofsl, ofw = pool_ofm.acquire()
        TT.waitp(info["ready"])
        state = {}
        state2 = {}
        pending_o = []
        obT_pt = {}
        first_v = True
        obank = {}
        LAG = 2

        def sim_unit(u):
            tt, h = divmod(u, 8)
            nonlocal first_v
            pj, pw = pool_sim.acquire()
            TT.waitp(pw)
            hp, ho = divmod(h, 2)
            lhsT = qfm[info["qsl"]][ho * 64: ho * 64 + 64,
                                    hp * 512 + tt * 128: hp * 512 + tt * 128 + 128]
            segs = []
            if prev_info is not None:
                kp = kfmp[prev_info["kpsl"]]
                segs.append((0, 512 - 128 * tt,
                             kp[ho * 64: ho * 64 + 64,
                                hp * 512 + tt * 128: hp * 512 + 512]))
            kc = kfmc[info["kcsl"]]
            segs.append((512 - 128 * tt, 128 * tt + 128,
                         kc[ho * 64: ho * 64 + 64,
                            hp * 512: hp * 512 + 128 * tt + 128]))
            for off, ln, rhs in segs:
                pos = 0
                while pos < ln:
                    oc = off + pos
                    lim = 512 - oc if oc < 512 else 640 - oc
                    take = min(ln - pos, lim)
                    TT.matmul(pairf(pj)[:, oc: oc + take],
                              lhsT, rhs[:, pos: pos + take],
                              start=True, stop=True)
                    pos += take
            TT.inc("pm", 1)
            sim_pm = ctx.val("pm")
            V.waitp(sim_pm)
            if first_v:
                V.waitp(dw)
                first_v = False
            if w == 0:
                V.memset(pairf(pj)[:, 0:512 - 128 * tt], -1e30)
                V.tensor_tensor(pairf(pj)[:, 512:640],
                                pairf(pj)[:, 512:640],
                                maskstack[:, 128:256], ALU.add)
            else:
                edges = pairf(pj)[:, 0:640].rearrange("p (a c) -> p a c", c=128)[:, 0::4, :]
                V.tensor_tensor(edges, edges,
                                maskstack.rearrange("p (a c) -> p a c", c=128), ALU.add)
            V.selfsync("vx")
            V.reduce_max(negm[dsl][:, u:u + 1], pairf(pj)[:, :], AX.X, negate=True)
            mask_ms = V.ms("vm")
            psl, ppw = pool_p.acquire()
            S.waitp(mask_ms)
            S.waitp(ppw)
            S.activation(p_sb[psl][:, :], pairf(pj)[:, :], AF.Exp,
                         bias=negm[dsl][:, u:u + 1], scale=1.0,
                         accum_out=denom[dsl][:, u:u + 1])
            exp_ms = S.ms("sg")
            state[u] = dict(pj=pj, psl=psl, exp_ms=exp_ms)

        def finA_unit(u):
            if pending_o and u >= pending_o[0][0] * 8 + 11:
                fin_ttB(*pending_o.pop(0))
            tt, h = divmod(u, 8)
            st = state.pop(u)
            pj, psl = st["pj"], st["psl"]
            tsl, tw = pool_pTs.acquire()
            TT.waitp(st["exp_ms"])
            for c in range(5):
                TT.transpose(pairtail_b(pj)[:, c * 128:(c + 1) * 128],
                             p_sb[psl][:, c * 128:(c + 1) * 128], identb[:, :])
            TT.inc("pt", 1)
            pool_p.release(ctx.val("pt"))
            V.waitp(ctx.val("pt"))
            V.waitp(tw)
            V.tensor_copy(pTs[tsl][:, :], pairtail_b(pj)[:, :])
            pT_evac = V.ms("vt")
            pool_sim.release(pT_evac)
            state2[u] = (tsl, pT_evac)

        def finB_unit(u):
            tt, h = divmod(u, 8)
            tsl, pT_evac = state2.pop(u)
            TT.waitp(pT_evac)
            if h == 0:
                jb, jw = pool01.acquire()
                TT.waitp(jw)
                obank[tt] = jb
            ob = obank[tt]
            for c in range(5):
                kb = tt + c
                slot = (prev_info["vsl"] if prev_info is not None else info["vsl"]) \
                    if kb < 4 else info["vsl"]
                vtt = kb % 4
                TT.matmul(bank(B01[ob])[:, h * 64:(h + 1) * 64],
                          pTs[tsl][:, c * 128:(c + 1) * 128],
                          vp[slot][:, vtt * 512 + h * 64: vtt * 512 + h * 64 + 64],
                          start=(c == 0), stop=(c == 4))
            TT.inc("pm", 1)
            pool_pTs.release(ctx.val("pm"))
            if h == 7:
                fin_ttA(tt)

        def fin_ttA(tt):
            ob = obank.pop(tt)
            o_pm = ctx.val("pm")
            V.waitp(o_pm)
            V.reciprocal(rdsc[:, :], denom[dsl][:, tt * 8:(tt + 1) * 8])
            V.selfsync("vx")
            V.waitp(obT_pt.pop(tt - 2, None))
            V.tensor_tensor(o_bf[tt % 2][:, :].rearrange("p (h d) -> p h d", d=64),
                            bank(B01[ob]).rearrange("p (h d) -> p h d", d=64),
                            rdsc.unsqueeze(2).broadcast_to([P, 8, 64]), ALU.mult)
            o_done = V.msd("vm")
            pending_o.append((tt, o_done, ob))

        def fin_ttB(tt, o_done, ob):
            TT.waitp(o_done)
            for ic in range(4):
                TT.transpose(bankb(B01[ob])[:, ic * 128:(ic + 1) * 128],
                             o_bf[tt % 2][:, ic * 128:(ic + 1) * 128], identb[:, :])
            TT.inc("pt", 1)
            obT_pt[tt] = ctx.val("pt")
            V.waitp(ctx.val("pt"))
            if tt == 0:
                V.waitp(ofw)
            V.tensor_copy(ofm[ofsl][:, tt * 512:(tt + 1) * 512],
                          bankb(B01[ob])[:, 0:512])
            pool01.release(V.msd("vt"))

        units = list(range(32))
        for u in units:
            if u >= LAG:
                finA_unit(u - LAG)
            sim_unit(u)
            if u >= LAG:
                finB_unit(u - LAG)
        for u in units[-LAG:]:
            finA_unit(u)
            finB_unit(u)
        while pending_o:
            fin_ttB(*pending_o.pop(0))
        pool_dn.release(ctx.val("vm"))
        pool_qfm.release(ctx.val("pm"))
        pool_kfmc.release(ctx.val("pm"))
        ofm_ready = ctx.val("vt")

        for tt in range(4):
            pj, pw = pool_sim.acquire()
            TT.waitp(pw)
            TT.waitp(ofm_ready)
            base = 2 + 2 * pj
            for half in range(2):
                for ic in range(4):
                    TT.matmul(bank(base + half)[:, :],
                              ofm[ofsl][:, tt * 512 + ic * 128: tt * 512 + ic * 128 + 128],
                              wob[:, ic * 1024 + half * 512: ic * 1024 + half * 512 + 512],
                              start=(ic == 0), stop=(ic == 3))
            TT.inc("pm", 1)
            if tt == 3:
                pool_ofm.release(ctx.val("pm"))
            wo_pm = ctx.val("pm")
            V.waitp(wo_pm)
            if tt == 0:
                V.waitp(ytm_lp)
            for half in range(2):
                V.scalar_tensor_tensor(
                    ytm[ytm_sl][:, tt * 1024 + half * 512: tt * 1024 + half * 512 + 512],
                    bank(base + half)[:, :], 1.0,
                    ytm[ytm_sl][:, tt * 1024 + half * 512: tt * 1024 + half * 512 + 512],
                    ALU.mult, ALU.add)
            pool_sim.release(V.ms("vm"))
        store_ready = ctx.val("vm")
        Q.waitp(store_ready)
        so = "so0" if i % 2 == 0 else "so1"
        sdp = Q.dma(s_out[b, w * 512:(w + 1) * 512, :].rearrange("(tt p) d -> p tt d", p=P),
                    ytm[ytm_sl][:, :].rearrange("p (tt d) -> p tt d", d=D), so)
        if DEBUG_STAGE == f"B{layer}":
            sdp = Q.dma(ns["dbg_tap"][b, w * 512:(w + 1) * 512, :]
                        .rearrange("(tt p) d -> p tt d", p=P),
                        ytm[ytm_sl][:, :].rearrange("p (tt d) -> p tt d", d=D), so)
        pool_ytm.release(sdp)
        ns["final_waits"].append(sdp)

    ybf_loads = {0: load_ybf(0), 1: load_ybf(1)}
    ytm_loads = {0: load_ytm(0)}
    fms = {}
    infos = {}
    fms[0] = ty_shift(0, ybf_loads[0], None)
    for i in range(NWIN):
        if i + 1 < NWIN:
            fms[i + 1] = ty_shift(i + 1, ybf_loads[i + 1], fms[i][1])
        infos[i] = qkv_win(i, fms[i][0], fms[i][1])
        fms.pop(i)
        if i >= 1:
            w_prev = (i - 1) % NW
            attention_win(i - 1, infos[i - 1],
                          infos.get(i - 2) if w_prev > 0 else None,
                          ytm_loads[i - 1][0], ytm_loads[i - 1][1])
            if i - 2 >= 0:
                pool_kfmp.release(ctx.val("pm"))
                pool_vp.release(ctx.val("pm"))
            if i - 2 in infos:
                infos.pop(i - 2)
            ytm_loads.pop(i - 1)
        if i + 2 < NWIN:
            ybf_loads[i + 2] = load_ybf(i + 2)
        if i + 1 < NWIN:
            ytm_loads[i + 1] = load_ytm(i + 1)
    w_last = (NWIN - 1) % NW
    attention_win(NWIN - 1, infos[NWIN - 1],
                  infos.get(NWIN - 2) if w_last > 0 else None,
                  ytm_loads[NWIN - 1][0], ytm_loads[NWIN - 1][1])


def phase_C(ctx, TT, V, S, G, Q, ns, layer, s_in, s_out, final_ln):
    nc = ns["nc"]
    ar = ns["arena"]
    ar.reset()
    ar.cur = 0
    w1p = ar.bf16(8 * DFF)
    w2p = ar.bf16(22 * 1024)
    rowsb = ar.bf16(HID)
    gbc = ar.bf16(1024)
    bbc = ar.bf16(1024)
    ytm = [ar.f32(1024) for _ in range(2)]
    ybf = [ar.bf16(1024) for _ in range(3)]
    x1tm = [ar.f32(1024) for _ in range(2)] if final_ln else None
    h2n = [ar.bf16(1024) for _ in range(2)]
    h2nfm = [ar.bf16(1024) for _ in range(2)]
    h4tm = [ar.bf16(2816) for _ in range(2)]
    h4fm = ar.bf16(2816)
    gl = [ar.bf16(512) for _ in range(2)]

    psA = ns["psA"]
    psAb = psA.ap().bitcast(BF16)

    def bank(j):
        return psA.ap()[:, j * 512:(j + 1) * 512]

    def bankb(j):
        return psAb[:, j * 1024:(j + 1) * 1024]

    identb = ns["identb"]
    shmatb = ns["shmatb"]
    onesb = ns["onesb"]
    stats = ns["stats"]
    scb = ns["bscr"]

    NT = B2 * T // P

    for kk in range(8):
        G.dma(w1p[:, kk * DFF:(kk + 1) * DFF], ns["w1"][layer, kk * 128:(kk + 1) * 128, :], "lbw")
    for kk in range(21):
        G.dma(w2p[:, kk * 1024:(kk + 1) * 1024], ns["w2"][layer, kk * 128:(kk + 1) * 128, :], "lbw")
    G.dma(w2p[:, 21 * 1024:22 * 1024][0:42, :], ns["w2"][layer, 21 * 128:2730, :], "lbw")
    G.dma(rowsb[0:1, 0:HID], ns["b1"][layer, 0:HID].unsqueeze(0), "lbw")
    G.dma(rowsb[32:33, 0:HID], ns["b1"][layer, HID:DFF].unsqueeze(0), "lbw")
    G.dma(rowsb[64:65, 0:1024], ns["b2"][layer].unsqueeze(0), "lbw")
    G.dma(ns["scrap1k"][0:1, 0:1024], ns["ln_g"][layer].unsqueeze(0), "lbw")
    G.dma(ns["scrap1k"][32:33, 0:1024], ns["ln_b"][layer].unsqueeze(0), "lbw")
    lbw_now = ctx.val("lbw")
    TT.wait_now("lbw")
    TT.matmul(bank(0)[:, :], onesb[0:1, 0:128], ns["scrap1k"][0:1, 0:512], start=True, stop=True)
    TT.matmul(bank(1)[:, :], onesb[0:1, 0:128], ns["scrap1k"][0:1, 512:1024], start=True, stop=True)
    TT.matmul(bank(2)[:, :], onesb[32:33, 0:128], ns["scrap1k"][32:33, 0:512], start=True, stop=True)
    TT.matmul(bank(3)[:, :], onesb[32:33, 0:128], ns["scrap1k"][32:33, 512:1024], start=True, stop=True)
    TT.inc("pq", 1)
    V.waitp(ctx.val("pq"))
    V.tensor_copy(gbc[:, 0:512], bank(0)[:, :])
    V.tensor_copy(gbc[:, 512:1024], bank(1)[:, :])
    V.tensor_copy(bbc[:, 0:512], bank(2)[:, :])
    V.tensor_copy(bbc[:, 512:1024], bank(3)[:, :])
    V.memset(h4tm[0][:, 2730:2816], 0.0)
    V.memset(h4tm[1][:, 2730:2816], 0.0)
    cw_ready = V.ms("vx")

    pool_ytm = Rot(2)
    pool_ybf = Rot(3)
    pool_x1 = Rot(2)
    pool_h2n = Rot(2)
    pool_h2nfm = Rot(2)
    pool_h4 = Rot(2)
    pool_gl = Rot(2)
    pool01 = Rot(2, seed=cw_ready)
    B01 = [0, 1]
    pool_pair = Rot(2, seed=cw_ready)
    pool_st = Rot(2)
    lysem = ["ly0", "ly1", "ly2"]
    lbsem = ["lb0", "lb1", "lb2"]
    lxsem = ["lx0", "lx1"]

    def load_tile(i):
        sl, wp = pool_ytm.acquire()
        Q.waitp(wp)
        b, t0 = divmod(i * P, T)
        lp = Q.dma(ytm[sl][:, :], s_in[b, t0:t0 + P, :], lysem[sl])
        sl2, wp2 = pool_ybf.acquire()
        G.waitp(wp2)
        lp2 = G.dma(ybf[sl2][:, :], s_in[b, t0:t0 + P, :], lbsem[sl2])
        ret = [sl, lp, sl2, lp2, None, None]
        if final_ln:
            sl3, wp3 = pool_x1.acquire()
            Q.waitp(wp3)
            lp3 = Q.dma(x1tm[sl3][:, :], ns["s_x1"][b, t0:t0 + P, :], lxsem[sl3])
            ret[4], ret[5] = sl3, lp3
        return ret

    def pre_tile(i, ld):
        b, t0 = divmod(i * P, T)
        sl, lp, sl2, lp2, sl3, lp3 = ld
        sbase = 32 + (i % 2) * 16
        stw = pool_st.acquire()[1]
        sbk = bank(6 + i % 2)
        TT.waitp(lp2)
        TT.waitp(gst["shift_free"].pop(i - 2, None))
        if i == 0:
            TT.waitp(cw_ready)
        first = (t0 == 0)
        TT.matmul(sbk[:, :], shmatb[:, 0:128], ybf[sl2][:, 512:1024],
                  start=True, stop=first)
        if not first:
            TT.waitp(gst["prev_ybf"][0])
            TT.matmul(sbk[:, :], shmatb[:, 128:256],
                      ybf[gst["prev_ybf"][1]][:, 512:1024], start=False, stop=True)
        TT.inc("pm", 1)
        sh_pm = ctx.val("pm")
        if i >= 1:
            pool_ybf.release(sh_pm)
        gst["prev_ybf"] = (lp2, sl2, sh_pm)
        S.waitp(sh_pm)
        S.waitp(stw)
        if i == 0:
            S.waitp(cw_ready)
        S.activation(ns["scrap1k"][:, 0:512], ybf[sl2][:, 0:512], AF.Square,
                     accum_out=stats[:, sbase + 2: sbase + 3])
        S.activation(ns["scrap1k"][:, 512:1024], sbk[:, :], AF.Square,
                     accum_out=stats[:, sbase + 3: sbase + 4])
        sq_ms = S.ms("sq")
        V.waitp(sq_ms)
        V.reduce_sum(stats[:, sbase + 0: sbase + 1], ybf[sl2][:, 0:512], AX.X)
        V.reduce_sum(stats[:, sbase + 1: sbase + 2], sbk[:, :], AX.X)
        V.selfsync("vx")
        V.tensor_tensor(stats[:, sbase + 4: sbase + 5], stats[:, sbase + 0: sbase + 1],
                        stats[:, sbase + 1: sbase + 2], ALU.add)
        V.tensor_tensor(stats[:, sbase + 5: sbase + 6], stats[:, sbase + 2: sbase + 3],
                        stats[:, sbase + 3: sbase + 4], ALU.add)
        V.selfsync("vx")
        V.tensor_scalar(stats[:, sbase + 6: sbase + 7], stats[:, sbase + 4: sbase + 5],
                        1.0 / D, None, ALU.mult)
        V.tensor_scalar(stats[:, sbase + 7: sbase + 8], stats[:, sbase + 5: sbase + 6],
                        1.0 / D, None, ALU.mult)
        V.selfsync("vx")
        V.tensor_tensor(stats[:, sbase + 8: sbase + 9], stats[:, sbase + 6: sbase + 7],
                        stats[:, sbase + 6: sbase + 7], ALU.mult)
        V.selfsync("vx")
        V.tensor_tensor(stats[:, sbase + 9: sbase + 10], stats[:, sbase + 7: sbase + 8],
                        stats[:, sbase + 8: sbase + 9], ALU.subtract)
        vstat = V.ms("vx")
        S.waitp(vstat)
        S.activation(stats[:, sbase + 10: sbase + 11], stats[:, sbase + 9: sbase + 10],
                     AF.Ln, bias=ns["epsc"][:, 0:1])
        S.selfsync("sq")
        S.activation(stats[:, sbase + 11: sbase + 12], stats[:, sbase + 10: sbase + 11],
                     AF.Exp, scale=-0.5)
        rstd_ms = S.ms("sq")
        h2sl, h2w = pool_h2n.acquire()
        V.waitp(rstd_ms)
        V.waitp(h2w)
        V.tensor_scalar(h2n[h2sl][:, 0:512], ybf[sl2][:, 0:512],
                        stats[:, sbase + 6: sbase + 7], stats[:, sbase + 11: sbase + 12],
                        ALU.subtract, ALU.mult)
        V.tensor_scalar(h2n[h2sl][:, 512:1024], sbk[:, :],
                        stats[:, sbase + 6: sbase + 7], stats[:, sbase + 11: sbase + 12],
                        ALU.subtract, ALU.mult)
        apl_ms = V.msd("vm")
        gst["shift_free"][i] = apl_ms
        G.waitp(apl_ms)
        G.tensor_tensor(h2n[h2sl][:, :], h2n[h2sl][:, :], gbc[:, :], ALU.mult)
        G.selfsync("gp")
        G.tensor_tensor(h2n[h2sl][:, :], h2n[h2sl][:, :], bbc[:, :], ALU.add)
        gdone = G.ms("gp")
        gst["pre"][i] = (sl, lp, sl3, lp3, sbase, h2sl, gdone)

    def preT_tile(i):
        _, _, _, _, _, h2sl, gdone = gst["pre"][i]
        fmsl, fmw = pool_h2nfm.acquire()
        jt, wt = pool01.acquire()
        TT.waitp(gdone)
        TT.waitp(wt)
        TT.waitp(fmw)
        for kk in range(8):
            TT.transpose(bankb(B01[jt])[:, kk * 128:(kk + 1) * 128],
                         h2n[h2sl][:, kk * 128:(kk + 1) * 128], identb[:, :])
        TT.inc("pt", 1)
        pool_h2n.release(ctx.val("pt"))
        V.waitp(ctx.val("pt"))
        V.tensor_copy(h2nfm[fmsl][:, :], bankb(B01[jt])[:, :])
        h2nfm_ms = V.msd("vt")
        pool01.release(h2nfm_ms)
        gst["fm"][i] = (fmsl, h2nfm_ms)

    def gemm_tile(i):
        b, t0 = divmod(i * P, T)
        sl, lp, sl3, lp3, sbase, h2sl, gdone = gst["pre"].pop(i)
        fmsl, h2nfm_ms = gst["fm"].pop(i)
        h4sl, h4w = pool_h4.acquire()
        TT.waitp(h2nfm_ms)
        segs = [(512 * jj, min(512, HID - 512 * jj)) for jj in range(6)]
        for si, (c0, ln) in enumerate(segs):
            pj, pw = pool_pair.acquire()
            TT.waitp(pw)
            base = 2 + 2 * pj
            for half, cc in ((0, c0), (1, HID + c0)):
                for kk in range(8):
                    TT.matmul(bank(base + half)[:, 0:ln],
                              h2nfm[fmsl][:, kk * 128:(kk + 1) * 128],
                              w1p[:, kk * DFF + cc: kk * DFF + cc + ln],
                              start=(kk == 0), stop=False)
                hp = 0 if half == 0 else 32
                TT.matmul(bank(base + half)[:, 0:ln], onesb[hp:hp + 1, 0:128],
                          rowsb[hp:hp + 1, c0: c0 + ln], start=False, stop=True)
            TT.inc("pm", 1)
            w1_pm = ctx.val("pm")
            glsl, glw = pool_gl.acquire()
            S.waitp(w1_pm)
            S.waitp(glw)
            S.activation(gl[glsl][:, 0:ln], bank(base + 1)[:, 0:ln], AF.Gelu)
            gelu_ms = S.ms("sg")
            V.waitp(gelu_ms)
            if si == 0:
                V.waitp(h4w)
            V.tensor_tensor(h4tm[h4sl][:, c0:c0 + ln], bank(base)[:, 0:ln],
                            gl[glsl][:, 0:ln], ALU.mult)
            vg = V.msd("vm")
            pool_gl.release(vg)
            pool_pair.release(vg)
        pool_h2nfm.release(ctx.val("pm"))
        h4_ms = ctx.val("vm")
        pj, pw = pool_pair.acquire()
        base = 2 + 2 * pj
        TT.waitp(pw)
        h4fm_parts = []
        for grp in range(3):
            jt2, wt2 = pool01.acquire()
            TT.waitp(wt2)
            if grp == 0:
                TT.waitp(h4_ms)
            n_in = min(8, 22 - grp * 8)
            for u in range(n_in):
                TT.transpose(bankb(B01[jt2])[:, u * 128:(u + 1) * 128],
                             h4tm[h4sl][:, (grp * 8 + u) * 128:(grp * 8 + u + 1) * 128],
                             identb[:, :])
            TT.inc("pt", 1)
            V.waitp(ctx.val("pt"))
            if grp == 0:
                V.waitp(gst["prev_w2"])
            V.tensor_copy(h4fm[:, grp * 1024: grp * 1024 + n_in * 128],
                          bankb(B01[jt2])[:, 0:n_in * 128])
            h4fm_parts.append(V.msd("vt"))
            pool01.release(h4fm_parts[-1])
        pool_h4.release(ctx.val("pt"))
        TT.waitp(h4fm_parts[-1])
        for half in range(2):
            for kk in range(22):
                kcount = 128 if kk < 21 else 42
                TT.matmul(bank(base + half)[:, :],
                          h4fm[0:kcount, kk * 128:(kk + 1) * 128],
                          w2p[0:kcount, kk * 1024 + half * 512: kk * 1024 + half * 512 + 512],
                          start=(kk == 0), stop=False)
            TT.matmul(bank(base + half)[:, :], onesb[64:65, 0:128],
                      rowsb[64:65, half * 512:(half + 1) * 512], start=False, stop=True)
        TT.inc("pm", 1)
        gst["prev_w2"] = ctx.val("pm")
        w2_pm = ctx.val("pm")
        V.waitp(w2_pm)
        V.waitp(lp)
        for half in range(2):
            V.scalar_tensor_tensor(ytm[sl][:, half * 512:(half + 1) * 512],
                                   bank(base + half)[:, :], 1.0,
                                   ytm[sl][:, half * 512:(half + 1) * 512],
                                   ALU.mult, ALU.add)
        pool_pair.release(V.msd("vm"))
        if final_ln:
            V.waitp(lp3)
            V.selfsync("vm")
            V.scalar_tensor_tensor(ytm[sl][:, :], x1tm[sl3][:, :], 1.0, ytm[sl][:, :],
                                   ALU.mult, ALU.add,
                                   accum_out=stats[:, sbase + 0: sbase + 1])
            fv_ms = V.ms("vm")
            pool_x1.release(fv_ms)
            S.waitp(fv_ms)
            S.activation(ns["scrap1k"][:, :], ytm[sl][:, :], AF.Square,
                         accum_out=stats[:, sbase + 1: sbase + 2])
            fsq = S.ms("sq")
            V.waitp(fsq)
            V.tensor_scalar(stats[:, sbase + 6: sbase + 7], stats[:, sbase + 0: sbase + 1],
                            1.0 / D, None, ALU.mult)
            V.tensor_scalar(stats[:, sbase + 7: sbase + 8], stats[:, sbase + 1: sbase + 2],
                            1.0 / D, None, ALU.mult)
            V.selfsync("vx")
            V.tensor_tensor(stats[:, sbase + 8: sbase + 9], stats[:, sbase + 6: sbase + 7],
                            stats[:, sbase + 6: sbase + 7], ALU.mult)
            V.selfsync("vx")
            V.tensor_tensor(stats[:, sbase + 9: sbase + 10], stats[:, sbase + 7: sbase + 8],
                            stats[:, sbase + 8: sbase + 9], ALU.subtract)
            fstat = V.ms("vx")
            S.waitp(fstat)
            S.activation(stats[:, sbase + 10: sbase + 11], stats[:, sbase + 9: sbase + 10],
                         AF.Ln, bias=ns["epsc"][:, 0:1])
            S.selfsync("sq")
            S.activation(stats[:, sbase + 11: sbase + 12], stats[:, sbase + 10: sbase + 11],
                         AF.Exp, scale=-0.5)
            frstd = S.ms("sq")
            V.waitp(frstd)
            V.tensor_scalar(ytm[sl][:, :], ytm[sl][:, :],
                            stats[:, sbase + 6: sbase + 7], stats[:, sbase + 11: sbase + 12],
                            ALU.subtract, ALU.mult)
        stile_ms = V.ms("vm")
        pool_st.release(stile_ms)
        Q.waitp(stile_ms)
        so = "so0" if i % 2 == 0 else "so1"
        dst = ns["out"] if final_ln else s_out
        sdp = Q.dma(dst[b, t0:t0 + P, :], ytm[sl][:, :], so)
        if DEBUG_STAGE == f"C{layer}" and not final_ln:
            sdp = Q.dma(ns["dbg_tap"][b, t0:t0 + P, :], ytm[sl][:, :], so)
        pool_ytm.release(sdp)
        ns["final_waits"].append(sdp)

    gst = {"prev_w2": None, "prev_ybf": (None, None, None),
           "shift_free": {}, "pre": {}, "fm": {}}
    pre_tile(0, load_tile(0))
    pre_tile(1, load_tile(1))
    preT_tile(0)
    for i in range(NT):
        gemm_tile(i)
        if i + 1 < NT:
            preT_tile(i + 1)
        if i + 2 < NT:
            pre_tile(i + 2, load_tile(i + 2))


def make_inputs_maps(inputs):
    hc = host_consts()
    maps = []
    x = np.ascontiguousarray(inputs["L0_attn_x"])
    full = dict(
        log_lambda=inputs["L0_attn_log_lambda"], b_w=inputs["L0_attn_b_w"],
        b_b=inputs["L0_attn_b_b"], c_w=inputs["L0_attn_c_w"], c_b=inputs["L0_attn_c_b"],
        wqkv=inputs["L0_ffn_wqkv"], q_scale=inputs["L0_ffn_q_scale"],
        k_scale=inputs["L0_ffn_k_scale"], wo=inputs["L0_ffn_wo"],
        ln_g=inputs["L0_ffn_ln_g"], ln_b=inputs["L0_ffn_ln_b"],
        w1=inputs["L0_ffn_w1"], b1=inputs["L0_ffn_b1"],
        w2=inputs["L0_ffn_w2"], b2=inputs["L0_ffn_b2"],
        **hc,
    )
    full = {k: np.ascontiguousarray(np.asarray(v, dtype=np.float32)) for k, v in full.items()}
    for c in range(8):
        m = dict(full)
        m["x0"] = x[2 * c:2 * c + 2]
        maps.append(m)
    return maps


def kernel(**inputs):
    nc = build_program()
    maps = make_inputs_maps(inputs)
    res = run_bass_kernel_spmd(nc, maps, core_ids=list(range(8)))
    return np.concatenate([r["out"] for r in res.results], axis=0)


if __name__ == "__main__":
    print("module ok")
